# version 23
# baseline (speedup 1.0000x reference)
import contextlib
import numpy as np
import concourse.bass as bass
import concourse.mybir as mybir
from concourse.bass_utils import run_bass_kernel_spmd

F32 = mybir.dt.float32
BF16 = mybir.dt.bfloat16
AF = mybir.ActivationFunctionType
ALU = mybir.AluOpType
AXL = mybir.AxisListType

D = 1024
S = 2048
NCH = 8
NSEQ = 2
TB = 512
NTB = S // TB
NTT = S // 128
EPS = 1e-5

ENGINES = ("tensor", "scalar", "vector", "gpsimd", "sync")


def psP_w(R, i):
    d = R["psP_free"][i]
    return d if d is not None else ("MM", 0)





class Op:
    __slots__ = ("eng", "fn", "waits", "tick")

    def __init__(self, eng, fn, waits, tick):
        self.eng = eng
        self.fn = fn
        self.waits = waits
        self.tick = tick


class Prog:

    def __init__(self, nc):
        self.nc = nc
        self.ops = []
        self.sems = {}
        self.cnt = {}
        self.stack = contextlib.ExitStack()
        self.uid = 0
        for name in ("MM", "EV", "VV", "PP",
                     "WD0", "WD1", "WD2", "WD3",
                     "XD0", "XD1", "XD2", "XD3"):
            self.sems[name] = self.stack.enter_context(nc.semaphore(name))
            self.cnt[name] = 0
        self.rot = {"gpsimd": 0, "sync": 0}
        self.pool = {"gpsimd": ["WD0", "WD1", "WD2", "WD3"],
                     "sync": ["XD0", "XD1", "XD2", "XD3"]}

    def fresh(self, tag):
        self.uid += 1
        return f"{tag}_{self.uid}"

    def sb(self, tag, shape, dtype):
        return self.stack.enter_context(self.nc.sbuf_tensor(self.fresh(tag), shape, dtype))

    def scope(self):
        return contextlib.ExitStack()

    def sb_in(self, st, tag, shape, dtype):
        return st.enter_context(self.nc.sbuf_tensor(self.fresh(tag), shape, dtype))

    def ps_in(self, st, tag, shape, dtype=F32):
        return st.enter_context(self.nc.psum_tensor(self.fresh(tag), shape, dtype))

    def op(self, eng, fn, waits=(), tick=None):
        w = [t for t in waits if isinstance(t, tuple) and t[1] and t[1] > 0]
        t = None
        val = None
        if tick is not None:
            if isinstance(tick, str):
                tick = (tick, 1)
            s, dd = tick
            self.cnt[s] += dd
            val = self.cnt[s]
            t = (s, dd)
        self.ops.append(Op(eng, fn, w, t))
        return val

    def dma(self, queue_eng, out, in_, waits=()):
        sem = self.pool[queue_eng][self.rot[queue_eng] % 4]
        self.rot[queue_eng] += 1
        w = list(waits) + [(sem, self.cnt[sem])]

        def fn(eng, out=out, in_=in_):
            return eng.dma_start(out=out, in_=in_)
        v = self.op(queue_eng, fn, waits=w, tick=(sem, 16))
        return (sem, v)

    def qdeps(self, queue):
        return [(s, self.cnt[s]) for s in self.pool[queue]]

    def flush(self, name=None):
        ops = self.ops
        self.ops = []
        if not ops:
            return
        nc = self.nc
        sems = self.sems
        with nc.Block(self.fresh(name or "blk")) as blk:
            for ename in ENGINES:
                mine = [o for o in ops if o.eng == ename]
                if not mine:
                    continue
                dec = getattr(blk, ename)

                def body(eng, mine=mine):
                    last_wait = {}
                    for o in mine:
                        for (s, v) in o.waits:
                            if last_wait.get(s, -1) >= v:
                                continue
                            last_wait[s] = v
                            eng.wait_ge(sems[s], v)
                        instr = o.fn(eng)
                        if o.tick is not None:
                            s, dd = o.tick
                            instr.then_inc(sems[s], dd)
                dec(body)

    def close(self):
        self.stack.close()


INPUT_SPECS = {
    "L0_attn_x": (16, 2048, 1024),
    "L0_attn_w_q": (1024, 1024), "L0_attn_b_q": (1024,),
    "L0_attn_w_k": (1024, 1024), "L0_attn_b_k": (1024,),
    "L0_attn_w_v": (1024, 1024), "L0_attn_b_v": (1024,),
    "L0_attn_w_o": (1024, 1024), "L0_attn_b_o": (1024,),
    "L0_attn_ln_gamma": (1024,), "L0_attn_ln_beta": (1024,),
    "L0_ffn_router_w": (1024, 4), "L0_ffn_router_b": (4,),
    "L0_ffn_load_balancer": (4,),
    "L0_ffn_sw_w1": (1024, 4096), "L0_ffn_sw_w3": (1024, 4096),
    "L0_ffn_sw_w2": (4096, 1024),
    "L0_ffn_me_eq_w": (1024, 1024), "L0_ffn_me_eq_b": (1024,),
    "L0_ffn_me_wv": (1024, 1024), "L0_ffn_me_bv": (1024,),
    "L0_ffn_me_wo": (1024, 1024), "L0_ffn_me_bo": (1024,),
    "L0_ffn_me_c1w": (1024, 2048), "L0_ffn_me_c1b": (2048,),
    "L0_ffn_me_c2w": (2048, 1024), "L0_ffn_me_c2b": (1024,),
    "L0_ffn_ce_syn_w": (1024, 1024), "L0_ffn_ce_syn_b": (1024,),
    "L0_ffn_ce_wv": (1024, 1024), "L0_ffn_ce_bv": (1024,),
    "L0_ffn_ce_wo": (1024, 1024), "L0_ffn_ce_bo": (1024,),
    "L0_ffn_ce_n1g": (1024,), "L0_ffn_ce_n1b": (1024,),
    "L0_ffn_ce_f1w": (1024, 2048), "L0_ffn_ce_f1b": (2048,),
    "L0_ffn_ce_f2w": (2048, 1024), "L0_ffn_ce_f2b": (1024,),
    "L0_ffn_ce_n2g": (1024,), "L0_ffn_ce_n2b": (1024,),
    "L0_ffn_ce_gen_w": (1024, 1024), "L0_ffn_ce_gen_b": (1024,),
    "L0_ffn_ml_w1": (1024, 4096), "L0_ffn_ml_b1": (4096,),
    "L0_ffn_ml_w2": (4096, 1024), "L0_ffn_ml_b2": (1024,),
    "L1_attn_ln_w": (1024,), "L1_attn_ln_b": (1024,),
    "L1_attn_in_proj_w": (3072, 1024), "L1_attn_in_proj_b": (3072,),
    "L1_attn_out_proj_w": (1024, 1024), "L1_attn_out_proj_b": (1024,),
    "L1_attn_w0": (4, 256, 256), "L1_attn_w1": (4, 256, 256),
    "L1_attn_w2": (4, 256, 256),
    "L1_ffn_Wr": (1024, 8), "L1_ffn_br": (8,),
    "L1_ffn_W1": (8, 1024, 2048), "L1_ffn_b1": (8, 2048),
    "L1_ffn_W2": (8, 2048, 1024), "L1_ffn_b2": (8, 1024),
}

BT_ROWS = [
    ("b_q8", "L0_attn_b_q", None, 0.125),
    ("b_k", "L0_attn_b_k", None, None),
    ("ln_g", "L0_attn_ln_gamma", None, None),
    ("ln_b", "L0_attn_ln_beta", None, None),
    ("me_eq_b", "L0_ffn_me_eq_b", None, None),
    ("me_bv", "L0_ffn_me_bv", None, None),
    ("me_bo", "L0_ffn_me_bo", None, None),
    ("ce_syn_b", "L0_ffn_ce_syn_b", None, None),
    ("ce_bv", "L0_ffn_ce_bv", None, None),
    ("ce_bo", "L0_ffn_ce_bo", None, None),
    ("ce_n1g", "L0_ffn_ce_n1g", None, None),
    ("ce_n1b", "L0_ffn_ce_n1b", None, None),
    ("ce_n2g", "L0_ffn_ce_n2g", None, None),
    ("ce_n2b", "L0_ffn_ce_n2b", None, None),
    ("ce_f2b", "L0_ffn_ce_f2b", None, None),
    ("l1_ln_w", "L1_attn_ln_w", None, None),
    ("l1_ln_b", "L1_attn_ln_b", None, None),
    ("me_c1b", "L0_ffn_me_c1b", None, None),
    ("ce_f1b", "L0_ffn_ce_f1b", None, None),
    ("ml_b1", "L0_ffn_ml_b1", None, None),
    ("inp_b8", "L1_attn_in_proj_b", (0, 1024), 0.125),
    ("inp_bk", "L1_attn_in_proj_b", (1024, 2048), None),
] + [(f"l1_b1_{e}", "L1_ffn_b1", ("row", e), None) for e in range(8)]
NROWS = len(BT_ROWS)


def build_nc(stage="full"):
    nc = bass.Bass("TRN2", target_bir_lowering=False)
    P = Prog(nc)
    dram = {}
    for name, shape in INPUT_SPECS.items():
        if name == "L0_attn_x":
            dram["x"] = nc.dram_tensor("x", [NSEQ, S, D], F32, kind="ExternalInput")
        else:
            dram[name] = nc.dram_tensor(name, list(shape), F32, kind="ExternalInput")
    out = nc.dram_tensor("out", [NSEQ, S, D], F32, kind="ExternalOutput")
    k = Kern(P, dram, out, stage)
    k.build()
    P.close()
    return nc


class Kern:
    def __init__(self, P, dram, out, stage):
        self.P = P
        self.dram = dram
        self.out = out
        self.stage = stage

    def build(self):
        P = self.P
        self.X = P.sb("X", [128, NCH, S], F32)
        self.XN = P.sb("XN", [128, NCH, S], BF16)
        self.ID32 = P.sb("id32", [128, 128], F32)
        self.ID16 = P.sb("id16", [128, 128], BF16)
        self.ONES32 = P.sb("ones32", [1, 512], F32)
        self.ONESC = P.sb("onesc", [128, 1], F32)
        self.ONESC16 = P.sb("onesc16", [128, 1], BF16)
        self.ONESR16 = P.sb("onesr16", [1, 256], BF16)
        self.MASKH = P.sb("maskh", [128, 2], BF16)

        self.prep_const_phase()
        self.prep_bias_phase()

        for seq in range(NSEQ):
            self.load_x_phase(seq)
            if self.stage == "io":
                self.store_phase(seq, self.X, F32)
                continue
            for layer in range(2):
                self.layer_st = P.scope()
                self.R_st = P.scope()
                self.R = self.alloc_hg(self.R_st)
                self.ln_router_phase(seq, layer)
                if self.stage == "ln" and layer == 0:
                    self.R_st.close()
                    self.layer_st.close()
                    break
                self.attn_phase(seq, layer)
                self.R_st.close()
                if layer == 1:
                    self.lact_phase(seq)
                    self.attn_st.close()
                if self.stage == f"l{layer}attn":
                    self.layer_st.close()
                    break
                self.ffn_phase(seq, layer)
                self.layer_st.close()
                if self.stage.split(":")[0] == f"l{layer}":
                    break
            if self.stage == "ln":
                self.store_phase(seq, self.XN, BF16)
            else:
                self.store_phase(seq, self.X, F32)
        P.op("sync", lambda e: e.nop(), waits=[(s, P.cnt[s]) for s in ("XD0", "XD1", "XD2", "XD3")])
        P.flush("fin")

    def prep_const_phase(self):
        P = self.P
        for ident in (self.ID32, self.ID16):
            pm = P.op("gpsimd", lambda e, t=ident: e.memset(t[:], 0.0), tick="PP")
            P.op("gpsimd", lambda e, t=ident: e.affine_select(
                out=t[:], in_=t[:], compare_op=ALU.not_equal, fill=1.0,
                base=0, pattern=[[-1, 128]], channel_multiplier=1),
                waits=[("PP", pm)], tick="PP")
        P.op("gpsimd", lambda e: e.memset(self.ONES32[:], 1.0), tick="PP")
        P.op("gpsimd", lambda e: e.memset(self.ONESC[:], 1.0), tick="PP")
        P.op("gpsimd", lambda e: e.memset(self.ONESC16[:], 1.0), tick="PP")
        P.op("gpsimd", lambda e: e.memset(self.ONESR16[:], 1.0), tick="PP")
        pm = P.op("gpsimd", lambda e: e.memset(self.MASKH[:], 0.0), tick="PP")
        pm = P.op("gpsimd", lambda e: e.memset(self.MASKH[0:64, 0:1], 1.0),
                  waits=[("PP", pm)], tick="PP")
        P.op("gpsimd", lambda e: e.memset(self.MASKH[64:128, 1:2], 1.0),
             waits=[("PP", pm)], tick="PP")
        P.flush("consts")

    def prep_bias_phase(self):
        P = self.P
        d = self.dram
        self.BT = P.sb("BT", [128, 32, NROWS], F32)
        stp = P.scope()
        rows = P.sb_in(stp, "brows", [128, 4096], F32)
        pm = P.op("gpsimd", lambda e: e.memset(rows[:], 0.0), tick="PP")
        self.row_of = {}
        dma_t = 0
        r = 0
        for (name, key, sl, scale) in BT_ROWS:
            self.row_of[name] = r
            if sl is None:
                src = d[key].ap()
                length = INPUT_SPECS[key][0]
            elif isinstance(sl, tuple) and sl[0] == "row":
                src = d[key][sl[1], :]
                length = INPUT_SPECS[key][1]
            else:
                src = d[key][sl[0]:sl[1]]
                length = sl[1] - sl[0]
            dma_t = P.dma("sync", rows[r:r + 1, 0:length], src,
                          waits=[("PP", pm)])
            r += 1
        row_deps = P.qdeps("sync")
        psb = [P.ps_in(stp, "btp", [128, 128], F32) for _ in range(2)]
        ps_free = [0, 0]
        ev = 0
        for cs in range(32):
            ps = psb[cs % 2]
            mt = P.op("tensor", lambda e, cs=cs, ps=ps: e.transpose(
                ps[:], rows[:, cs * 128:(cs + 1) * 128], self.ID32[:]),
                waits=row_deps + [("PP", P.cnt["PP"]), ("EV", ps_free[cs % 2])],
                tick="MM")
            ev = P.op("scalar", lambda e, cs=cs, ps=ps: e.activation(
                self.BT[:, cs, :], ps[:, 0:NROWS], AF.Copy), waits=[("MM", mt)], tick="EV")
            ps_free[cs % 2] = ev
        for (name, key, sl, scale) in BT_ROWS:
            if scale is None:
                continue
            for c in range(8):
                P.op("vector", lambda e, name=name, c=c, s=scale: e.tensor_scalar_mul(
                    self.bias_col(name, c), self.bias_col(name, c), s),
                    waits=[("EV", ev)], tick="VV")
        stp.close()
        P.flush("bias")

    def bias_col(self, name, chunk):
        r = self.row_of[name]
        return self.BT[:, chunk, r:r + 1]

    def load_row(self, st, key, sl=None, waits=()):
        P = self.P
        d = self.dram
        if sl is None:
            src = d[key].ap()
            length = INPUT_SPECS[key][0]
        elif isinstance(sl, tuple) and sl[0] == "row":
            src = d[key][sl[1], :]
            length = INPUT_SPECS[key][1]
        else:
            src = d[key][sl[0]:sl[1]]
            length = sl[1] - sl[0]
        t = P.sb_in(st, "row", [1, length], F32)
        dma_t = P.dma("sync", t[:], src, waits=waits)
        return t, dma_t

    def load_x_phase(self, seq):
        P = self.P
        x = self.dram["x"]
        stp = P.scope()
        STG = P.sb_in(stp, "stg", [128, 2, D], F32)
        psb = [P.ps_in(stp, "xl", [128, 512], F32) for _ in range(4)]
        ps_free = [0, 0, 0, 0]
        stg_free = [0, 0]
        for tt in range(NTT):
            buf = tt % 2
            dma_t = P.dma("sync", STG[:, buf, :], x[seq, tt * 128:(tt + 1) * 128, :],
                          waits=[("MM", stg_free[buf])])
            for half in range(2):
                b = (tt % 2) * 2 + half
                ps = psb[b]
                mt = 0
                for k in range(4):
                    c = half * 4 + k
                    mt = P.op("tensor", lambda e, ps=ps, k=k, c=c, buf=buf: e.transpose(
                        ps[:, k * 128:(k + 1) * 128],
                        STG[:, buf, c * 128:(c + 1) * 128], self.ID32[:]),
                        waits=[dma_t, ("EV", ps_free[b])], tick="MM")
                ps_free[b] = P.op("scalar", lambda e, ps=ps, half=half, tt=tt: e.activation(
                    self.X[:, half * 4:half * 4 + 4, tt * 128:(tt + 1) * 128],
                    ps[:].rearrange("p (c t) -> p c t", t=128),
                    AF.Copy), waits=[("MM", mt)], tick="EV")
                if half == 1:
                    stg_free[buf] = mt
        stp.close()
        P.flush(f"loadx{seq}")

    def store_phase(self, seq, srcten, dtype):
        P = self.P
        ident = self.ID32 if dtype == F32 else self.ID16
        stp = P.scope()
        STG = P.sb_in(stp, "stg", [128, 2, D], F32)
        psb = [P.ps_in(stp, "xs", [128, 512], dtype) for _ in range(4)]
        ps_free = [0, 0, 0, 0]
        stg_free = [0, 0]
        for tt in range(NTT):
            buf = tt % 2
            evs = []
            for half in range(2):
                b = (tt % 2) * 2 + half
                ps = psb[b]
                mt = 0
                for k in range(4):
                    c = half * 4 + k
                    mt = P.op("tensor", lambda e, ps=ps, k=k, c=c, tt=tt: e.transpose(
                        ps[:, k * 128:(k + 1) * 128],
                        srcten[:, c, tt * 128:(tt + 1) * 128], ident[:]),
                        waits=[("EV", ps_free[b])], tick="MM")
                ev = P.op("scalar", lambda e, ps=ps, half=half, buf=buf: e.activation(
                    STG[:, buf, half * 512:half * 512 + 512], ps[:], AF.Copy),
                    waits=[("MM", mt), stg_free[buf]], tick="EV")
                ps_free[b] = ev
                evs.append(ev)
            dma_t = P.dma("sync", self.out[seq, tt * 128:(tt + 1) * 128, :],
                          STG[:, buf, :], waits=[("EV", evs[-1])])
            stg_free[buf] = dma_t
        P.op("sync", lambda e: e.nop(), waits=P.qdeps("sync"))
        stp.close()
        P.flush(f"storex{seq}")

    def alloc_ln_views(self, st, R, n=512):
        P = self.P
        psS = R["psS"]
        lp = dict(
            ps_s=psS[0:1, 0, 0, 0:n],
            ps_q=psS[0:1, 0, 1, 0:n],
            ps_mu=psS[:, 1, 0, 0:n],
            ps_rs=psS[:, 1, 1, 0:n],
            sq=P.sb_in(st, "sq", [128, 2, n], F32),
            mu=P.sb_in(st, "mu", [1, n], F32),
            e2r=P.sb_in(st, "e2r", [1, n], F32),
            var=P.sb_in(st, "var", [1, n], F32),
            rstd=P.sb_in(st, "rstd", [1, n], F32),
            sq_mm_prev=[0, 0], stats_ev_prev=0, apply_prev=0,
        )
        return lp

    def alloc_ln(self, st, n=512):
        P = self.P
        lnps = dict(
            ps_s=P.ps_in(st, "lns", [1, n], F32),
            ps_q=P.ps_in(st, "lnq", [1, n], F32),
            ps_mu=P.ps_in(st, "mub", [128, n], F32),
            ps_rs=P.ps_in(st, "rsb", [128, n], F32),
            sq=P.sb_in(st, "sq", [128, 2, n], F32),
            mu=P.sb_in(st, "mu", [1, n], F32),
            e2r=P.sb_in(st, "e2r", [1, n], F32),
            var=P.sb_in(st, "var", [1, n], F32),
            rstd=P.sb_in(st, "rstd", [1, n], F32),
            sq_mm_prev=[0, 0], stats_ev_prev=0, apply_prev=0,
        )
        return lnps

    def emit_ln_tb(self, lp, get_x, fp32_src, waits_base):
        P = self.P
        w0 = list(waits_base)
        mts = 0
        onesc = self.ONESC if fp32_src else self.ONESC16
        for c in range(NCH):
            mts = P.op("tensor", lambda e, c=c: e.matmul(
                lp["ps_s"][:], onesc[:], get_x(c),
                start=(c == 0), stop=(c == NCH - 1)),
                waits=w0 + [("EV", lp["stats_ev_prev"])], tick="MM")
        sq_mm = 0
        sq_ev = 0
        sq_dve = lp.get("sq_dve", False)
        for c in range(NCH):
            b = c % 2
            if sq_dve:
                sq_ev = P.op("vector", lambda e, c=c, b=b: e.tensor_tensor(
                    lp["sq"][:, b, :], get_x(c), get_x(c), ALU.mult),
                    waits=w0 + [("MM", lp["sq_mm_prev"][b])], tick="VV")
                sq_mm = P.op("tensor", lambda e, c=c, b=b: e.matmul(
                    lp["ps_q"][:], self.ONESC[:], lp["sq"][:, b, :],
                    start=(c == 0), stop=(c == NCH - 1)),
                    waits=[("VV", sq_ev), ("EV", lp["stats_ev_prev"])], tick="MM")
            else:
                sq_ev = P.op("scalar", lambda e, c=c, b=b: e.activation(
                    lp["sq"][:, b, :], get_x(c), AF.Square),
                    waits=w0 + [("MM", lp["sq_mm_prev"][b])], tick="EV")
                sq_mm = P.op("tensor", lambda e, c=c, b=b: e.matmul(
                    lp["ps_q"][:], self.ONESC[:], lp["sq"][:, b, :],
                    start=(c == 0), stop=(c == NCH - 1)),
                    waits=[("EV", max(sq_ev, lp["stats_ev_prev"]))], tick="MM")
            lp["sq_mm_prev"][b] = sq_mm
        ev_mu = P.op("scalar", lambda e: e.activation(
            lp["mu"][:], lp["ps_s"][:], AF.Copy, scale=1.0 / D),
            waits=[("MM", mts)], tick="EV")
        ev_e2 = P.op("scalar", lambda e: e.activation(
            lp["e2r"][:], lp["ps_q"][:], AF.Copy, scale=1.0 / D),
            waits=[("MM", sq_mm)], tick="EV")
        lp["stats_ev_prev"] = ev_e2
        v1 = P.op("vector", lambda e: e.tensor_tensor(
            lp["var"][:], lp["mu"][:], lp["mu"][:], ALU.mult),
            waits=[("EV", ev_mu)], tick="VV")
        v2 = P.op("vector", lambda e: e.scalar_tensor_tensor(
            out=lp["var"][:], in0=lp["e2r"][:], scalar=EPS, in1=lp["var"][:],
            op0=ALU.add, op1=ALU.subtract),
            waits=[("EV", ev_e2), ("VV", v1)], tick="VV")
        e_ln = P.op("scalar", lambda e: e.activation(
            lp["rstd"][:], lp["var"][:], AF.Ln), waits=[("VV", v2)], tick="EV")
        e_rs = P.op("scalar", lambda e: e.activation(
            lp["rstd"][:], lp["rstd"][:], AF.Exp, scale=-0.5),
            waits=[("EV", e_ln)], tick="EV")
        bmu = P.op("tensor", lambda e: e.matmul(
            lp["ps_mu"][:], self.ONES32[0:1, 0:128], lp["mu"][:],
            start=True, stop=True),
            waits=[("EV", ev_mu), ("VV", lp["apply_prev"])], tick="MM")
        brs = P.op("tensor", lambda e: e.matmul(
            lp["ps_rs"][:], self.ONES32[0:1, 0:128], lp["rstd"][:],
            start=True, stop=True),
            waits=[("EV", e_rs), ("VV", lp["apply_prev"])], tick="MM")
        return dict(bmu=bmu, brs=brs, src_mm=max(mts, sq_mm),
                    src_ev=0 if sq_dve else sq_ev,
                    src_vv=sq_ev if sq_dve else 0)

    def ln_router_phase(self, seq, layer):
        P = self.P
        d = self.dram
        E = 4 if layer == 0 else 8
        self.CW_TOK = P.sb_in(self.layer_st, "cwtok", [128, NTT, 8], F32)
        self.xn_ev = [0] * NTB
        st = P.scope()
        rw = P.sb_in(st, "rw", [128, NCH, E], F32)
        key = "L0_ffn_router_w" if layer == 0 else "L1_ffn_Wr"
        rload = P.dma("sync", rw[:], d[key].ap().rearrange("(c p) e -> p c e", p=128))
        rb = P.sb_in(st, "rb", [1, 16], F32)
        rbc = P.sb_in(st, "rbc", [8, 1], F32)
        if layer == 0:
            t1 = P.dma("sync", rb[0:1, 0:4], d["L0_ffn_router_b"].ap())
            t2 = P.dma("sync", rb[0:1, 8:12], d["L0_ffn_load_balancer"].ap())
            v = P.op("vector", lambda e: e.tensor_tensor(
                rb[0:1, 0:4], rb[0:1, 0:4], rb[0:1, 8:12], ALU.add),
                waits=[t1, t2], tick="VV")
        else:
            t1 = P.dma("sync", rb[0:1, 0:8], d["L1_ffn_br"].ap())
            v = P.op("vector", lambda e: e.tensor_copy(rb[0:1, 0:8], rb[0:1, 0:8]),
                     waits=[t1], tick="VV")
        psrb = self.R["psP"][0][0:8, 0:1]
        mtrb = P.op("tensor", lambda e, E=E: e.matmul(
            psrb[0:E, 0:1], rb[0:1, 0:E], self.ONES32[0:1, 0:1],
            start=True, stop=True), waits=[("VV", v)], tick="MM")
        evrb = P.op("scalar", lambda e, E=E: e.activation(
            rbc[0:E, :], psrb[0:E, 0:1], AF.Copy), waits=[("MM", mtrb)], tick="EV")
        self.R["psP_free"][0] = ("EV", evrb)

        lp = self.alloc_ln_views(st, self.R)
        xt32 = P.sb_in(st, "xt32", [128, NCH, 512], F32)
        lgf = P.sb_in(st, "lgf", [8, 512], F32)
        lgt = P.sb_in(st, "lgt", [128, 4, 8], F32)
        tmp = P.sb_in(st, "tmp", [128, 4, 8], F32)
        keep = P.sb_in(st, "keep", [128, 4, 8], F32)
        m1 = P.sb_in(st, "m1", [128, 4], F32)
        m2 = P.sb_in(st, "m2", [128, 4], F32)
        rden = P.sb_in(st, "rden", [128, 4], F32)
        ps_lg = self.R["psC"][0][0:8, :]
        ps_tm = self.R["psC"][1][:, 0:32].rearrange("p (a b) -> p a b", b=8)

        prev_rmm = 0
        prev_lg_ev = 0
        prev_tm_ev = 0
        for tb in range(NTB):
            tsl = slice(tb * TB, (tb + 1) * TB)
            r = self.emit_ln_tb(lp, lambda c, tsl=tsl: self.X[:, c, tsl], True, [])
            apply_vv = 0
            for c in range(NCH):
                vs = P.op("vector", lambda e, c=c, tsl=tsl: e.tensor_tensor(
                    xt32[:, c, :], self.X[:, c, tsl], lp["ps_mu"][:], ALU.subtract),
                    waits=[("MM", max(r["bmu"], prev_rmm))], tick="VV")
                vm = P.op("vector", lambda e, c=c: e.tensor_tensor(
                    xt32[:, c, :], xt32[:, c, :], lp["ps_rs"][:], ALU.mult),
                    waits=[("MM", r["brs"]), ("VV", vs)], tick="VV")
                apply_vv = vm
                xnev = P.op("scalar", lambda e, c=c, tsl=tsl: e.activation(
                    self.XN[:, c, tsl], xt32[:, c, :], AF.Copy),
                    waits=[("VV", vm)], tick="EV")
            self.xn_ev[tb] = xnev
            lp["apply_prev"] = apply_vv
            rmm = 0
            for c in range(NCH):
                rmm = P.op("tensor", lambda e, c=c: e.matmul(
                    ps_lg[0:E, :], rw[:, c, :], xt32[:, c, :],
                    start=(c == 0), stop=(c == NCH - 1)),
                    waits=[("VV", apply_vv), rload,
                           ("EV", prev_lg_ev)], tick="MM")
            prev_rmm = rmm
            ev_lg = P.op("scalar", lambda e: e.activation(
                lgf[0:E, :], ps_lg[0:E, :], AF.Identity, bias=rbc[0:E, 0:1]),
                waits=[("MM", rmm), ("EV", evrb)], tick="EV")
            prev_lg_ev = ev_lg
            mtt = 0
            for k in range(4):
                mtt = P.op("tensor", lambda e, k=k: e.transpose(
                    ps_tm[:, k, 0:E], lgf[0:E, k * 128:(k + 1) * 128],
                    self.ID32[0:E, 0:E]),
                    waits=[("EV", max(ev_lg, prev_tm_ev))], tick="MM")
            ev_tm = P.op("scalar", lambda e: e.activation(
                lgt[:, :, 0:E], ps_tm[:, :, 0:E], AF.Copy),
                waits=[("MM", mtt)], tick="EV")
            prev_tm_ev = ev_tm
            for k in range(4):
                lg = lgt[:, k, 0:E]
                cwk = self.CW_TOK[:, tb * 4 + k, 0:E]
                tm = tmp[:, k, 0:E]
                kp = keep[:, k, 0:E]
                m1c = m1[:, k:k + 1]
                m2c = m2[:, k:k + 1]
                rdc = rden[:, k:k + 1]
                a = P.op("vector", lambda e, lg=lg, m1c=m1c: e.tensor_reduce(
                    m1c, lg, AXL.X, ALU.max), waits=[("EV", ev_tm)], tick="VV")
                b_ = P.op("vector", lambda e, lg=lg, tm=tm, m1c=m1c: e.tensor_scalar(
                    tm, lg, m1c, None, ALU.subtract), waits=[("VV", a)], tick="VV")
                if layer == 0:
                    c_ = P.op("vector", lambda e, tm=tm, kp=kp: e.tensor_scalar(
                        kp, tm, 0.0, None, ALU.is_equal), waits=[("VV", b_)], tick="VV")
                    d_ = P.op("vector", lambda e, kp=kp, tm=tm, cwk=cwk: e.scalar_tensor_tensor(
                        out=cwk, in0=kp, scalar=-1e30, in1=tm, op0=ALU.mult, op1=ALU.add),
                        waits=[("VV", c_)], tick="VV")
                    e_ = P.op("vector", lambda e, cwk=cwk, m2c=m2c: e.tensor_reduce(
                        m2c, cwk, AXL.X, ALU.max), waits=[("VV", d_)], tick="VV")
                    f_ = P.op("vector", lambda e, tm=tm, kp=kp, m2c=m2c: e.tensor_scalar(
                        kp, tm, m2c, None, ALU.is_ge), waits=[("VV", e_)], tick="VV")
                    g_ = P.op("scalar", lambda e, tm=tm: e.activation(
                        tm, tm, AF.Exp), waits=[("VV", f_)], tick="EV")
                    h_ = P.op("scalar", lambda e, m2c=m2c, rdc=rdc: e.activation(
                        rdc, m2c, AF.Exp), waits=[("VV", f_)], tick="EV")
                    i_ = P.op("vector", lambda e, rdc=rdc: e.tensor_scalar_add(
                        rdc, rdc, 1.0), waits=[("EV", h_)], tick="VV")
                    j_ = P.op("vector", lambda e, rdc=rdc: e.reciprocal(
                        rdc, rdc), waits=[("VV", i_)], tick="VV")
                    k_ = P.op("vector", lambda e, cwk=cwk, tm=tm, kp=kp: e.tensor_tensor(
                        cwk, tm, kp, ALU.mult), waits=[("EV", g_)], tick="VV")
                    P.op("vector", lambda e, cwk=cwk, rdc=rdc: e.tensor_scalar(
                        cwk, cwk, rdc, None, ALU.mult),
                        waits=[("VV", max(k_, j_))], tick="VV")
                else:
                    c_ = P.op("scalar", lambda e, tm=tm: e.activation(
                        tm, tm, AF.Exp), waits=[("VV", b_)], tick="EV")
                    d_ = P.op("vector", lambda e, tm=tm, rdc=rdc: e.tensor_reduce(
                        rdc, tm, AXL.X, ALU.add), waits=[("EV", c_)], tick="VV")
                    e_ = P.op("vector", lambda e, rdc=rdc: e.reciprocal(
                        rdc, rdc), waits=[("VV", d_)], tick="VV")
                    f_ = P.op("vector", lambda e, tm=tm, rdc=rdc: e.tensor_scalar(
                        tm, tm, rdc, None, ALU.mult), waits=[("VV", e_)], tick="VV")
                    g_ = P.op("vector", lambda e, tm=tm, m1c=m1c: e.tensor_reduce(
                        m1c, tm, AXL.X, ALU.max), waits=[("VV", f_)], tick="VV")
                    h_ = P.op("vector", lambda e, tm=tm, kp=kp, m1c=m1c: e.tensor_scalar(
                        kp, tm, m1c, None, ALU.is_equal), waits=[("VV", g_)], tick="VV")
                    i_ = P.op("vector", lambda e, kp=kp, tm=tm, cwk=cwk: e.scalar_tensor_tensor(
                        out=cwk, in0=kp, scalar=-2.0, in1=tm, op0=ALU.mult, op1=ALU.add),
                        waits=[("VV", h_)], tick="VV")
                    j_ = P.op("vector", lambda e, cwk=cwk, m2c=m2c: e.tensor_reduce(
                        m2c, cwk, AXL.X, ALU.max), waits=[("VV", i_)], tick="VV")
                    k_ = P.op("vector", lambda e, tm=tm, kp=kp, m2c=m2c: e.tensor_scalar(
                        kp, tm, m2c, None, ALU.is_ge), waits=[("VV", j_)], tick="VV")
                    P.op("vector", lambda e, cwk=cwk, tm=tm, kp=kp: e.tensor_tensor(
                        cwk, tm, kp, ALU.mult), waits=[("VV", k_)], tick="VV")
        self.cw_ready = P.cnt["VV"]
        R = self.R
        R["psS_free"][0] = ("EV", lp["stats_ev_prev"])
        R["psS_free"][1] = ("VV", lp["apply_prev"])
        R["psC_free"][0] = ("EV", prev_lg_ev)
        R["psC_free"][1] = ("EV", prev_tm_ev)
        st.close()

    def xn2_phase(self, seq):
        P = self.P
        st = P.scope()
        lp = self.alloc_ln_views(st, self.R)
        if self.R["psS_free"][0] is not None:
            lp["stats_ev_prev"] = self.R["psS_free"][0][1]
        if self.R["psS_free"][1] is not None and self.R["psS_free"][1][0] == "VV":
            lp["apply_prev"] = self.R["psS_free"][1][1]
        t32 = P.sb_in(st, "t32", [128, 512], F32)
        prev_ap = 0
        self.xn2_ev = [0] * NTB
        for tb in range(NTB):
            tsl = slice(tb * TB, (tb + 1) * TB)
            r = self.emit_ln_tb(lp, lambda c, tsl=tsl: self.XN[:, c, tsl], False,
                                [("EV", self.xn_ev[tb])])
            apply_vv = 0
            for c in range(NCH):
                vs = P.op("vector", lambda e, c=c, tsl=tsl: e.tensor_tensor(
                    t32[:], self.XN[:, c, tsl], lp["ps_mu"][:], ALU.subtract),
                    waits=[("MM", r["bmu"]), ("EV", prev_ap)], tick="VV")
                vm = P.op("vector", lambda e: e.tensor_tensor(
                    t32[:], t32[:], lp["ps_rs"][:], ALU.mult),
                    waits=[("MM", r["brs"]), ("VV", vs)], tick="VV")
                apply_vv = vm
                prev_ap = P.op("scalar", lambda e, c=c, tsl=tsl: e.activation(
                    self.XN2[:, c, tsl], t32[:], AF.Identity,
                    scale=self.bias_col("l1_ln_w", c),
                    bias=self.bias_col("l1_ln_b", c)),
                    waits=[("VV", vm)], tick="EV")
            self.xn2_ev[tb] = prev_ap
            lp["apply_prev"] = apply_vv
        R = self.R
        R["psS_free"][0] = ("EV", lp["stats_ev_prev"])
        R["psS_free"][1] = ("VV", lp["apply_prev"])
        st.close()

    def attn_phase(self, seq, layer):
        P = self.P
        src = self.XN
        if layer == 1:
            self.attn_st = P.scope()
            self.XN2 = P.sb_in(self.attn_st, "XN2", [128, NCH, S], BF16)
            self.xn2_phase(seq)
            src = self.XN2
        R = self.R
        if layer == 0:
            st = P.scope()
            CTX = P.sb_in(st, "ctx", [128, NCH, S], BF16)
            stsb = P.scope()
            self.alloc_hg_sb(stsb, R)
            R["vrow"], R["vrow_dep"] = self.load_row(stsb, "L0_attn_b_v")
            self.emit_hg_set(seq, layer, src, list(range(NCH)), CTX, R)
            stsb.close()
            self.out_proj_l0(seq, CTX, st)
            P.flush(f"attn{seq}_{layer}")
            st.close()
        else:
            for half in range(2):
                st = P.scope()
                CTX = P.sb_in(st, "ctx", [128, 4, S], BF16)
                stsb = P.scope()
                self.alloc_hg_sb(stsb, R)
                R["vrow"], R["vrow_dep"] = self.load_row(
                    stsb, "L1_attn_in_proj_b", (2048, 3072))
                self.emit_hg_set(seq, layer, src,
                                 [half * 4 + lg for lg in range(4)], CTX, R)
                stsb.close()
                self.out_proj_l1(seq, CTX, half, st)
                P.flush(f"attn{seq}_{layer}_{half}")
                st.close()

    def cnt_guard(self):
        P = self.P
        return [(s, P.cnt[s]) for s in ("MM", "EV", "VV", "PP")]

    def alloc_hg(self, st):
        P = self.P
        R = dict(
            psS=P.ps_in(st, "psS", [128, 2, 2, 512], F32),
            psC=[P.ps_in(st, "psC", [128, 512], F32) for _ in range(2)],
            psP=[P.ps_in(st, "psP", [128, 512], F32) for _ in range(2)],
            wnat_free=[0, 0],
            wt_free=0,
            kt_free=0, qt_free=0,
            v_free=0,
            pt_free=[0, 0],
            psS_free=[None, None],
            psC_free=[None, None],
            psP_free=[None, None],
            sq_free=[0, 0],
            stat_free=[0, 0],
            rl_free=0,
            ct32_free=0,
            ones_pp=0,
        )
        return R

    def alloc_hg_sb(self, st, R):
        P = self.P
        R.update(
            wnat=[P.sb_in(st, "wnat", [128, D], F32)],
            WT=[P.sb_in(st, "wt", [128, 3, NCH, 128], BF16) for _ in range(2)],
            KT=[P.sb_in(st, "kt", [128, S], BF16) for _ in range(2)],
            QT=[P.sb_in(st, "qt", [128, S], BF16) for _ in range(2)],
            V=[P.sb_in(st, "v", [128, NTT, 130], BF16) for _ in range(2)],
            PT=P.sb_in(st, "pt", [128, 2, 2, 512], BF16),
            NEGC=[P.sb_in(st, "negc", [128, 2, 4], F32) for _ in range(2)],
            CT32=P.sb_in(st, "ct32", [64, 512], F32),
            RLROW=P.sb_in(st, "rl", [1, 512], F32),
            SQ=P.sb_in(st, "sqn", [128, 2, 512], BF16),
            statq=[[P.sb_in(st, "stq", [1, 8], F32) for _ in range(2)]
                   for _ in range(2)],
            statk=[[P.sb_in(st, "stk", [1, 8], F32) for _ in range(2)]
                   for _ in range(2)],
        )
        g = self.cnt_guard()
        pp = P.op("gpsimd", lambda e: e.memset(R["V"][0][:, :, 64:65], 1.0),
                  waits=g, tick="PP")
        pp = P.op("gpsimd", lambda e: e.memset(R["V"][0][:, :, 129:130], 1.0),
                  tick="PP")
        pp = P.op("gpsimd", lambda e: e.memset(R["V"][1][:, :, 64:65], 1.0),
                  tick="PP")
        R["ones_pp"] = P.op("gpsimd", lambda e: e.memset(R["V"][1][:, :, 129:130], 1.0),
                            tick="PP")
        R["wnat_free"] = [0]
        R["wt_free"] = [0, 0]
        R["kt_free"] = [0, 0]
        R["qt_free"] = [0, 0]
        R["v_free"] = [0, 0]
        R["sq_free"] = [0, 0]
        R["stat_free"] = [[0, 0], [0, 0]]
        return R

    def emit_hg_proj(self, seq, layer, src, ch, par, R, out):
        P = self.P
        d = self.dram
        vrow, vrow_dep = R["vrow"], R["vrow_dep"]
        xn_deps = self.xn2_ev if layer == 1 else self.xn_ev
        WT, KT, QT, V = R["WT"][par], R["KT"][par], R["QT"][par], R["V"][par]
        NEGC, SQ = R["NEGC"][par], R["SQ"]
        statq, statk = R["statq"][par], R["statk"][par]
        psP = R["psP"]
        wkeys = {
            0: [("L0_attn_w_q", 0, 0.125), ("L0_attn_w_k", 0, None),
                ("L0_attn_w_v", 0, None)],
            1: [("L1_attn_in_proj_w", 0, 0.125), ("L1_attn_in_proj_w", 1024, None),
                ("L1_attn_in_proj_w", 2048, None)],
        }[layer]
        wt_ready = 0
        for mi, (key, roff, scale) in enumerate(wkeys):
            wb = 0
            wnat = R["wnat"][wb]
            dma_t = P.dma("sync", wnat[:],
                          d[key][roff + ch * 128: roff + (ch + 1) * 128, :],
                          waits=[("MM", R["wnat_free"][wb])])
            for g in range(2):
                ps = psP[g]
                mt = 0
                for k in range(4):
                    kc = g * 4 + k
                    mt = P.op("tensor", lambda e, ps=ps, k=k, kc=kc, wnat=wnat: e.transpose(
                        ps[:, k * 128:(k + 1) * 128],
                        wnat[:, kc * 128:(kc + 1) * 128], self.ID32[:]),
                        waits=[dma_t, psP_w(R, g)], tick="MM")
                if g == 1:
                    R["wnat_free"][wb] = mt
                if scale is None:
                    wt_ready = P.op("vector", lambda e, ps=ps, mi=mi, g=g: e.tensor_copy(
                        WT[:, mi, g * 4:g * 4 + 4, :],
                        ps[:].rearrange("p (c t) -> p c t", t=128)),
                        waits=[("MM", max(mt, R["wt_free"][par]))], tick="VV")
                else:
                    wt_ready = P.op("vector", lambda e, ps=ps, mi=mi, g=g, s=scale: e.tensor_scalar(
                        WT[:, mi, g * 4:g * 4 + 4, :],
                        ps[:].rearrange("p (c t) -> p c t", t=128),
                        s, None, ALU.mult),
                        waits=[("MM", max(mt, R["wt_free"][par]))], tick="VV")
                R["psP_free"][g] = ("VV", wt_ready)
                yield

        bq, bk = ("b_q8", "b_k") if layer == 0 else ("inp_b8", "inp_bk")
        kq_ev = 0
        for mi, (dst, bias, rd_free) in enumerate(
                ((QT, bq, R["qt_free"][par]), (KT, bk, R["kt_free"][par]))):
            for tb in range(NTB):
                tsl = slice(tb * TB, (tb + 1) * TB)
                b = (mi * NTB + tb) % 2
                ps = psP[b]
                mt = 0
                for kc in range(NCH):
                    mt = P.op("tensor", lambda e, ps=ps, mi=mi, kc=kc, tsl=tsl: e.matmul(
                        ps[:], WT[:, mi, kc, :], src[:, kc, tsl],
                        start=(kc == 0), stop=(kc == NCH - 1)),
                        waits=[("VV", wt_ready), ("EV", xn_deps[tb]),
                               psP_w(R, b)], tick="MM")
                kq_ev = P.op("vector", lambda e, ps=ps, dst=dst, bias=bias, tsl=tsl: e.tensor_scalar(
                    dst[:, tsl], ps[:], self.bias_col(bias, ch), None, ALU.add),
                    waits=[("MM", max(mt, rd_free))], tick="VV")
                R["psP_free"][b] = ("VV", kq_ev)
                yield
        v_ev = 0
        last_proj_mm = 0
        for tt in range(NTT):
            b = tt % 2
            ps = psP[b]
            for kc in range(NCH):
                P.op("tensor", lambda e, ps=ps, kc=kc, tt=tt: e.matmul(
                    ps[:, 0:128], src[:, kc, tt * 128:(tt + 1) * 128],
                    WT[:, 2, kc, :], start=(kc == 0), stop=False),
                    waits=[("VV", wt_ready), ("EV", xn_deps[tt // 4]),
                           psP_w(R, b)], tick="MM")
            mt = P.op("tensor", lambda e, ps=ps: e.matmul(
                ps[:, 0:128], self.ONES32[0:1, 0:128],
                vrow[0:1, ch * 128:(ch + 1) * 128], start=False, stop=True),
                waits=[vrow_dep], tick="MM")
            last_proj_mm = mt
            P.op("vector", lambda e, ps=ps, tt=tt: e.tensor_copy(
                V[:, tt, 0:64], ps[:, 0:64]),
                waits=[("MM", max(mt, R["v_free"][par]))], tick="VV")
            v_ev = P.op("vector", lambda e, ps=ps, tt=tt: e.tensor_copy(
                V[:, tt, 65:129], ps[:, 64:128]), waits=[("MM", mt)], tick="VV")
            R["psP_free"][b] = ("VV", v_ev)
            if tt % 2 == 1:
                yield

        sqf = R["sq_free"]
        negc_ev = 0
        sidx = 0
        for hj in range(2):
            last_red = 0
            for (dstt, stat) in ((QT, statq[hj]), (KT, statk[hj])):
                for tb in range(NTB):
                    tsl = slice(tb * TB, (tb + 1) * TB)
                    b = sidx % 2
                    sidx += 1
                    evq = P.op("gpsimd", lambda e, dstt=dstt, tsl=tsl, b=b: e.tensor_mul(
                        SQ[:, b, :], dstt[:, tsl], dstt[:, tsl]),
                        waits=[("VV", max(kq_ev, v_ev)), ("MM", sqf[b])], tick="PP")
                    mmb = P.op("tensor", lambda e, hj=hj, b=b: e.matmul(
                        psP[b][0:1, :], self.MASKH[:, hj:hj + 1], SQ[:, b, :],
                        start=True, stop=True),
                        waits=[("PP", evq), psP_w(R, b)], tick="MM")
                    sqf[b] = mmb
                    last_red = P.op("vector", lambda e, stat=stat, tb=tb, b=b: e.tensor_reduce(
                        stat[0:1, tb:tb + 1], psP[b][0:1, :], AXL.X, ALU.max),
                        waits=[("MM", max(mmb, R["stat_free"][par][hj]))], tick="VV")
                    R["psP_free"][b] = ("VV", last_red)
                    if tb % 2 == 1:
                        yield
            vk = P.op("vector", lambda e, hj=hj, statk=statk: e.tensor_reduce(
                statk[hj][0:1, 4:5], statk[hj][0:1, 0:4], AXL.X, ALU.max),
                waits=[("VV", last_red)], tick="VV")
            vq = P.op("vector", lambda e, hj=hj, statq=statq, statk=statk: e.tensor_scalar(
                statq[hj][0:1, 0:4], statq[hj][0:1, 0:4],
                statk[hj][0:1, 4:5], None, ALU.mult),
                waits=[("VV", vk)], tick="VV")
            el = P.op("scalar", lambda e, hj=hj, statq=statq: e.activation(
                statq[hj][0:1, 0:4], statq[hj][0:1, 0:4], AF.Ln),
                waits=[("VV", vq)], tick="EV")
            ee = P.op("scalar", lambda e, hj=hj, statq=statq: e.activation(
                statq[hj][0:1, 0:4], statq[hj][0:1, 0:4], AF.Exp, scale=0.5),
                waits=[("EV", el)], tick="EV")
            vm = P.op("vector", lambda e, hj=hj, statq=statq: e.tensor_scalar_mul(
                statq[hj][0:1, 0:4], statq[hj][0:1, 0:4], -1.05),
                waits=[("EV", ee)], tick="VV")
            mtb = P.op("tensor", lambda e, hj=hj, statq=statq: e.matmul(
                psP[0][:, 0:4], self.ONES32[0:1, 0:128], statq[hj][0:1, 0:4],
                start=True, stop=True),
                waits=[("VV", vm), psP_w(R, 0)], tick="MM")
            R["stat_free"][par][hj] = mtb
            negc_ev = P.op("vector", lambda e, hj=hj, NEGC=NEGC: e.tensor_copy(
                NEGC[:, hj, :], psP[0][:, 0:4]),
                waits=[("MM", mtb)], tick="VV")
            R["psP_free"][0] = ("VV", negc_ev)
        out.update(kq_ev=kq_ev, v_ev=v_ev, negc_ev=negc_ev,
                   last_proj_mm=last_proj_mm)
        yield

    def emit_hg_scores(self, seq, layer, par, CTX, ctxcol, R, out, feeder):
        P = self.P
        KT, QT, V, PT = R["KT"][par], R["QT"][par], R["V"][par], R["PT"]
        NEGC, CT32, RLROW = R["NEGC"][par], R["CT32"], R["RLROW"]
        psS, psC = R["psS"], R["psC"]
        kq_ev = max(out["kq_ev"], out["v_ev"])
        negc_ev = out["negc_ev"]
        last_score_mm = 0
        last_ctx_mm = 0
        pend = None
        prev_ctx = None

        def emit_ctx_hj(ktp, hj, ctx_mms, exp_evs):
            nonlocal last_ctx_mm
            for pr in range(2):
                kt = 2 * ktp + pr
                cm = P.op("tensor", lambda e, hj=hj, pr=pr, kt=kt: e.matmul(
                    psC[hj][0:65, :], V[:, kt, hj * 65:hj * 65 + 65],
                    PT[:, hj, pr, :], start=(kt == 0), stop=(kt == NTT - 1)),
                    waits=[("EV", exp_evs[hj]),
                           (R["psC_free"][hj] if kt == 0 else None),
                           ("VV", out["v_ev"] if kt == 0 else 0),
                           ("PP", R["ones_pp"])], tick="MM")
                R["pt_free"][hj] = cm
                ctx_mms[hj] = cm
                last_ctx_mm = cm

        for qb in range(NTB):
            qsl = slice(qb * TB, (qb + 1) * TB)
            ctx_mms = [0, 0]
            for ktp in range(NTT // 2):
                exp_evs = [0, 0]
                for hj in range(2):
                    base = 64 * hj
                    smm = 0
                    for pr in range(2):
                        kt = 2 * ktp + pr
                        smm = P.op(
                            "tensor", lambda e, hj=hj, pr=pr, base=base, kt=kt, qsl=qsl: e.matmul(
                                psS[:, hj, pr, :],
                                KT[base:base + 64, kt * 128:(kt + 1) * 128],
                                QT[base:base + 64, qsl], start=True, stop=True),
                            waits=[R["psS_free"][hj],
                                   ("VV", max(negc_ev, kq_ev))],
                            tick="MM")
                    last_score_mm = smm
                    if prev_ctx is not None:
                        emit_ctx_hj(prev_ctx[0], hj, prev_ctx[1], prev_ctx[2])
                    ev = P.op("scalar", lambda e, hj=hj, qb=qb, NEGC=NEGC: e.activation(
                        PT[:, hj, :, :], psS[:, hj, :, :], AF.Exp,
                        bias=NEGC[:, hj, qb:qb + 1]),
                        waits=[("MM", max(smm, R["pt_free"][hj]))],
                        tick="EV")
                    R["psS_free"][hj] = ("EV", ev)
                    exp_evs[hj] = ev
                if ktp == 0 and pend is not None:
                    pend()
                    pend = None
                next(feeder, None)
                prev_ctx = (ktp, ctx_mms, exp_evs)

            def fin(ctx_mms=ctx_mms, qsl=qsl):
                for hj in range(2):
                    e_ln = P.op("scalar", lambda e, hj=hj: e.activation(
                        RLROW[:], psC[hj][64:65, :], AF.Ln),
                        waits=[("MM", max(ctx_mms[hj], R["rl_free"]))], tick="EV")
                    e_rl = P.op("scalar", lambda e: e.activation(
                        RLROW[:], RLROW[:], AF.Exp, scale=-1.0),
                        waits=[("EV", e_ln)], tick="EV")
                    ev_ct = P.op("vector", lambda e, hj=hj: e.tensor_copy(
                        CT32[:], psC[hj][0:64, :]),
                        waits=[("MM", ctx_mms[hj]), ("VV", R["ct32_free"])], tick="VV")
                    mt_rb = P.op("tensor", lambda e, hj=hj: e.matmul(
                        psC[hj][0:64, :], self.ONES32[0:1, 0:64], RLROW[:],
                        start=True, stop=True),
                        waits=[("EV", e_rl), ("VV", ev_ct)], tick="MM")
                    R["rl_free"] = mt_rb
                    v_fin = P.op("vector", lambda e, hj=hj, qsl=qsl: e.tensor_tensor(
                        CTX[hj * 64:hj * 64 + 64, ctxcol, qsl], CT32[:],
                        psC[hj][0:64, :], ALU.mult),
                        waits=[("MM", mt_rb), ("VV", ev_ct)], tick="VV")
                    R["psC_free"][hj] = ("VV", v_fin)
                    R["ct32_free"] = v_fin
            pend = fin
            next(feeder, None)
        if prev_ctx is not None:
            for hj in range(2):
                emit_ctx_hj(prev_ctx[0], hj, prev_ctx[1], prev_ctx[2])
        if pend is not None:
            pend()
        for _ in feeder:
            pass
        R["kt_free"][par] = last_score_mm
        R["qt_free"][par] = last_score_mm
        R["v_free"][par] = last_ctx_mm
        R["wt_free"][par] = out["last_proj_mm"]
        self.ctx_ready = P.cnt["VV"]

    def emit_hg_set(self, seq, layer, src, chs, CTX, R):
        outs = [dict() for _ in chs]
        g = self.emit_hg_proj(seq, layer, src, chs[0], 0, R, outs[0])
        for _ in g:
            pass
        for i in range(len(chs)):
            if i + 1 < len(chs):
                feeder = self.emit_hg_proj(seq, layer, src, chs[i + 1],
                                           (i + 1) % 2, R, outs[i + 1])
            else:
                feeder = iter(())
            self.emit_hg_scores(seq, layer, i % 2, CTX,
                                chs[i] if layer == 0 else chs[i] % 4,
                                R, outs[i], feeder)

    def prep_wot(self, st, key, kclist, guard=()):
        P = self.P
        d = self.dram
        nk = len(kclist)
        WOT = P.sb_in(st, "wot", [128, nk, D], BF16)
        wnat = P.sb_in(st, "wnat2", [128, D], F32)
        ps_tr = [self.R["psP"][0], self.R["psP"][1]]
        tr_free = [0, 0]
        prev_use = 0
        for co in range(8):
            dma_t = P.dma("sync", wnat[:], d[key][co * 128:(co + 1) * 128, :],
                          waits=list(guard) + [("MM", prev_use)])
            for g in range(0, nk, 4):
                gn = min(4, nk - g)
                b = (g // 4) % 2
                ps = ps_tr[b]
                mt = 0
                for j in range(gn):
                    kc = kclist[g + j]
                    mt = P.op("tensor", lambda e, ps=ps, j=j, kc=kc: e.transpose(
                        ps[:, j * 128:(j + 1) * 128],
                        wnat[:, kc * 128:(kc + 1) * 128], self.ID32[:]),
                        waits=list(guard) + [dma_t, ("EV", tr_free[b])], tick="MM")
                prev_use = mt
                tr_free[b] = P.op("scalar", lambda e, ps=ps, g=g, gn=gn, co=co: e.activation(
                    WOT[:, g:g + gn, co * 128:(co + 1) * 128],
                    ps[:, 0:gn * 128].rearrange("p (c t) -> p c t", t=128), AF.Copy),
                    waits=[("MM", mt)], tick="EV")
        return WOT, max(tr_free)

    def out_proj_l0(self, seq, CTX, st):
        P = self.P
        guard = self.cnt_guard()
        WOT, wot_ev = self.prep_wot(st, "L0_attn_w_o", list(range(8)), guard)
        borow, bo_dep = self.load_row(st, "L0_attn_b_o", waits=guard)
        Y32 = P.sb_in(st, "y32", [128, NCH, 512], F32)
        T32 = P.sb_in(st, "t32o", [128, 512], F32)
        R = self.R
        lp = self.alloc_ln_views(st, R)
        if R["psS_free"][0] is not None:
            lp["stats_ev_prev"] = R["psS_free"][0][1]
        ps_o = [R["psC"][0], R["psC"][1]]
        pso_free = [0, 0]
        prev_x_add = 0
        prev_src_mm = 0
        prev_src_ev = 0
        for tb in range(NTB):
            first = guard if tb == 0 else []
            tsl = slice(tb * TB, (tb + 1) * TB)
            yvv = 0
            for oc in range(NCH):
                b = oc % 2
                ps = ps_o[b]
                for kc in range(NCH):
                    P.op("tensor", lambda e, ps=ps, kc=kc, oc=oc, tsl=tsl: e.matmul(
                        ps[:], WOT[:, kc, oc * 128:(oc + 1) * 128], CTX[:, kc, tsl],
                        start=(kc == 0), stop=False),
                        waits=first + [("EV", wot_ev),
                                       ("VV", max(pso_free[b], self.ctx_ready))],
                        tick="MM")
                mt = P.op("tensor", lambda e, ps=ps, oc=oc: e.matmul(
                    ps[:], borow[0:1, oc * 128:(oc + 1) * 128], self.ONES32[0:1, :],
                    start=False, stop=True), waits=[bo_dep], tick="MM")
                yvv = P.op("vector", lambda e, ps=ps, oc=oc, tsl=tsl: e.tensor_tensor(
                    Y32[:, oc, :], self.XN[:, oc, tsl], ps[:], ALU.add),
                    waits=first + [("MM", max(mt, prev_src_mm)),
                                   ("EV", prev_src_ev)], tick="VV")
                pso_free[b] = yvv
            r = self.emit_ln_tb(lp, lambda c: Y32[:, c, :], True,
                                first + [("VV", yvv)])
            prev_src_mm = r["src_mm"]
            prev_src_ev = r["src_ev"]
            apply_vv = 0
            for oc in range(NCH):
                vs = P.op("vector", lambda e, oc=oc: e.tensor_tensor(
                    T32[:], Y32[:, oc, :], lp["ps_mu"][:], ALU.subtract),
                    waits=first + [("MM", r["bmu"]), ("PP", prev_x_add)], tick="VV")
                vm = P.op("vector", lambda e: e.tensor_tensor(
                    T32[:], T32[:], lp["ps_rs"][:], ALU.mult),
                    waits=[("MM", r["brs"]), ("VV", vs)], tick="VV")
                apply_vv = vm
                vg = P.op("vector", lambda e, oc=oc: e.tensor_scalar(
                    T32[:], T32[:], self.bias_col("ln_g", oc),
                    self.bias_col("ln_b", oc), ALU.mult, ALU.add),
                    waits=[("VV", vm)], tick="VV")
                prev_x_add = P.op("gpsimd", lambda e, oc=oc, tsl=tsl: e.tensor_tensor(
                    self.X[:, oc, tsl], self.X[:, oc, tsl], T32[:], ALU.add),
                    waits=[("VV", vg)], tick="PP")
            lp["apply_prev"] = apply_vv

    def out_proj_l1(self, seq, CTX, half, st):
        P = self.P
        guard = self.cnt_guard()
        kclist = [half * 4 + j for j in range(4)]
        WOT, wot_ev = self.prep_wot(st, "L1_attn_out_proj_w", kclist, guard)
        borow, bo_dep = (None, None)
        if half == 0:
            borow, bo_dep = self.load_row(st, "L1_attn_out_proj_b", waits=guard)
        ps_o = [self.R["psC"][0], self.R["psC"][1]]
        pso_free = [0, 0]
        for tb in range(NTB):
            first = guard if tb == 0 else []
            tsl = slice(tb * TB, (tb + 1) * TB)
            for oc in range(NCH):
                b = (tb * NCH + oc) % 2
                ps = ps_o[b]
                mt = 0
                for j in range(4):
                    mt = P.op("tensor", lambda e, ps=ps, j=j, oc=oc, tsl=tsl: e.matmul(
                        ps[:], WOT[:, j, oc * 128:(oc + 1) * 128], CTX[:, j, tsl],
                        start=(j == 0), stop=(j == 3 and half == 1)),
                        waits=first + [("EV", wot_ev),
                                       ("VV", max(pso_free[b], self.ctx_ready))],
                        tick="MM")
                if half == 0:
                    mt = P.op("tensor", lambda e, ps=ps, oc=oc: e.matmul(
                        ps[:], borow[0:1, oc * 128:(oc + 1) * 128],
                        self.ONES32[0:1, :], start=False, stop=True),
                        waits=[bo_dep], tick="MM")
                pso_free[b] = P.op("vector", lambda e, ps=ps, oc=oc, tsl=tsl: e.scalar_tensor_tensor(
                    out=self.X[:, oc, tsl], in0=ps[:], scalar=1.0,
                    in1=self.X[:, oc, tsl], op0=ALU.mult, op1=ALU.add),
                    waits=[("MM", mt)], tick="VV")

    def lact_phase(self, seq):
        P = self.P
        d = self.dram
        st = P.scope()
        W1NAT = P.sb_in(st, "w1n", [128, 2, 256], BF16)
        W0T = P.sb_in(st, "w0t", [128, 2, 256], BF16)
        W2T = P.sb_in(st, "w2t", [128, 2, 256], BF16)
        W1T = P.sb_in(st, "w1t", [128, 2, 256], BF16)
        wsm = P.sb_in(st, "wsm", [128, 2, 256], F32)
        ps_w = P.ps_in(st, "psw", [128, 256], F32)
        B = dict(
            KVT=P.sb_in(st, "kvt", [128, 8, 256], BF16),
            HID=P.sb_in(st, "hid", [128, 8, 256], BF16),
            DG=P.sb_in(st, "dgb", [128, 8, 256], BF16),
            DP=P.sb_in(st, "dpb", [128, 8, 256], BF16),
            XXT=P.sb_in(st, "xxt", [128, 2, 8, 128], BF16),
            SGf=P.sb_in(st, "sgf", [128, 256], F32),
            Hf=P.sb_in(st, "hf", [128, 256], F32),
            Gf=P.sb_in(st, "gf", [128, 256], F32),
            DHf=P.sb_in(st, "dhf", [128, 256], F32),
            T1f=P.sb_in(st, "t1f", [128, 256], F32),
            DHPf=P.sb_in(st, "dhpf", [128, 256], F32),
            W0N=P.sb_in(st, "w0nb", [128, 2, 256], BF16),
            W2N=P.sb_in(st, "w2nb", [128, 2, 256], BF16),
            W1N=P.sb_in(st, "w1nb", [128, 2, 256], BF16),
            HQf=P.sb_in(st, "hqf", [128, 256], F32),
            XXb=P.sb_in(st, "xxb", [128, 256], BF16),
            ps_a=P.ps_in(st, "psa", [128, 256], F32),
            ps_b=P.ps_in(st, "psb", [128, 256], F32),
            ps_c=P.ps_in(st, "psc", [128, 256], F32),
            ps_t=P.ps_in(st, "pst", [128, 256], BF16),
            ps_u=P.ps_in(st, "psu", [128, 256], F32),
            ps_o=P.ps_in(st, "pso2", [128, 512], F32),
        )
        statics_ev = 0
        prev_wsm_mm = 0
        for h in range(4):
            mm_gate = P.cnt["MM"]
            w1nat_dep = P.dma("gpsimd", W1NAT[:],
                              d["L1_attn_w1"][h].rearrange("(c p) e -> p c e", p=128),
                              waits=[("MM", mm_gate)])
            for (dst, key) in ((W0T, "L1_attn_w0"), (W2T, "L1_attn_w2"),
                               (W1T, "L1_attn_w1")):
                dma_t = P.dma("sync", wsm[:],
                              d[key][h].rearrange("(c p) e -> p c e", p=128),
                              waits=[("MM", max(mm_gate, prev_wsm_mm))])
                for dc in range(2):
                    mt = 0
                    for er in range(2):
                        mt = P.op("tensor", lambda e, dc=dc, er=er: e.transpose(
                            ps_w[:, er * 128:(er + 1) * 128],
                            wsm[:, er, dc * 128:(dc + 1) * 128], self.ID32[:]),
                            waits=[dma_t, ("EV", statics_ev)], tick="MM")
                    prev_wsm_mm = mt
                    statics_ev = P.op("scalar", lambda e, dst=dst, dc=dc: e.activation(
                        dst[:, dc, :], ps_w[:], AF.Copy), waits=[("MM", mt)], tick="EV")
            for n in range(2):
                gate = [("EV", P.cnt["EV"]), ("VV", P.cnt["VV"])]
                self.lact_unit(seq, h, n, B, W1NAT, W0T, W2T, W1T,
                               [("EV", statics_ev), w1nat_dep] + gate)
        st.close()

    def lact_unit(self, seq, h, n, B, W1NAT, W0T, W2T, W1T, wdeps):
        P = self.P
        KVT, HID, DG, DP, XXT = B["KVT"], B["HID"], B["DG"], B["DP"], B["XXT"]
        SGf, Hf, Gf, DHf, T1f = B["SGf"], B["Hf"], B["Gf"], B["DHf"], B["T1f"]
        DHPf, W0N, W2N, W1N = B["DHPf"], B["W0N"], B["W2N"], B["W1N"]
        HQf, XXb = B["HQf"], B["XXb"]
        ps_a, ps_b, ps_c = B["ps_a"], B["ps_b"], B["ps_c"]
        ps_t, ps_u, ps_o = B["ps_t"], B["ps_u"], B["ps_o"]
        base_t = n * 1024
        ch0 = 2 * h

        kv_ev = 0
        prev = 0
        for mi in range(8):
            mm = 0
            for kc in range(2):
                mm = P.op("tensor", lambda e, mi=mi, kc=kc: e.transpose(
                    ps_t[:, kc * 128:(kc + 1) * 128],
                    self.XN2[:, ch0 + kc,
                             base_t + mi * 128: base_t + (mi + 1) * 128],
                    self.ID16[:]),
                    waits=wdeps + [("EV", prev)], tick="MM")
            prev = P.op("scalar", lambda e, mi=mi: e.activation(
                KVT[:, mi, :], ps_t[:], AF.Copy), waits=[("MM", mm)], tick="EV")
            kv_ev = prev

        pp_done = 0
        psa_vv = 0
        ev_free = [0, 0, 0]
        for mi in range(8):
            csl = slice(base_t + mi * 128, base_t + (mi + 1) * 128)
            mmg = 0
            for kc in range(2):
                mmg = P.op("tensor", lambda e, kc=kc, csl=csl: e.matmul(
                    ps_a[:], self.XN2[:, ch0 + kc, csl], W0T[:, kc, :],
                    start=(kc == 0), stop=(kc == 1)),
                    waits=wdeps + [("VV", max(psa_vv, pp_done))], tick="MM")
            ev_sg = P.op("scalar", lambda e: e.activation(
                SGf[:], ps_a[:], AF.Sigmoid), waits=[("MM", mmg)], tick="EV")
            ev_h = P.op("vector", lambda e: e.tensor_tensor(
                Hf[:], SGf[:], ps_a[:], ALU.mult),
                waits=[("EV", ev_sg)], tick="VV")
            psa_vv = ev_h
            mm2 = 0
            for kc in range(2):
                mm2 = P.op("tensor", lambda e, kc=kc, csl=csl: e.matmul(
                    ps_b[:], self.XN2[:, ch0 + kc, csl], W2T[:, kc, :],
                    start=(kc == 0), stop=(kc == 1)),
                    waits=wdeps + [("EV", ev_free[1]), ("VV", pp_done)], tick="MM")
            ev_g = P.op("scalar", lambda e: e.activation(
                Gf[:], ps_b[:], AF.Copy), waits=[("MM", mm2)], tick="EV")
            ev_free[1] = ev_g
            mm3 = 0
            for kc in range(2):
                mm3 = P.op("tensor", lambda e, kc=kc, csl=csl: e.matmul(
                    ps_c[:], self.XN2[:, ch0 + kc, csl], W1NAT[:, kc, :],
                    start=(kc == 0), stop=(kc == 1)),
                    waits=wdeps + [("EV", ev_free[2]), ("VV", pp_done)], tick="MM")
            ev_dh = P.op("scalar", lambda e: e.activation(
                DHf[:], ps_c[:], AF.Copy), waits=[("MM", mm3)], tick="EV")
            ev_free[2] = ev_dh
            p1 = P.op("vector", lambda e, mi=mi: e.tensor_tensor(
                HID[:, mi, :], Hf[:], Gf[:], ALU.mult),
                waits=[("VV", ev_h), ("EV", ev_g)], tick="VV")
            p2 = P.op("vector", lambda e: e.tensor_tensor(
                DHPf[:], DHf[:], Gf[:], ALU.mult),
                waits=[("EV", max(ev_dh, ev_g))], tick="VV")
            p3 = P.op("vector", lambda e, mi=mi: e.tensor_tensor(
                DG[:, mi, :], DHf[:], Hf[:], ALU.mult),
                waits=[("EV", ev_dh), ("VV", ev_h)], tick="VV")
            ev_t1 = P.op("scalar", lambda e: e.activation(
                T1f[:], SGf[:], AF.Identity, scale=-1.0, bias=1.0),
                waits=[("EV", ev_sg), ("VV", pp_done)], tick="EV")
            p4 = P.op("vector", lambda e: e.tensor_tensor(
                T1f[:], Hf[:], T1f[:], ALU.mult),
                waits=[("EV", ev_t1), ("VV", p3)], tick="VV")
            p5 = P.op("vector", lambda e: e.tensor_tensor(
                T1f[:], SGf[:], T1f[:], ALU.add), waits=[("VV", p4)], tick="VV")
            pp_done = P.op("vector", lambda e, mi=mi: e.tensor_tensor(
                DP[:, mi, :], DHPf[:], T1f[:], ALU.mult),
                waits=[("VV", max(p5, p2))], tick="VV")

        upd_ev = 0
        for (OUT, LHS, WSTAT) in ((W1N, HID, W1T), (W0N, None, W0T), (W2N, None, W2T)):
            RHS = DP if OUT is W0N else (DG if OUT is W2N else None)
            for m in range(2):
                mm = 0
                for mi in range(8):
                    if OUT is W1N:
                        la = HID[:, mi, m * 128:(m + 1) * 128]
                        ra = KVT[:, mi, :]
                    else:
                        la = KVT[:, mi, m * 128:(m + 1) * 128]
                        ra = RHS[:, mi, :]
                    mm = P.op("tensor", lambda e, la=la, ra=ra, mi=mi: e.matmul(
                        ps_u[:], la, ra, start=(mi == 0), stop=(mi == 7)),
                        waits=[("VV", max(pp_done, upd_ev)), ("EV", kv_ev)],
                        tick="MM")
                upd_ev = P.op("vector", lambda e, OUT=OUT, WSTAT=WSTAT, m=m: e.tensor_tensor(
                    OUT[:, m, :], WSTAT[:, m, :], ps_u[:], ALU.add),
                    waits=[("MM", mm)], tick="VV")

        xxt_ev = 0
        psa_free = ("VV", psa_vv)
        psb_free = ("EV", ev_free[1])
        prev_vxx = 0
        prev_mmx = 0
        for mi in range(8):
            csl = slice(base_t + mi * 128, base_t + (mi + 1) * 128)
            mmq = 0
            for kc in range(2):
                mmq = P.op("tensor", lambda e, kc=kc, csl=csl: e.matmul(
                    ps_a[:], self.XN2[:, ch0 + kc, csl], W0N[:, kc, :],
                    start=(kc == 0), stop=(kc == 1)),
                    waits=[("VV", upd_ev), psa_free], tick="MM")
            ev_sgq = P.op("scalar", lambda e: e.activation(
                HQf[:], ps_a[:], AF.Sigmoid),
                waits=[("MM", mmq)], tick="EV")
            ev_hq = P.op("vector", lambda e: e.tensor_tensor(
                HQf[:], HQf[:], ps_a[:], ALU.mult),
                waits=[("EV", ev_sgq), ("VV", prev_vxx)], tick="VV")
            psa_free = ("VV", ev_hq)
            mmq2 = 0
            for kc in range(2):
                mmq2 = P.op("tensor", lambda e, kc=kc, csl=csl: e.matmul(
                    ps_b[:], self.XN2[:, ch0 + kc, csl], W2N[:, kc, :],
                    start=(kc == 0), stop=(kc == 1)),
                    waits=[("VV", upd_ev), psb_free], tick="MM")
            v_xx = P.op("vector", lambda e: e.tensor_tensor(
                XXb[:], HQf[:], ps_b[:], ALU.mult),
                waits=[("VV", ev_hq), ("MM", max(mmq2, prev_mmx))], tick="VV")
            psb_free = ("VV", v_xx)
            prev_vxx = v_xx
            mmx = 0
            for ec in range(2):
                mmx = P.op("tensor", lambda e, ec=ec: e.transpose(
                    ps_t[:, ec * 128:(ec + 1) * 128],
                    XXb[:, ec * 128:(ec + 1) * 128], self.ID16[:]),
                    waits=[("VV", v_xx), ("EV", max(xxt_ev, kv_ev))], tick="MM")
            prev_mmx = mmx
            xxt_ev = P.op("scalar", lambda e, mi=mi: e.activation(
                XXT[:, :, mi, :], ps_t[:].rearrange("p (c t) -> p c t", t=128),
                AF.Copy), waits=[("MM", mmx)], tick="EV")

        prev_add = 0
        for dm in range(2):
            for ncb in range(2):
                mm = 0
                for ec in range(2):
                    mm = P.op("tensor", lambda e, dm=dm, ec=ec, ncb=ncb: e.matmul(
                        ps_o[:], W1N[:, ec, dm * 128:(dm + 1) * 128],
                        XXT[:, ec, ncb * 4:(ncb + 1) * 4, :].rearrange(
                            "p a b -> p (a b)"),
                        start=(ec == 0), stop=(ec == 1)),
                        waits=[("EV", xxt_ev), ("VV", max(upd_ev, prev_add))],
                        tick="MM")
                tsl = slice(base_t + ncb * 512, base_t + (ncb + 1) * 512)
                prev_add = P.op("vector", lambda e, dm=dm, tsl=tsl: e.scalar_tensor_tensor(
                    out=self.X[:, ch0 + dm, tsl], in0=ps_o[:], scalar=1.0,
                    in1=self.X[:, ch0 + dm, tsl], op0=ALU.mult, op1=ALU.add),
                    waits=[("MM", mm)], tick="VV")

    def emit_cbs(self, st, ei):
        P = self.P
        CBS = P.sb_in(st, "cbsx", [128, S], F32)
        stp = P.scope()
        psb = [P.ps_in(stp, "pscb", [128, 512], F32) for _ in range(2)]
        prev_ev = [0, 0]
        ev = 0
        for tb in range(NTB):
            ps = psb[tb % 2]
            mtb = 0
            for k in range(4):
                tt = tb * 4 + k
                mtb = P.op("tensor", lambda e, ps=ps, tt=tt, k=k: e.matmul(
                    ps[:, k * 128:(k + 1) * 128],
                    self.CW_TOK[:, tt, ei:ei + 1].broadcast_to((128, 128)),
                    self.ID32[:], start=True, stop=True),
                    waits=[("VV", self.cw_ready), ("EV", prev_ev[tb % 2])], tick="MM")
            ev = P.op("scalar", lambda e, ps=ps, tb=tb: e.activation(
                CBS[:, tb * TB:(tb + 1) * TB], ps[:], AF.Copy),
                waits=[("MM", mtb)], tick="EV")
            prev_ev[tb % 2] = ev
        self.P.flush("cbs")
        stp.close()
        return CBS, ev

    def stage_proj(self, src, dst, wkey, bname, act, W, pspair, psfree,
                   w_free=(), src_dep=()):
        P = self.P
        d = self.dram
        wdep = P.dma("gpsimd", W[:],
                     d[wkey].ap().rearrange("(c p) o -> p c o", p=128),
                     waits=list(w_free))
        last = 0
        last_mm = 0
        for tb in range(NTB):
            tsl = slice(tb * TB, (tb + 1) * TB)
            for oc in range(NCH):
                b = (tb * NCH + oc) % 2
                ps = pspair[b]
                mt = 0
                for kc in range(NCH):
                    mt = P.op("tensor", lambda e, ps=ps, kc=kc, oc=oc, tsl=tsl: e.matmul(
                        ps[:], W[:, kc, oc * 128:(oc + 1) * 128], src[:, kc, tsl],
                        start=(kc == 0), stop=(kc == NCH - 1)),
                        waits=[wdep, ("EV", psfree[b])] + list(src_dep),
                        tick="MM")
                ev = P.op("scalar", lambda e, ps=ps, oc=oc, tsl=tsl: e.activation(
                    dst[:, oc, tsl], ps[:], act, bias=self.bias_col(bname, oc)),
                    waits=[("MM", mt)], tick="EV")
                psfree[b] = ev
                last = ev
                last_mm = mt
        return last, last_mm

    def alloc_2mat(self, st):
        P = self.P
        return dict(
            H=[P.sb_in(st, "h2m", [128, 8, 512], BF16) for _ in range(2)],
            TF=P.sb_in(st, "tf2m", [128, 512], F32),
            W1b=[P.sb_in(st, "w1g", [128, NCH, 1024], BF16) for _ in range(2)],
            W2b=[P.sb_in(st, "w2g", [128, 8, 1024], BF16) for _ in range(2)],
            CBS=[P.sb_in(st, "cbsx", [128, S], F32) for _ in range(2)],
            ps1=[P.ps_in(st, "ps1", [128, 512], F32) for _ in range(2)],
            ps2=[P.ps_in(st, "ps2", [128, 512], F32) for _ in range(2)],
            pcb=[P.ps_in(st, "pscb2", [128, 512], F32) for _ in range(2)],
            ROW=P.sb_in(st, "b2row2", [1, 1024], BF16),
            row_free=0,
            w_free=[(None, None), (None, None)],
            **(lambda t: dict(CBSB=[t, t]))(P.sb_in(st, "cbsb", [1, S], BF16)),
            ps1_free=[0, 0], ps2_free=[0, 0],
            h_free=[0, 0], tf_free=0,
            cbs_free=[0, 0],
            pcb_free=[0, 0],
            gidx=0,
        )

    def emit_cbs2(self, B, ei, epar):
        P = self.P
        CBS = B["CBS"][epar]
        ev = 0
        for tb in range(NTB):
            ps = B["pcb"][tb % 2]
            mtb = 0
            for k in range(4):
                tt = tb * 4 + k
                mtb = P.op("tensor", lambda e, ps=ps, tt=tt, k=k, ei=ei: e.matmul(
                    ps[:, k * 128:(k + 1) * 128],
                    self.CW_TOK[:, tt, ei:ei + 1].broadcast_to((128, 128)),
                    self.ID32[:], start=True, stop=True),
                    waits=[("VV", max(self.cw_ready, B.get("gvv", 0))),
                           ("EV", B["pcb_free"][tb % 2]),
                           ("MM", B["cbs_free"][epar])], tick="MM")
            ev = P.op("scalar", lambda e, ps=ps, CBS=CBS, tb=tb: e.activation(
                CBS[:, tb * TB:(tb + 1) * TB], ps[:], AF.Copy),
                waits=[("MM", mtb)], tick="EV")
            B["pcb_free"][tb % 2] = ev
        if "CBSB" in B:
            ev = P.op("scalar", lambda e, CBS=CBS, epar=epar: e.activation(
                B["CBSB"][epar][0:1, :], CBS[0:1, :], AF.Copy),
                waits=[("EV", ev), ("MM", max(B["cbs_free"])),
                       ("VV", B.get("gvv", 0))], tick="EV")
        return CBS, ev

    def expert_2mat(self, seq, tag, B, w1src, w2src, I, act, b1pfx, b2row_key,
                    ei, strow):
        P = self.P
        epar = ei % 2
        CBS, cbs_ev = self.emit_cbs2(B, ei, epar)
        key, bsl = b2row_key
        b2src = self.dram[key].ap() if bsl is None else self.dram[key][bsl[1], :]
        b2row = B["ROW"]
        b2dep = P.dma("gpsimd", b2row[0:1, :], b2src,
                      waits=([("MM", B["row_free"])] if B["row_free"] else [])
                      + B.get("gates", []))
        CBSB = B["CBSB"][epar]
        NG = I // 1024
        wsrc = []
        for g in range(NG):
            wsrc.append((
                w1src[:, g * 1024:(g + 1) * 1024].rearrange("(c p) o -> p c o", p=128),
                w2src[g * 1024:(g + 1) * 1024, :].rearrange("(c p) o -> p c o", p=128),
            ))
        w_dep = [None, None]

        def load_group(g):
            gp = B["gidx"] % 2
            B["gidx"] += 1
            fr = B["w_free"][gp]
            gts = B.get("gates", [])
            w_dep[gp] = (
                P.dma("gpsimd", B["W1b"][gp][:], wsrc[g][0],
                      waits=([fr[0]] if fr[0] else []) + gts),
                P.dma("gpsimd", B["W2b"][gp][:], wsrc[g][1],
                      waits=([fr[1]] if fr[1] else []) + gts),
            )
            return gp
        gps = [load_group(0)]
        if NG > 1:
            gps.append(load_group(1))
        last_cbs_mm = B["cbs_free"][epar]
        for g in range(NG):
            gp = gps[g]
            s1_last = 0
            s2_last = 0
            for tb in range(NTB):
                tsl = slice(tb * TB, (tb + 1) * TB)
                hp = tb % 2
                pp_hs = [0] * 8
                for ic in range(8):
                    b = ic % 2
                    mt = 0
                    for kc in range(NCH):
                        mt = P.op("tensor", lambda e, b=b, kc=kc, ic=ic, gp=gp, tsl=tsl: e.matmul(
                            B["ps1"][b][:], B["W1b"][gp][:, kc, ic * 128:(ic + 1) * 128],
                            self.XN[:, kc, tsl], start=(kc == 0), stop=(kc == NCH - 1)),
                            waits=[w_dep[gp][0], ("EV", B["ps1_free"][b]),
                                   ("EV", self.xn_ev[tb])],
                            tick="MM")
                    s1_last = mt
                    ev = P.op("scalar", lambda e, b=b, g=g, ic=ic: e.activation(
                        B["TF"][:], B["ps1"][b][:], act,
                        bias=self.bias_col(b1pfx, g * 8 + ic)),
                        waits=[("MM", mt), ("VV", B["tf_free"])], tick="EV")
                    B["ps1_free"][b] = ev
                    pp_hs[ic] = P.op("vector", lambda e, hp=hp, ic=ic, tsl=tsl, CBS=CBS: e.tensor_tensor(
                        B["H"][hp][:, ic, :], B["TF"][:], CBS[:, tsl], ALU.mult),
                        waits=[("EV", max(ev, cbs_ev)), ("MM", B["h_free"][hp])],
                        tick="VV")
                    B["tf_free"] = pp_hs[ic]
                for oc in range(NCH):
                    b = oc % 2
                    mt = 0
                    for ic in range(8):
                        mt = P.op("tensor", lambda e, b=b, ic=ic, oc=oc, gp=gp, hp=hp, g=g: e.matmul(
                            B["ps2"][b][:], B["W2b"][gp][:, ic, oc * 128:(oc + 1) * 128],
                            B["H"][hp][:, ic, :], start=(ic == 0),
                            stop=(ic == 7 and g > 0)),
                            waits=[w_dep[gp][1],
                                   ("VV", max(B["ps2_free"][b], pp_hs[ic]))],
                            tick="MM")
                    if g == 0:
                        mt = P.op("tensor", lambda e, b=b, oc=oc, tsl=tsl, CBSB=CBSB: e.matmul(
                            B["ps2"][b][:], b2row[0:1, oc * 128:(oc + 1) * 128],
                            CBSB[0:1, tsl], start=False, stop=True),
                            waits=[b2dep, ("EV", cbs_ev)], tick="MM")
                        last_cbs_mm = mt
                    s2_last = mt
                    if oc == NCH - 1:
                        B["h_free"][hp] = mt
                    B["ps2_free"][b] = P.op("vector", lambda e, b=b, oc=oc, tsl=tsl: e.scalar_tensor_tensor(
                        out=self.X[:, oc, tsl], in0=B["ps2"][b][:], scalar=1.0,
                        in1=self.X[:, oc, tsl], op0=ALU.mult, op1=ALU.add),
                        waits=[("MM", mt)], tick="VV")
            B["w_free"][gp] = (("MM", s1_last), ("MM", s2_last))
            if g + 2 < NG:
                gps.append(load_group(g + 2))
        B["cbs_free"][epar] = max(last_cbs_mm, s2_last)
        B["row_free"] = last_cbs_mm

    def emit_e0(self, seq):
        P = self.P
        d = self.dram
        gmm, gev, gvv = P.cnt["MM"], P.cnt["EV"], P.cnt["VV"]
        gts = [("MM", gmm), ("EV", gev), ("VV", gvv)]
        st = P.scope()
        Bc = dict(CBS=[P.sb_in(st, "cbsx", [128, S], F32)],
                  pcb=[P.ps_in(st, "pscb", [128, 512], F32) for _ in range(2)],
                  pcb_free=[gev, gev], cbs_free=[gmm, gmm], gvv=gvv)
        CBS, cbs_ev = self.emit_cbs2(Bc, 0, 0)
        H = [P.sb_in(st, "h0", [128, 4, 512], BF16) for _ in range(2)]
        TF = P.sb_in(st, "tf0", [128, 512], F32)
        W1b = [P.sb_in(st, "w1g0", [128, NCH, 512], BF16) for _ in range(2)]
        W3b = [P.sb_in(st, "w3g0", [128, NCH, 512], BF16) for _ in range(2)]
        W2b = [P.sb_in(st, "w2g0", [128, 4, 1024], BF16) for _ in range(2)]
        ps1 = [P.ps_in(st, "ps10", [128, 512], F32) for _ in range(2)]
        ps3 = [P.ps_in(st, "ps30", [128, 512], F32) for _ in range(2)]
        ps2 = [P.ps_in(st, "ps20", [128, 512], F32) for _ in range(2)]
        NG = 8
        IG = 512
        w_dep = [None, None]

        def load_group(g, waits):
            waits = list(waits) + gts
            gp = g % 2
            gsl = slice(g * IG, (g + 1) * IG)
            w_dep[gp] = (
                P.dma("gpsimd", W1b[gp][:],
                      d["L0_ffn_sw_w1"][:, gsl].rearrange("(c p) o -> p c o", p=128),
                      waits=waits),
                P.dma("gpsimd", W3b[gp][:],
                      d["L0_ffn_sw_w3"][:, gsl].rearrange("(c p) o -> p c o", p=128),
                      waits=waits),
                P.dma("gpsimd", W2b[gp][:],
                      d["L0_ffn_sw_w2"][gsl, :].rearrange("(c p) o -> p c o", p=128),
                      waits=waits),
            )
        load_group(0, [])
        load_group(1, [])
        ps1_free = [0, 0]
        ps3_free = [0, 0]
        ps2_free = [0, 0]
        h_free = [0, 0]
        tf_free = 0
        tf_free_pp = 0
        for g in range(NG):
            gp = g % 2
            s_last = 0
            for tb in range(NTB):
                tsl = slice(tb * TB, (tb + 1) * TB)
                hp = tb % 2
                pp_hs = [0] * 4
                for ic in range(4):
                    b = ic % 2
                    mt1 = 0
                    mt3 = 0
                    for kc in range(NCH):
                        mt1 = P.op("tensor", lambda e, b=b, kc=kc, ic=ic, gp=gp, tsl=tsl: e.matmul(
                            ps1[b][:], W1b[gp][:, kc, ic * 128:(ic + 1) * 128],
                            self.XN[:, kc, tsl], start=(kc == 0), stop=(kc == NCH - 1)),
                            waits=[w_dep[gp][0], ("EV", ps1_free[b])],
                            tick="MM")
                    for kc in range(NCH):
                        mt3 = P.op("tensor", lambda e, b=b, kc=kc, ic=ic, gp=gp, tsl=tsl: e.matmul(
                            ps3[b][:], W3b[gp][:, kc, ic * 128:(ic + 1) * 128],
                            self.XN[:, kc, tsl], start=(kc == 0), stop=(kc == NCH - 1)),
                            waits=[w_dep[gp][1], ("VV", ps3_free[b])],
                            tick="MM")
                    ev = P.op("scalar", lambda e, b=b: e.activation(
                        TF[:], ps1[b][:], AF.Silu),
                        waits=[("MM", mt1), ("VV", max(tf_free, tf_free_pp))],
                        tick="EV")
                    ps1_free[b] = ev
                    vv = P.op("vector", lambda e, b=b: e.tensor_tensor(
                        TF[:], TF[:], ps3[b][:], ALU.mult),
                        waits=[("EV", ev), ("MM", mt3)], tick="VV")
                    ps3_free[b] = vv
                    tf_free = vv
                    pp_hs[ic] = P.op("vector", lambda e, hp=hp, ic=ic, tsl=tsl: e.tensor_tensor(
                        H[hp][:, ic, :], TF[:], CBS[:, tsl], ALU.mult),
                        waits=[("VV", vv), ("EV", cbs_ev), ("MM", h_free[hp])],
                        tick="VV")
                    tf_free_pp = pp_hs[ic]
                for oc in range(NCH):
                    b = oc % 2
                    mt = 0
                    for ic in range(4):
                        mt = P.op("tensor", lambda e, b=b, ic=ic, oc=oc, gp=gp, hp=hp: e.matmul(
                            ps2[b][:], W2b[gp][:, ic, oc * 128:(oc + 1) * 128],
                            H[hp][:, ic, :], start=(ic == 0), stop=(ic == 3)),
                            waits=[w_dep[gp][2], ("VV", max(ps2_free[b], pp_hs[ic]))],
                            tick="MM")
                    s_last = mt
                    if oc == NCH - 1:
                        h_free[hp] = mt
                    ps2_free[b] = P.op("vector", lambda e, b=b, oc=oc, tsl=tsl: e.scalar_tensor_tensor(
                        out=self.X[:, oc, tsl], in0=ps2[b][:], scalar=1.0,
                        in1=self.X[:, oc, tsl], op0=ALU.mult, op1=ALU.add),
                        waits=[("MM", mt)], tick="VV")
            if g + 2 < NG:
                load_group(g + 2, [("MM", s_last)])
        st.close()
        P.flush(f"l0ffn_{seq}")

    def emit_e1(self, seq):
        P = self.P
        d = self.dram
        gmm, gev, gvv = P.cnt["MM"], P.cnt["EV"], P.cnt["VV"]
        gts = [("MM", gmm), ("EV", gev), ("VV", gvv)]
        st = P.scope()
        Bc = dict(CBS=[P.sb_in(st, "cbsx", [128, S], F32)],
                  pcb=[P.ps_in(st, "pscb", [128, 512], F32) for _ in range(2)],
                  pcb_free=[gev, gev], cbs_free=[gmm, gmm], gvv=gvv)
        Bc["CBSB"] = [P.sb_in(st, "cbsb", [1, S], BF16)]
        Bc["CBS"] = Bc["CBS"] * 1
        CBS, cbs_ev = self.emit_cbs2(Bc, 1, 0)
        CBSB = Bc["CBSB"][0]
        c2row = P.sb_in(st, "c2rb", [1, 1024], BF16)
        c2dep = P.dma("gpsimd", c2row[0:1, :], self.dram["L0_ffn_me_c2b"].ap(),
                      waits=gts)
        BUF1 = P.sb_in(st, "e1b1", [128, NCH, S], BF16)
        SPW = P.sb_in(st, "spw", [128, NCH, D], BF16)
        sp_ps = [P.ps_in(st, "psp", [128, 512], F32) for _ in range(2)]
        spfree = [gev, gev]
        stb = P.scope()
        BUF2 = P.sb_in(stb, "e1b2", [128, NCH, S], BF16)
        d1, m1 = self.stage_proj(self.XN, BUF1, "L0_ffn_me_eq_w", "me_eq_b",
                                 AF.Identity, SPW, sp_ps, spfree,
                                 w_free=gts, src_dep=[("VV", gvv)])
        d2, m2 = self.stage_proj(BUF1, BUF2, "L0_ffn_me_wv", "me_bv",
                                 AF.Identity, SPW, sp_ps, spfree,
                                 w_free=[("MM", m1)], src_dep=[("EV", d1)])
        d3, m3 = self.stage_proj(BUF2, BUF1, "L0_ffn_me_wo", "me_bo",
                                 AF.Identity, SPW, sp_ps, spfree,
                                 w_free=[("MM", m2)], src_dep=[("EV", d2)])
        stb.close()
        U = P.sb_in(st, "e1u", [128, 8, 512], BF16)
        TF = P.sb_in(st, "e1tf", [128, 512], F32)
        C1 = P.sb_in(st, "c1h", [128, NCH, 1024], BF16)
        C2 = P.sb_in(st, "c2h", [128, 8, 1024], BF16)
        ps1 = [P.ps_in(st, "e1p1", [128, 512], F32) for _ in range(2)]
        ps2 = [P.ps_in(st, "e1p2", [128, 512], F32) for _ in range(2)]
        last2 = 0
        last1 = 0
        for g in range(2):
            gsl = slice(g * 1024, (g + 1) * 1024)
            wd1 = P.dma("gpsimd", C1[:],
                        d["L0_ffn_me_c1w"][:, gsl].rearrange("(c p) o -> p c o", p=128),
                        waits=[("EV", d3)] if g == 0 else [("MM", last1)])
            wd2 = P.dma("gpsimd", C2[:],
                        d["L0_ffn_me_c2w"][gsl, :].rearrange("(c p) o -> p c o", p=128),
                        waits=[("EV", d3)] if g == 0 else [("MM", last2)])
            ps1_free = [0, 0]
            ps2_free = [0, 0]
            u_free = 0
            tf_free = 0
            for tb in range(NTB):
                tsl = slice(tb * TB, (tb + 1) * TB)
                pp_us = [0] * 8
                for ic in range(8):
                    b = ic % 2
                    mt = 0
                    for kc in range(NCH):
                        mt = P.op("tensor", lambda e, b=b, kc=kc, ic=ic, tsl=tsl: e.matmul(
                            ps1[b][:], C1[:, kc, ic * 128:(ic + 1) * 128],
                            BUF1[:, kc, tsl], start=(kc == 0), stop=(kc == NCH - 1)),
                            waits=[wd1, ("EV", max(ps1_free[b], d3)),
                                   ("MM", u_free)], tick="MM")
                    last1 = mt
                    ev = P.op("scalar", lambda e, b=b, g=g, ic=ic: e.activation(
                        TF[:], ps1[b][:], AF.Gelu,
                        bias=self.bias_col("me_c1b", g * 8 + ic)),
                        waits=[("MM", mt), ("VV", tf_free)], tick="EV")
                    ps1_free[b] = ev
                    pp_us[ic] = P.op("vector", lambda e, ic=ic, tsl=tsl: e.tensor_tensor(
                        U[:, ic, :], TF[:], CBS[:, tsl], ALU.mult),
                        waits=[("EV", max(ev, cbs_ev))], tick="VV")
                    tf_free = pp_us[ic]
                for oc in range(NCH):
                    b = oc % 2
                    mt = 0
                    for ic in range(8):
                        mt = P.op("tensor", lambda e, b=b, ic=ic, oc=oc, g=g: e.matmul(
                            ps2[b][:], C2[:, ic, oc * 128:(oc + 1) * 128],
                            U[:, ic, :], start=(ic == 0), stop=(ic == 7 and g == 1)),
                            waits=[wd2, ("VV", max(ps2_free[b], pp_us[ic]))], tick="MM")
                    if g == 0:
                        mt = P.op("tensor", lambda e, b=b, oc=oc, tsl=tsl: e.matmul(
                            ps2[b][:], c2row[0:1, oc * 128:(oc + 1) * 128],
                            CBSB[0:1, tsl], start=False, stop=True),
                            waits=[c2dep, ("EV", cbs_ev)], tick="MM")
                    last2 = mt
                    if oc == NCH - 1:
                        u_free = mt
                    ps2_free[b] = P.op("vector", lambda e, b=b, oc=oc, tsl=tsl: e.scalar_tensor_tensor(
                        out=self.X[:, oc, tsl], in0=ps2[b][:], scalar=1.0,
                        in1=self.X[:, oc, tsl], op0=ALU.mult, op1=ALU.add),
                        waits=[("MM", mt)], tick="VV")
        st.close()

    def emit_e2(self, seq):
        P = self.P
        d = self.dram
        st = P.scope()
        stp = P.scope()
        Bc = dict(CBS=[P.sb_in(st, "cbsx", [128, S], F32)],
                  CBSB=[P.sb_in(st, "cbsb", [1, S], BF16)],
                  pcb=[P.ps_in(stp, "pscb", [128, 512], F32) for _ in range(2)],
                  pcb_free=[0, 0], cbs_free=[0, 0])
        CBS, cbs_ev = self.emit_cbs2(Bc, 2, 0)
        stp.close()
        CBSB = Bc["CBSB"][0]
        genrow = P.sb_in(st, "genrb", [1, 1024], BF16)
        gendep = P.dma("gpsimd", genrow[0:1, :], d["L0_ffn_ce_gen_b"].ap())
        borow = P.sb_in(st, "borb", [1, 1024], BF16)
        bodep = P.dma("gpsimd", borow[0:1, :], d["L0_ffn_ce_bo"].ap())
        HS = S // 2
        BUF1 = P.sb_in(st, "e2b1", [128, NCH, HS], BF16)
        BUF2 = P.sb_in(st, "e2b2", [128, NCH, HS], BF16)
        WX = P.sb_in(st, "e2wx", [128, 16, 1024], BF16)
        Wp = WX[:, 0:8, :]
        F2v = WX[:, 8:16, :]
        Y32 = P.sb_in(st, "e2y", [128, NCH, 256], F32)
        T32P = P.sb_in(st, "e2t", [128, 2, 256], F32)
        U = P.sb_in(st, "e2u", [128, 8, 256], BF16)
        lp = self.alloc_ln(st, 256)
        lp["sq_dve"] = True
        psA = [P.ps_in(st, "e2pa", [128, 512], F32) for _ in range(2)]
        psB = [P.ps_in(st, "e2pb", [128, 512], F32) for _ in range(2)]
        lp["stats_ev_prev"] = cbs_ev
        pAf = [("EV", cbs_ev), ("EV", cbs_ev)]
        pBf = [("EV", cbs_ev), ("EV", cbs_ev)]
        ctx = dict(wx_a=0, wx_b=0, u_free=0, y32_mm=0, y32_ev=0)

        def proj_half(src, srcoff, dst, wkey, bname, src_dep=()):
            wdep = P.dma("gpsimd", Wp,
                         d[wkey].ap().rearrange("(c p) o -> p c o", p=128),
                         waits=[("MM", ctx["wx_a"])] if ctx["wx_a"] else [])
            lastv = 0
            for tb in range(HS // 512):
                ssl = slice(srcoff + tb * 512, srcoff + (tb + 1) * 512)
                dsl = slice(tb * 512, (tb + 1) * 512)
                for oc in range(NCH):
                    b = (tb * NCH + oc) % 2
                    ps = psA[b]
                    mt = 0
                    for kc in range(NCH):
                        mt = P.op("tensor", lambda e, ps=ps, kc=kc, oc=oc, ssl=ssl: e.matmul(
                            ps[:], Wp[:, kc, oc * 128:(oc + 1) * 128],
                            src[:, kc, ssl],
                            start=(kc == 0), stop=(kc == NCH - 1)),
                            waits=[wdep, pAf[b]] + list(src_dep), tick="MM")
                    ev = P.op("scalar", lambda e, ps=ps, oc=oc, dsl=dsl: e.activation(
                        dst[:, oc, dsl], ps[:], AF.Identity,
                        bias=self.bias_col(bname, oc)),
                        waits=[("MM", mt)], tick="EV")
                    pAf[b] = ("EV", ev)
                    lastv = ev
                    ctx["wx_a"] = mt
            return lastv

        for hseq in range(2):
            off = hseq * HS
            s1 = proj_half(self.XN, off, BUF1, "L0_ffn_ce_syn_w", "ce_syn_b")
            s2 = proj_half(BUF1, 0, BUF2, "L0_ffn_ce_wv", "ce_bv",
                           src_dep=[("EV", s1)])
            wdep = P.dma("gpsimd", Wp,
                         d["L0_ffn_ce_wo"].ap().rearrange("(c p) o -> p c o", p=128),
                         waits=[("MM", ctx["wx_a"])])
            h2_deps = [("EV", 0)] * 4
            prev_src_mm = ctx["y32_mm"]
            prev_src_ev = ctx["y32_ev"]
            prev_src_vv = ctx.get("y32_vv", 0)
            t32f = [("VV", 0), ("VV", 0)]
            for tb in range(4):
                tsl = slice(tb * 256, (tb + 1) * 256)
                yvv = 0
                for oc in range(NCH):
                    b = oc % 2
                    ps = psB[b][:, 0:256]
                    mt = 0
                    for kc in range(NCH):
                        mt = P.op("tensor", lambda e, ps=ps, kc=kc, oc=oc, tsl=tsl: e.matmul(
                            ps[:], Wp[:, kc, oc * 128:(oc + 1) * 128],
                            BUF2[:, kc, tsl],
                            start=(kc == 0), stop=False),
                            waits=[wdep, pBf[b], ("EV", s2)],
                            tick="MM")
                    mt = P.op("tensor", lambda e, ps=ps, oc=oc: e.matmul(
                        ps[:], borow[0:1, oc * 128:(oc + 1) * 128],
                        self.ONESR16[0:1, 0:256], start=False, stop=True),
                        waits=[bodep], tick="MM")
                    ctx["wx_a"] = mt
                    yvv = P.op("vector", lambda e, ps=ps, oc=oc, tsl=tsl: e.tensor_tensor(
                        Y32[:, oc, :], BUF1[:, oc, tsl], ps[:], ALU.add),
                        waits=[("MM", max(mt, prev_src_mm)),
                               ("EV", prev_src_ev), ("VV", prev_src_vv)],
                        tick="VV")
                    pBf[b] = ("VV", yvv)
                r = self.emit_ln_tb(lp, lambda c: Y32[:, c, :], True, [("VV", yvv)])
                prev_src_mm = r["src_mm"]
                prev_src_ev = r["src_ev"]
                prev_src_vv = r["src_vv"]
                apply_vv = 0
                for oc in range(NCH):
                    b = oc % 2
                    T32 = T32P[:, b, :]
                    vs = P.op("vector", lambda e, oc=oc, T32=T32: e.tensor_tensor(
                        T32[:], Y32[:, oc, :], lp["ps_mu"][:], ALU.subtract),
                        waits=[("MM", r["bmu"]), t32f[b]], tick="VV")
                    vm = P.op("vector", lambda e, T32=T32: e.tensor_tensor(
                        T32[:], T32[:], lp["ps_rs"][:], ALU.mult),
                        waits=[("MM", r["brs"]), ("VV", vs)], tick="VV")
                    apply_vv = vm
                    h2_ev = P.op("scalar", lambda e, oc=oc, tsl=tsl, T32=T32: e.activation(
                        BUF2[:, oc, tsl], T32[:], AF.Identity,
                        scale=self.bias_col("ce_n1g", oc),
                        bias=self.bias_col("ce_n1b", oc)),
                        waits=[("VV", vm)], tick="EV")
                    h2_deps[tb] = ("EV", h2_ev)
                    t32f[b] = ("EV", h2_ev)
                lp["apply_prev"] = apply_vv
            wo_yvv = yvv
            buf1_ev = 0
            out_pp = 0
            out_pp_tb = [0] * 4
            for g in range(2):
                gsl = slice(g * 1024, (g + 1) * 1024)
                wd1 = P.dma("gpsimd", Wp,
                            d["L0_ffn_ce_f1w"][:, gsl].rearrange("(c p) o -> p c o", p=128),
                            waits=[("MM", ctx["wx_a"])])
                wd2 = P.dma("gpsimd", F2v,
                            d["L0_ffn_ce_f2w"][gsl, :].rearrange("(c p) o -> p c o", p=128),
                            waits=[("MM", ctx["wx_b"])] if ctx["wx_b"] else [])
                last_v2 = 0
                for tb in range(4):
                    tsl = slice(tb * 256, (tb + 1) * 256)
                    ev_us = [0] * 8
                    mt1 = 0
                    for ic in range(8):
                        b = ic % 2
                        mt = 0
                        for kc in range(NCH):
                            mt = P.op("tensor", lambda e, b=b, kc=kc, ic=ic, tsl=tsl: e.matmul(
                                psA[b][:, 0:256], Wp[:, kc, ic * 128:(ic + 1) * 128],
                                BUF2[:, kc, tsl], start=(kc == 0), stop=(kc == NCH - 1)),
                                waits=[wd1, pAf[b], ("MM", ctx["u_free"]),
                                       h2_deps[tb]], tick="MM")
                        mt1 = mt
                        ev_us[ic] = P.op("scalar", lambda e, b=b, g=g, ic=ic: e.activation(
                            U[:, ic, :], psA[b][:, 0:256], AF.Relu,
                            bias=self.bias_col("ce_f1b", g * 8 + ic)),
                            waits=[("MM", mt)], tick="EV")
                        pAf[b] = ("EV", ev_us[ic])
                    vvs = []
                    mt2 = 0
                    for oc in range(NCH):
                        b = oc % 2
                        mt = 0
                        for ic in range(8):
                            mt = P.op("tensor", lambda e, b=b, ic=ic, oc=oc: e.matmul(
                                psB[b][:, 0:256], F2v[:, ic, oc * 128:(oc + 1) * 128],
                                U[:, ic, :], start=(ic == 0), stop=(ic == 7)),
                                waits=[wd2, ("EV", ev_us[ic]), pBf[b],
                                       ("VV", out_pp)], tick="MM")
                        mt2 = mt
                        if oc == NCH - 1:
                            ctx["u_free"] = mt
                        if g == 0:
                            buf1_ev = P.op("scalar", lambda e, b=b, oc=oc, tsl=tsl: e.activation(
                                BUF1[:, oc, tsl], psB[b][:, 0:256], AF.Identity,
                                bias=self.bias_col("ce_f2b", oc)),
                                waits=[("MM", mt), ("VV", wo_yvv)], tick="EV")
                            pBf[b] = ("EV", buf1_ev)
                        else:
                            T32 = T32P[:, b, :]
                            v1 = P.op("vector", lambda e, b=b, oc=oc, tsl=tsl, T32=T32: e.scalar_tensor_tensor(
                                out=T32[:], in0=psB[b][:, 0:256], scalar=1.0,
                                in1=BUF1[:, oc, tsl], op0=ALU.mult, op1=ALU.add),
                                waits=[("MM", max(mt, prev_src_mm)),
                                       ("EV", max(prev_src_ev, buf1_ev)),
                                       ("VV", max(out_pp, last_v2, prev_src_vv)),
                                       t32f[b]],
                                tick="VV")
                            v2 = P.op("vector", lambda e, oc=oc, tsl=tsl, T32=T32: e.tensor_tensor(
                                Y32[:, oc, :], BUF2[:, oc, tsl], T32[:], ALU.add),
                                waits=[("VV", v1)], tick="VV")
                            t32f[b] = ("VV", v2)
                            last_v2 = v2
                            pBf[b] = ("VV", v2)
                            vvs.append(v2)
                    if g == 1:
                        r = self.emit_ln_tb(lp, lambda c: Y32[:, c, :], True,
                                            [("VV", vvs[-1] if vvs else 0)])
                        prev_src_mm = r["src_mm"]
                        prev_src_ev = r["src_ev"]
                        prev_src_vv = r["src_vv"]
                        apply_vv = 0
                        for oc in range(NCH):
                            b2 = oc % 2
                            T32 = T32P[:, b2, :]
                            vs = P.op("vector", lambda e, oc=oc, T32=T32: e.tensor_tensor(
                                T32[:], Y32[:, oc, :], lp["ps_mu"][:], ALU.subtract),
                                waits=[("MM", r["bmu"]), ("VV", out_pp), t32f[b2]],
                                tick="VV")
                            vm = P.op("vector", lambda e, T32=T32: e.tensor_tensor(
                                T32[:], T32[:], lp["ps_rs"][:], ALU.mult),
                                waits=[("MM", r["brs"]), ("VV", vs)], tick="VV")
                            apply_vv = vm
                            ev3 = P.op("scalar", lambda e, oc=oc, T32=T32: e.activation(
                                T32[:], T32[:], AF.Identity,
                                scale=self.bias_col("ce_n2g", oc),
                                bias=self.bias_col("ce_n2b", oc)),
                                waits=[("VV", vm)], tick="EV")
                            out_pp = P.op("vector", lambda e, oc=oc, tsl=tsl, off=off, T32=T32: e.tensor_tensor(
                                BUF1[:, oc, tsl], T32[:],
                                CBS[:, off + tsl.start:off + tsl.stop], ALU.mult),
                                waits=[("EV", max(ev3, cbs_ev))], tick="VV")
                            t32f[b2] = ("VV", out_pp)
                            out_pp_tb[tb] = out_pp
                        lp["apply_prev"] = apply_vv
                ctx["wx_a"] = mt1
                ctx["wx_b"] = mt2
            ctx["y32_mm"] = prev_src_mm
            ctx["y32_ev"] = prev_src_ev
            ctx["y32_vv"] = prev_src_vv
            wdep = P.dma("gpsimd", Wp,
                         d["L0_ffn_ce_gen_w"].ap().rearrange("(c p) o -> p c o", p=128),
                         waits=[("MM", ctx["wx_a"])])
            for tb in range(HS // 512):
                lsl = slice(tb * 512, (tb + 1) * 512)
                xsl = slice(off + tb * 512, off + (tb + 1) * 512)
                for oc in range(NCH):
                    b = (tb * NCH + oc) % 2
                    ps = psA[b]
                    mt = 0
                    for kc in range(NCH):
                        mt = P.op("tensor", lambda e, ps=ps, kc=kc, oc=oc, lsl=lsl: e.matmul(
                            ps[:], Wp[:, kc, oc * 128:(oc + 1) * 128], BUF1[:, kc, lsl],
                            start=(kc == 0), stop=False),
                            waits=[wdep, pAf[b],
                                   ("VV", max(out_pp_tb[2 * tb], out_pp_tb[2 * tb + 1]))],
                            tick="MM")
                    mt = P.op("tensor", lambda e, ps=ps, oc=oc, xsl=xsl: e.matmul(
                        ps[:], genrow[0:1, oc * 128:(oc + 1) * 128], CBSB[0:1, xsl],
                        start=False, stop=True),
                        waits=[gendep, ("EV", cbs_ev)], tick="MM")
                    ctx["wx_a"] = mt
                    stt = P.op("vector", lambda e, ps=ps, oc=oc, xsl=xsl: e.scalar_tensor_tensor(
                        out=self.X[:, oc, xsl], in0=ps[:], scalar=1.0,
                        in1=self.X[:, oc, xsl], op0=ALU.mult, op1=ALU.add),
                        waits=[("MM", mt)], tick="VV")
                    pAf[b] = ("VV", stt)
        st.close()

    def ffn_phase(self, seq, layer):
        P = self.P
        d = self.dram
        if layer == 0:
            sel = self.stage.split(":")[1] if ":" in self.stage else "e0e1e2e3"
            if "e2" in sel:
                self.emit_e2(seq)
            if "e1" in sel:
                self.emit_e1(seq)
            if "e3" in sel:
                st = P.scope()
                B = self.alloc_2mat(st)
                B["gates"] = [("MM", P.cnt["MM"]), ("EV", P.cnt["EV"]),
                              ("VV", P.cnt["VV"])]
                B["pcb_free"] = [P.cnt["EV"]] * 2
                B["cbs_free"] = [P.cnt["MM"]] * 2
                B["gvv"] = P.cnt["VV"]
                self.expert_2mat(seq, "e3", B, d["L0_ffn_ml_w1"].ap(),
                                 d["L0_ffn_ml_w2"].ap(), 4096, AF.Gelu, "ml_b1",
                                 ("L0_ffn_ml_b2", None), 3, st)
                st.close()
            if "e0" in sel:
                self.emit_e0(seq)
        else:
            st = P.scope()
            B = self.alloc_2mat(st)
            B["gates"] = [("MM", P.cnt["MM"]), ("EV", P.cnt["EV"]),
                          ("VV", P.cnt["VV"]), ("PP", P.cnt["PP"])]
            B["pcb_free"] = [P.cnt["EV"]] * 2
            B["cbs_free"] = [P.cnt["MM"]] * 2
            B["gvv"] = P.cnt["VV"]
            for e in range(8):
                self.expert_2mat(seq, f"x{e}", B, d["L1_ffn_W1"][e],
                                 d["L1_ffn_W2"][e], 2048, AF.Relu, f"l1_b1_{e}",
                                 ("L1_ffn_b2", ("row", e)), e, st)
            P.flush(f"l1ffn_{seq}")
            st.close()


def kernel(**inputs):
    x = np.ascontiguousarray(inputs["L0_attn_x"])
    nc = build_nc("full")
    in_maps = []
    for core in range(8):
        m = {"x": x[2 * core:2 * core + 2]}
        for name in INPUT_SPECS:
            if name == "L0_attn_x":
                continue
            m[name] = np.ascontiguousarray(inputs[name])
        in_maps.append(m)
    res = run_bass_kernel_spmd(nc, in_maps, core_ids=list(range(8)))
    return np.concatenate([r["out"] for r in res.results], axis=0)


# version 24
# speedup vs baseline: 1.0376x; 1.0376x over previous
import contextlib
import numpy as np
import concourse.bass as bass
import concourse.mybir as mybir
from concourse.bass_utils import run_bass_kernel_spmd

F32 = mybir.dt.float32
BF16 = mybir.dt.bfloat16
AF = mybir.ActivationFunctionType
ALU = mybir.AluOpType
AXL = mybir.AxisListType

D = 1024
S = 2048
NCH = 8
NSEQ = 2
TB = 512
NTB = S // TB
NTT = S // 128
EPS = 1e-5

ENGINES = ("tensor", "scalar", "vector", "gpsimd", "sync")


def psP_w(R, i):
    d = R["psP_free"][i]
    return d if d is not None else ("MM", 0)





class Op:
    __slots__ = ("eng", "fn", "waits", "tick")

    def __init__(self, eng, fn, waits, tick):
        self.eng = eng
        self.fn = fn
        self.waits = waits
        self.tick = tick


class Prog:

    def __init__(self, nc):
        self.nc = nc
        self.ops = []
        self.sems = {}
        self.cnt = {}
        self.stack = contextlib.ExitStack()
        self.uid = 0
        for name in ("MM", "EV", "VV", "PP",
                     "WD0", "WD1", "WD2", "WD3",
                     "XD0", "XD1", "XD2", "XD3"):
            self.sems[name] = self.stack.enter_context(nc.semaphore(name))
            self.cnt[name] = 0
        self.rot = {"gpsimd": 0, "sync": 0}
        self.pool = {"gpsimd": ["WD0", "WD1", "WD2", "WD3"],
                     "sync": ["XD0", "XD1", "XD2", "XD3"]}

    def fresh(self, tag):
        self.uid += 1
        return f"{tag}_{self.uid}"

    def sb(self, tag, shape, dtype):
        return self.stack.enter_context(self.nc.sbuf_tensor(self.fresh(tag), shape, dtype))

    def scope(self):
        return contextlib.ExitStack()

    def sb_in(self, st, tag, shape, dtype):
        return st.enter_context(self.nc.sbuf_tensor(self.fresh(tag), shape, dtype))

    def ps_in(self, st, tag, shape, dtype=F32):
        return st.enter_context(self.nc.psum_tensor(self.fresh(tag), shape, dtype))

    def op(self, eng, fn, waits=(), tick=None):
        w = [t for t in waits if isinstance(t, tuple) and t[1] and t[1] > 0]
        t = None
        val = None
        if tick is not None:
            if isinstance(tick, str):
                tick = (tick, 1)
            s, dd = tick
            self.cnt[s] += dd
            val = self.cnt[s]
            t = (s, dd)
        self.ops.append(Op(eng, fn, w, t))
        return val

    def dma(self, queue_eng, out, in_, waits=()):
        sem = self.pool[queue_eng][self.rot[queue_eng] % 4]
        self.rot[queue_eng] += 1
        w = list(waits) + [(sem, self.cnt[sem])]

        def fn(eng, out=out, in_=in_):
            return eng.dma_start(out=out, in_=in_)
        v = self.op(queue_eng, fn, waits=w, tick=(sem, 16))
        return (sem, v)

    def qdeps(self, queue):
        return [(s, self.cnt[s]) for s in self.pool[queue]]

    def flush(self, name=None):
        ops = self.ops
        self.ops = []
        if not ops:
            return
        nc = self.nc
        sems = self.sems
        with nc.Block(self.fresh(name or "blk")) as blk:
            for ename in ENGINES:
                mine = [o for o in ops if o.eng == ename]
                if not mine:
                    continue
                dec = getattr(blk, ename)

                def body(eng, mine=mine):
                    last_wait = {}
                    for o in mine:
                        for (s, v) in o.waits:
                            if last_wait.get(s, -1) >= v:
                                continue
                            last_wait[s] = v
                            eng.wait_ge(sems[s], v)
                        instr = o.fn(eng)
                        if o.tick is not None:
                            s, dd = o.tick
                            instr.then_inc(sems[s], dd)
                dec(body)

    def close(self):
        self.stack.close()


INPUT_SPECS = {
    "L0_attn_x": (16, 2048, 1024),
    "L0_attn_w_q": (1024, 1024), "L0_attn_b_q": (1024,),
    "L0_attn_w_k": (1024, 1024), "L0_attn_b_k": (1024,),
    "L0_attn_w_v": (1024, 1024), "L0_attn_b_v": (1024,),
    "L0_attn_w_o": (1024, 1024), "L0_attn_b_o": (1024,),
    "L0_attn_ln_gamma": (1024,), "L0_attn_ln_beta": (1024,),
    "L0_ffn_router_w": (1024, 4), "L0_ffn_router_b": (4,),
    "L0_ffn_load_balancer": (4,),
    "L0_ffn_sw_w1": (1024, 4096), "L0_ffn_sw_w3": (1024, 4096),
    "L0_ffn_sw_w2": (4096, 1024),
    "L0_ffn_me_eq_w": (1024, 1024), "L0_ffn_me_eq_b": (1024,),
    "L0_ffn_me_wv": (1024, 1024), "L0_ffn_me_bv": (1024,),
    "L0_ffn_me_wo": (1024, 1024), "L0_ffn_me_bo": (1024,),
    "L0_ffn_me_c1w": (1024, 2048), "L0_ffn_me_c1b": (2048,),
    "L0_ffn_me_c2w": (2048, 1024), "L0_ffn_me_c2b": (1024,),
    "L0_ffn_ce_syn_w": (1024, 1024), "L0_ffn_ce_syn_b": (1024,),
    "L0_ffn_ce_wv": (1024, 1024), "L0_ffn_ce_bv": (1024,),
    "L0_ffn_ce_wo": (1024, 1024), "L0_ffn_ce_bo": (1024,),
    "L0_ffn_ce_n1g": (1024,), "L0_ffn_ce_n1b": (1024,),
    "L0_ffn_ce_f1w": (1024, 2048), "L0_ffn_ce_f1b": (2048,),
    "L0_ffn_ce_f2w": (2048, 1024), "L0_ffn_ce_f2b": (1024,),
    "L0_ffn_ce_n2g": (1024,), "L0_ffn_ce_n2b": (1024,),
    "L0_ffn_ce_gen_w": (1024, 1024), "L0_ffn_ce_gen_b": (1024,),
    "L0_ffn_ml_w1": (1024, 4096), "L0_ffn_ml_b1": (4096,),
    "L0_ffn_ml_w2": (4096, 1024), "L0_ffn_ml_b2": (1024,),
    "L1_attn_ln_w": (1024,), "L1_attn_ln_b": (1024,),
    "L1_attn_in_proj_w": (3072, 1024), "L1_attn_in_proj_b": (3072,),
    "L1_attn_out_proj_w": (1024, 1024), "L1_attn_out_proj_b": (1024,),
    "L1_attn_w0": (4, 256, 256), "L1_attn_w1": (4, 256, 256),
    "L1_attn_w2": (4, 256, 256),
    "L1_ffn_Wr": (1024, 8), "L1_ffn_br": (8,),
    "L1_ffn_W1": (8, 1024, 2048), "L1_ffn_b1": (8, 2048),
    "L1_ffn_W2": (8, 2048, 1024), "L1_ffn_b2": (8, 1024),
}

BT_ROWS = [
    ("b_q8", "L0_attn_b_q", None, 0.125),
    ("b_k", "L0_attn_b_k", None, None),
    ("ln_g", "L0_attn_ln_gamma", None, None),
    ("ln_b", "L0_attn_ln_beta", None, None),
    ("me_eq_b", "L0_ffn_me_eq_b", None, None),
    ("me_bv", "L0_ffn_me_bv", None, None),
    ("me_bo", "L0_ffn_me_bo", None, None),
    ("ce_syn_b", "L0_ffn_ce_syn_b", None, None),
    ("ce_bv", "L0_ffn_ce_bv", None, None),
    ("ce_bo", "L0_ffn_ce_bo", None, None),
    ("ce_n1g", "L0_ffn_ce_n1g", None, None),
    ("ce_n1b", "L0_ffn_ce_n1b", None, None),
    ("ce_n2g", "L0_ffn_ce_n2g", None, None),
    ("ce_n2b", "L0_ffn_ce_n2b", None, None),
    ("ce_f2b", "L0_ffn_ce_f2b", None, None),
    ("l1_ln_w", "L1_attn_ln_w", None, None),
    ("l1_ln_b", "L1_attn_ln_b", None, None),
    ("me_c1b", "L0_ffn_me_c1b", None, None),
    ("ce_f1b", "L0_ffn_ce_f1b", None, None),
    ("ml_b1", "L0_ffn_ml_b1", None, None),
    ("inp_b8", "L1_attn_in_proj_b", (0, 1024), 0.125),
    ("inp_bk", "L1_attn_in_proj_b", (1024, 2048), None),
] + [(f"l1_b1_{e}", "L1_ffn_b1", ("row", e), None) for e in range(8)]
NROWS = len(BT_ROWS)


def build_nc(stage="full"):
    nc = bass.Bass("TRN2", target_bir_lowering=False)
    P = Prog(nc)
    dram = {}
    for name, shape in INPUT_SPECS.items():
        if name == "L0_attn_x":
            dram["x"] = nc.dram_tensor("x", [NSEQ, S, D], F32, kind="ExternalInput")
        else:
            dram[name] = nc.dram_tensor(name, list(shape), F32, kind="ExternalInput")
    out = nc.dram_tensor("out", [NSEQ, S, D], F32, kind="ExternalOutput")
    k = Kern(P, dram, out, stage)
    k.build()
    P.close()
    return nc


class Kern:
    def __init__(self, P, dram, out, stage):
        self.P = P
        self.dram = dram
        self.out = out
        self.stage = stage

    def build(self):
        P = self.P
        self.X = P.sb("X", [128, NCH, S], F32)
        self.XN = P.sb("XN", [128, NCH, S], BF16)
        self.ID32 = P.sb("id32", [128, 128], F32)
        self.ID16 = P.sb("id16", [128, 128], BF16)
        self.ONES32 = P.sb("ones32", [1, 512], F32)
        self.ONESC = P.sb("onesc", [128, 1], F32)
        self.ONESC16 = P.sb("onesc16", [128, 1], BF16)
        self.ONESR16 = P.sb("onesr16", [1, 256], BF16)
        self.MASKH = P.sb("maskh", [128, 2], BF16)

        self.prep_const_phase()
        self.prep_bias_phase()

        for seq in range(NSEQ):
            self.load_x_phase(seq)
            if self.stage == "io":
                self.store_phase(seq, self.X, F32)
                continue
            for layer in range(2):
                self.layer_st = P.scope()
                self.R_st = P.scope()
                self.R = self.alloc_hg(self.R_st)
                self.ln_router_phase(seq, layer)
                if self.stage == "ln" and layer == 0:
                    self.R_st.close()
                    self.layer_st.close()
                    break
                self.attn_phase(seq, layer)
                self.R_st.close()
                if layer == 1:
                    self.lact_phase(seq)
                    self.attn_st.close()
                if self.stage == f"l{layer}attn":
                    self.layer_st.close()
                    break
                self.ffn_phase(seq, layer)
                self.layer_st.close()
                if self.stage.split(":")[0] == f"l{layer}":
                    break
            if self.stage == "ln":
                self.store_phase(seq, self.XN, BF16)
            else:
                self.store_phase(seq, self.X, F32)
        P.op("sync", lambda e: e.nop(), waits=[(s, P.cnt[s]) for s in ("XD0", "XD1", "XD2", "XD3")])
        P.flush("fin")

    def prep_const_phase(self):
        P = self.P
        for ident in (self.ID32, self.ID16):
            pm = P.op("gpsimd", lambda e, t=ident: e.memset(t[:], 0.0), tick="PP")
            P.op("gpsimd", lambda e, t=ident: e.affine_select(
                out=t[:], in_=t[:], compare_op=ALU.not_equal, fill=1.0,
                base=0, pattern=[[-1, 128]], channel_multiplier=1),
                waits=[("PP", pm)], tick="PP")
        P.op("gpsimd", lambda e: e.memset(self.ONES32[:], 1.0), tick="PP")
        P.op("gpsimd", lambda e: e.memset(self.ONESC[:], 1.0), tick="PP")
        P.op("gpsimd", lambda e: e.memset(self.ONESC16[:], 1.0), tick="PP")
        P.op("gpsimd", lambda e: e.memset(self.ONESR16[:], 1.0), tick="PP")
        pm = P.op("gpsimd", lambda e: e.memset(self.MASKH[:], 0.0), tick="PP")
        pm = P.op("gpsimd", lambda e: e.memset(self.MASKH[0:64, 0:1], 1.0),
                  waits=[("PP", pm)], tick="PP")
        P.op("gpsimd", lambda e: e.memset(self.MASKH[64:128, 1:2], 1.0),
             waits=[("PP", pm)], tick="PP")
        P.flush("consts")

    def prep_bias_phase(self):
        P = self.P
        d = self.dram
        self.BT = P.sb("BT", [128, 32, NROWS], F32)
        stp = P.scope()
        rows = P.sb_in(stp, "brows", [128, 4096], F32)
        pm = P.op("gpsimd", lambda e: e.memset(rows[:], 0.0), tick="PP")
        self.row_of = {}
        dma_t = 0
        r = 0
        for (name, key, sl, scale) in BT_ROWS:
            self.row_of[name] = r
            if sl is None:
                src = d[key].ap()
                length = INPUT_SPECS[key][0]
            elif isinstance(sl, tuple) and sl[0] == "row":
                src = d[key][sl[1], :]
                length = INPUT_SPECS[key][1]
            else:
                src = d[key][sl[0]:sl[1]]
                length = sl[1] - sl[0]
            dma_t = P.dma("sync", rows[r:r + 1, 0:length], src,
                          waits=[("PP", pm)])
            r += 1
        row_deps = P.qdeps("sync")
        psb = [P.ps_in(stp, "btp", [128, 128], F32) for _ in range(2)]
        ps_free = [0, 0]
        ev = 0
        for cs in range(32):
            ps = psb[cs % 2]
            mt = P.op("tensor", lambda e, cs=cs, ps=ps: e.transpose(
                ps[:], rows[:, cs * 128:(cs + 1) * 128], self.ID32[:]),
                waits=row_deps + [("PP", P.cnt["PP"]), ("EV", ps_free[cs % 2])],
                tick="MM")
            ev = P.op("scalar", lambda e, cs=cs, ps=ps: e.activation(
                self.BT[:, cs, :], ps[:, 0:NROWS], AF.Copy), waits=[("MM", mt)], tick="EV")
            ps_free[cs % 2] = ev
        for (name, key, sl, scale) in BT_ROWS:
            if scale is None:
                continue
            for c in range(8):
                P.op("vector", lambda e, name=name, c=c, s=scale: e.tensor_scalar_mul(
                    self.bias_col(name, c), self.bias_col(name, c), s),
                    waits=[("EV", ev)], tick="VV")
        stp.close()
        P.flush("bias")

    def bias_col(self, name, chunk):
        r = self.row_of[name]
        return self.BT[:, chunk, r:r + 1]

    def load_row(self, st, key, sl=None, waits=()):
        P = self.P
        d = self.dram
        if sl is None:
            src = d[key].ap()
            length = INPUT_SPECS[key][0]
        elif isinstance(sl, tuple) and sl[0] == "row":
            src = d[key][sl[1], :]
            length = INPUT_SPECS[key][1]
        else:
            src = d[key][sl[0]:sl[1]]
            length = sl[1] - sl[0]
        t = P.sb_in(st, "row", [1, length], F32)
        dma_t = P.dma("sync", t[:], src, waits=waits)
        return t, dma_t

    def load_x_phase(self, seq):
        P = self.P
        x = self.dram["x"]
        stp = P.scope()
        STG = P.sb_in(stp, "stg", [128, 2, D], F32)
        psb = [P.ps_in(stp, "xl", [128, 512], F32) for _ in range(4)]
        ps_free = [0, 0, 0, 0]
        stg_free = [0, 0]
        for tt in range(NTT):
            buf = tt % 2
            dma_t = P.dma("sync", STG[:, buf, :], x[seq, tt * 128:(tt + 1) * 128, :],
                          waits=[("MM", stg_free[buf])])
            for half in range(2):
                b = (tt % 2) * 2 + half
                ps = psb[b]
                mt = 0
                for k in range(4):
                    c = half * 4 + k
                    mt = P.op("tensor", lambda e, ps=ps, k=k, c=c, buf=buf: e.transpose(
                        ps[:, k * 128:(k + 1) * 128],
                        STG[:, buf, c * 128:(c + 1) * 128], self.ID32[:]),
                        waits=[dma_t, ("EV", ps_free[b])], tick="MM")
                ps_free[b] = P.op("scalar", lambda e, ps=ps, half=half, tt=tt: e.activation(
                    self.X[:, half * 4:half * 4 + 4, tt * 128:(tt + 1) * 128],
                    ps[:].rearrange("p (c t) -> p c t", t=128),
                    AF.Copy), waits=[("MM", mt)], tick="EV")
                if half == 1:
                    stg_free[buf] = mt
        stp.close()
        P.flush(f"loadx{seq}")

    def store_phase(self, seq, srcten, dtype):
        P = self.P
        ident = self.ID32 if dtype == F32 else self.ID16
        stp = P.scope()
        STG = P.sb_in(stp, "stg", [128, 2, D], F32)
        psb = [P.ps_in(stp, "xs", [128, 512], dtype) for _ in range(4)]
        ps_free = [0, 0, 0, 0]
        stg_free = [0, 0]
        for tt in range(NTT):
            buf = tt % 2
            evs = []
            for half in range(2):
                b = (tt % 2) * 2 + half
                ps = psb[b]
                mt = 0
                for k in range(4):
                    c = half * 4 + k
                    mt = P.op("tensor", lambda e, ps=ps, k=k, c=c, tt=tt: e.transpose(
                        ps[:, k * 128:(k + 1) * 128],
                        srcten[:, c, tt * 128:(tt + 1) * 128], ident[:]),
                        waits=[("EV", ps_free[b])], tick="MM")
                ev = P.op("scalar", lambda e, ps=ps, half=half, buf=buf: e.activation(
                    STG[:, buf, half * 512:half * 512 + 512], ps[:], AF.Copy),
                    waits=[("MM", mt), stg_free[buf]], tick="EV")
                ps_free[b] = ev
                evs.append(ev)
            dma_t = P.dma("sync", self.out[seq, tt * 128:(tt + 1) * 128, :],
                          STG[:, buf, :], waits=[("EV", evs[-1])])
            stg_free[buf] = dma_t
        P.op("sync", lambda e: e.nop(), waits=P.qdeps("sync"))
        stp.close()
        P.flush(f"storex{seq}")

    def alloc_ln_views(self, st, R, n=512):
        P = self.P
        psS = R["psS"]
        lp = dict(
            ps_s=psS[0:1, 0, 0, 0:n],
            ps_q=psS[0:1, 0, 1, 0:n],
            ps_mu=psS[:, 1, 0, 0:n],
            ps_rs=psS[:, 1, 1, 0:n],
            sq=P.sb_in(st, "sq", [128, 2, n], F32),
            mu=P.sb_in(st, "mu", [1, n], F32),
            e2r=P.sb_in(st, "e2r", [1, n], F32),
            var=P.sb_in(st, "var", [1, n], F32),
            rstd=P.sb_in(st, "rstd", [1, n], F32),
            sq_mm_prev=[0, 0], stats_ev_prev=0, apply_prev=0,
        )
        return lp

    def alloc_ln(self, st, n=512):
        P = self.P
        lnps = dict(
            ps_s=P.ps_in(st, "lns", [1, n], F32),
            ps_q=P.ps_in(st, "lnq", [1, n], F32),
            ps_mu=P.ps_in(st, "mub", [128, n], F32),
            ps_rs=P.ps_in(st, "rsb", [128, n], F32),
            sq=P.sb_in(st, "sq", [128, 2, n], F32),
            mu=P.sb_in(st, "mu", [1, n], F32),
            e2r=P.sb_in(st, "e2r", [1, n], F32),
            var=P.sb_in(st, "var", [1, n], F32),
            rstd=P.sb_in(st, "rstd", [1, n], F32),
            sq_mm_prev=[0, 0], stats_ev_prev=0, apply_prev=0,
        )
        return lnps

    def emit_ln_tb(self, lp, get_x, fp32_src, waits_base):
        P = self.P
        w0 = list(waits_base)
        mts = 0
        onesc = self.ONESC if fp32_src else self.ONESC16
        for c in range(NCH):
            mts = P.op("tensor", lambda e, c=c: e.matmul(
                lp["ps_s"][:], onesc[:], get_x(c),
                start=(c == 0), stop=(c == NCH - 1)),
                waits=w0 + [("EV", lp["stats_ev_prev"])], tick="MM")
        sq_mm = 0
        sq_ev = 0
        sq_dve = lp.get("sq_dve", False)
        for c in range(NCH):
            b = c % 2
            if sq_dve:
                sq_ev = P.op("vector", lambda e, c=c, b=b: e.tensor_tensor(
                    lp["sq"][:, b, :], get_x(c), get_x(c), ALU.mult),
                    waits=w0 + [("MM", lp["sq_mm_prev"][b])], tick="VV")
                sq_mm = P.op("tensor", lambda e, c=c, b=b: e.matmul(
                    lp["ps_q"][:], self.ONESC[:], lp["sq"][:, b, :],
                    start=(c == 0), stop=(c == NCH - 1)),
                    waits=[("VV", sq_ev), ("EV", lp["stats_ev_prev"])], tick="MM")
            else:
                sq_ev = P.op("scalar", lambda e, c=c, b=b: e.activation(
                    lp["sq"][:, b, :], get_x(c), AF.Square),
                    waits=w0 + [("MM", lp["sq_mm_prev"][b])], tick="EV")
                sq_mm = P.op("tensor", lambda e, c=c, b=b: e.matmul(
                    lp["ps_q"][:], self.ONESC[:], lp["sq"][:, b, :],
                    start=(c == 0), stop=(c == NCH - 1)),
                    waits=[("EV", max(sq_ev, lp["stats_ev_prev"]))], tick="MM")
            lp["sq_mm_prev"][b] = sq_mm
        ev_mu = P.op("scalar", lambda e: e.activation(
            lp["mu"][:], lp["ps_s"][:], AF.Copy, scale=1.0 / D),
            waits=[("MM", mts)], tick="EV")
        ev_e2 = P.op("scalar", lambda e: e.activation(
            lp["e2r"][:], lp["ps_q"][:], AF.Copy, scale=1.0 / D),
            waits=[("MM", sq_mm)], tick="EV")
        lp["stats_ev_prev"] = ev_e2
        v1 = P.op("vector", lambda e: e.tensor_tensor(
            lp["var"][:], lp["mu"][:], lp["mu"][:], ALU.mult),
            waits=[("EV", ev_mu)], tick="VV")
        v2 = P.op("vector", lambda e: e.scalar_tensor_tensor(
            out=lp["var"][:], in0=lp["e2r"][:], scalar=EPS, in1=lp["var"][:],
            op0=ALU.add, op1=ALU.subtract),
            waits=[("EV", ev_e2), ("VV", v1)], tick="VV")
        e_ln = P.op("scalar", lambda e: e.activation(
            lp["rstd"][:], lp["var"][:], AF.Ln), waits=[("VV", v2)], tick="EV")
        e_rs = P.op("scalar", lambda e: e.activation(
            lp["rstd"][:], lp["rstd"][:], AF.Exp, scale=-0.5),
            waits=[("EV", e_ln)], tick="EV")
        bmu = P.op("tensor", lambda e: e.matmul(
            lp["ps_mu"][:], self.ONES32[0:1, 0:128], lp["mu"][:],
            start=True, stop=True),
            waits=[("EV", ev_mu), ("VV", lp["apply_prev"])], tick="MM")
        brs = P.op("tensor", lambda e: e.matmul(
            lp["ps_rs"][:], self.ONES32[0:1, 0:128], lp["rstd"][:],
            start=True, stop=True),
            waits=[("EV", e_rs), ("VV", lp["apply_prev"])], tick="MM")
        return dict(bmu=bmu, brs=brs, src_mm=max(mts, sq_mm),
                    src_ev=0 if sq_dve else sq_ev,
                    src_vv=sq_ev if sq_dve else 0)

    def ln_router_phase(self, seq, layer):
        P = self.P
        d = self.dram
        E = 4 if layer == 0 else 8
        self.CW_TOK = P.sb_in(self.layer_st, "cwtok", [128, NTT, 8], F32)
        self.xn_ev = [0] * NTB
        st = P.scope()
        rw = P.sb_in(st, "rw", [128, NCH, E], F32)
        key = "L0_ffn_router_w" if layer == 0 else "L1_ffn_Wr"
        rload = P.dma("sync", rw[:], d[key].ap().rearrange("(c p) e -> p c e", p=128))
        rb = P.sb_in(st, "rb", [1, 16], F32)
        rbc = P.sb_in(st, "rbc", [8, 1], F32)
        if layer == 0:
            t1 = P.dma("sync", rb[0:1, 0:4], d["L0_ffn_router_b"].ap())
            t2 = P.dma("sync", rb[0:1, 8:12], d["L0_ffn_load_balancer"].ap())
            v = P.op("vector", lambda e: e.tensor_tensor(
                rb[0:1, 0:4], rb[0:1, 0:4], rb[0:1, 8:12], ALU.add),
                waits=[t1, t2], tick="VV")
        else:
            t1 = P.dma("sync", rb[0:1, 0:8], d["L1_ffn_br"].ap())
            v = P.op("vector", lambda e: e.tensor_copy(rb[0:1, 0:8], rb[0:1, 0:8]),
                     waits=[t1], tick="VV")
        psrb = self.R["psP"][0][0:8, 0:1]
        mtrb = P.op("tensor", lambda e, E=E: e.matmul(
            psrb[0:E, 0:1], rb[0:1, 0:E], self.ONES32[0:1, 0:1],
            start=True, stop=True), waits=[("VV", v)], tick="MM")
        evrb = P.op("scalar", lambda e, E=E: e.activation(
            rbc[0:E, :], psrb[0:E, 0:1], AF.Copy), waits=[("MM", mtrb)], tick="EV")
        self.R["psP_free"][0] = ("EV", evrb)

        lp = self.alloc_ln_views(st, self.R)
        xt32 = P.sb_in(st, "xt32", [128, NCH, 512], F32)
        lgf = P.sb_in(st, "lgf", [8, 512], F32)
        lgt = P.sb_in(st, "lgt", [128, 4, 8], F32)
        tmp = P.sb_in(st, "tmp", [128, 4, 8], F32)
        keep = P.sb_in(st, "keep", [128, 4, 8], F32)
        m1 = P.sb_in(st, "m1", [128, 4], F32)
        m2 = P.sb_in(st, "m2", [128, 4], F32)
        rden = P.sb_in(st, "rden", [128, 4], F32)
        ps_lg = self.R["psC"][0][0:8, :]
        ps_tm = self.R["psC"][1][:, 0:32].rearrange("p (a b) -> p a b", b=8)

        prev_rmm = 0
        prev_lg_ev = 0
        prev_tm_ev = 0
        for tb in range(NTB):
            tsl = slice(tb * TB, (tb + 1) * TB)
            r = self.emit_ln_tb(lp, lambda c, tsl=tsl: self.X[:, c, tsl], True, [])
            apply_vv = 0
            for c in range(NCH):
                vs = P.op("vector", lambda e, c=c, tsl=tsl: e.tensor_tensor(
                    xt32[:, c, :], self.X[:, c, tsl], lp["ps_mu"][:], ALU.subtract),
                    waits=[("MM", max(r["bmu"], prev_rmm))], tick="VV")
                vm = P.op("vector", lambda e, c=c: e.tensor_tensor(
                    xt32[:, c, :], xt32[:, c, :], lp["ps_rs"][:], ALU.mult),
                    waits=[("MM", r["brs"]), ("VV", vs)], tick="VV")
                apply_vv = vm
                xnev = P.op("scalar", lambda e, c=c, tsl=tsl: e.activation(
                    self.XN[:, c, tsl], xt32[:, c, :], AF.Copy),
                    waits=[("VV", vm)], tick="EV")
            self.xn_ev[tb] = xnev
            lp["apply_prev"] = apply_vv
            rmm = 0
            for c in range(NCH):
                rmm = P.op("tensor", lambda e, c=c: e.matmul(
                    ps_lg[0:E, :], rw[:, c, :], xt32[:, c, :],
                    start=(c == 0), stop=(c == NCH - 1)),
                    waits=[("VV", apply_vv), rload,
                           ("EV", prev_lg_ev)], tick="MM")
            prev_rmm = rmm
            ev_lg = P.op("scalar", lambda e: e.activation(
                lgf[0:E, :], ps_lg[0:E, :], AF.Identity, bias=rbc[0:E, 0:1]),
                waits=[("MM", rmm), ("EV", evrb)], tick="EV")
            prev_lg_ev = ev_lg
            mtt = 0
            for k in range(4):
                mtt = P.op("tensor", lambda e, k=k: e.transpose(
                    ps_tm[:, k, 0:E], lgf[0:E, k * 128:(k + 1) * 128],
                    self.ID32[0:E, 0:E]),
                    waits=[("EV", max(ev_lg, prev_tm_ev))], tick="MM")
            ev_tm = P.op("scalar", lambda e: e.activation(
                lgt[:, :, 0:E], ps_tm[:, :, 0:E], AF.Copy),
                waits=[("MM", mtt)], tick="EV")
            prev_tm_ev = ev_tm
            for k in range(4):
                lg = lgt[:, k, 0:E]
                cwk = self.CW_TOK[:, tb * 4 + k, 0:E]
                tm = tmp[:, k, 0:E]
                kp = keep[:, k, 0:E]
                m1c = m1[:, k:k + 1]
                m2c = m2[:, k:k + 1]
                rdc = rden[:, k:k + 1]
                a = P.op("vector", lambda e, lg=lg, m1c=m1c: e.tensor_reduce(
                    m1c, lg, AXL.X, ALU.max), waits=[("EV", ev_tm)], tick="VV")
                b_ = P.op("vector", lambda e, lg=lg, tm=tm, m1c=m1c: e.tensor_scalar(
                    tm, lg, m1c, None, ALU.subtract), waits=[("VV", a)], tick="VV")
                if layer == 0:
                    c_ = P.op("vector", lambda e, tm=tm, kp=kp: e.tensor_scalar(
                        kp, tm, 0.0, None, ALU.is_equal), waits=[("VV", b_)], tick="VV")
                    d_ = P.op("vector", lambda e, kp=kp, tm=tm, cwk=cwk: e.scalar_tensor_tensor(
                        out=cwk, in0=kp, scalar=-1e30, in1=tm, op0=ALU.mult, op1=ALU.add),
                        waits=[("VV", c_)], tick="VV")
                    e_ = P.op("vector", lambda e, cwk=cwk, m2c=m2c: e.tensor_reduce(
                        m2c, cwk, AXL.X, ALU.max), waits=[("VV", d_)], tick="VV")
                    f_ = P.op("vector", lambda e, tm=tm, kp=kp, m2c=m2c: e.tensor_scalar(
                        kp, tm, m2c, None, ALU.is_ge), waits=[("VV", e_)], tick="VV")
                    g_ = P.op("scalar", lambda e, tm=tm: e.activation(
                        tm, tm, AF.Exp), waits=[("VV", f_)], tick="EV")
                    h_ = P.op("scalar", lambda e, m2c=m2c, rdc=rdc: e.activation(
                        rdc, m2c, AF.Exp), waits=[("VV", f_)], tick="EV")
                    i_ = P.op("vector", lambda e, rdc=rdc: e.tensor_scalar_add(
                        rdc, rdc, 1.0), waits=[("EV", h_)], tick="VV")
                    j_ = P.op("vector", lambda e, rdc=rdc: e.reciprocal(
                        rdc, rdc), waits=[("VV", i_)], tick="VV")
                    k_ = P.op("vector", lambda e, cwk=cwk, tm=tm, kp=kp: e.tensor_tensor(
                        cwk, tm, kp, ALU.mult), waits=[("EV", g_)], tick="VV")
                    P.op("vector", lambda e, cwk=cwk, rdc=rdc: e.tensor_scalar(
                        cwk, cwk, rdc, None, ALU.mult),
                        waits=[("VV", max(k_, j_))], tick="VV")
                else:
                    c_ = P.op("scalar", lambda e, tm=tm: e.activation(
                        tm, tm, AF.Exp), waits=[("VV", b_)], tick="EV")
                    d_ = P.op("vector", lambda e, tm=tm, rdc=rdc: e.tensor_reduce(
                        rdc, tm, AXL.X, ALU.add), waits=[("EV", c_)], tick="VV")
                    e_ = P.op("vector", lambda e, rdc=rdc: e.reciprocal(
                        rdc, rdc), waits=[("VV", d_)], tick="VV")
                    f_ = P.op("vector", lambda e, tm=tm, rdc=rdc: e.tensor_scalar(
                        tm, tm, rdc, None, ALU.mult), waits=[("VV", e_)], tick="VV")
                    g_ = P.op("vector", lambda e, tm=tm, m1c=m1c: e.tensor_reduce(
                        m1c, tm, AXL.X, ALU.max), waits=[("VV", f_)], tick="VV")
                    h_ = P.op("vector", lambda e, tm=tm, kp=kp, m1c=m1c: e.tensor_scalar(
                        kp, tm, m1c, None, ALU.is_equal), waits=[("VV", g_)], tick="VV")
                    i_ = P.op("vector", lambda e, kp=kp, tm=tm, cwk=cwk: e.scalar_tensor_tensor(
                        out=cwk, in0=kp, scalar=-2.0, in1=tm, op0=ALU.mult, op1=ALU.add),
                        waits=[("VV", h_)], tick="VV")
                    j_ = P.op("vector", lambda e, cwk=cwk, m2c=m2c: e.tensor_reduce(
                        m2c, cwk, AXL.X, ALU.max), waits=[("VV", i_)], tick="VV")
                    k_ = P.op("vector", lambda e, tm=tm, kp=kp, m2c=m2c: e.tensor_scalar(
                        kp, tm, m2c, None, ALU.is_ge), waits=[("VV", j_)], tick="VV")
                    P.op("vector", lambda e, cwk=cwk, tm=tm, kp=kp: e.tensor_tensor(
                        cwk, tm, kp, ALU.mult), waits=[("VV", k_)], tick="VV")
        self.cw_ready = P.cnt["VV"]
        R = self.R
        R["psS_free"][0] = ("EV", lp["stats_ev_prev"])
        R["psS_free"][1] = ("VV", lp["apply_prev"])
        R["psC_free"][0] = ("EV", prev_lg_ev)
        R["psC_free"][1] = ("EV", prev_tm_ev)
        st.close()

    def xn2_phase(self, seq):
        P = self.P
        st = P.scope()
        lp = self.alloc_ln_views(st, self.R)
        if self.R["psS_free"][0] is not None:
            lp["stats_ev_prev"] = self.R["psS_free"][0][1]
        if self.R["psS_free"][1] is not None and self.R["psS_free"][1][0] == "VV":
            lp["apply_prev"] = self.R["psS_free"][1][1]
        t32 = P.sb_in(st, "t32", [128, 512], F32)
        prev_ap = 0
        self.xn2_ev = [0] * NTB
        for tb in range(NTB):
            tsl = slice(tb * TB, (tb + 1) * TB)
            r = self.emit_ln_tb(lp, lambda c, tsl=tsl: self.XN[:, c, tsl], False,
                                [("EV", self.xn_ev[tb])])
            apply_vv = 0
            for c in range(NCH):
                vs = P.op("vector", lambda e, c=c, tsl=tsl: e.tensor_tensor(
                    t32[:], self.XN[:, c, tsl], lp["ps_mu"][:], ALU.subtract),
                    waits=[("MM", r["bmu"]), ("EV", prev_ap)], tick="VV")
                vm = P.op("vector", lambda e: e.tensor_tensor(
                    t32[:], t32[:], lp["ps_rs"][:], ALU.mult),
                    waits=[("MM", r["brs"]), ("VV", vs)], tick="VV")
                apply_vv = vm
                prev_ap = P.op("scalar", lambda e, c=c, tsl=tsl: e.activation(
                    self.XN2[:, c, tsl], t32[:], AF.Identity,
                    scale=self.bias_col("l1_ln_w", c),
                    bias=self.bias_col("l1_ln_b", c)),
                    waits=[("VV", vm)], tick="EV")
            self.xn2_ev[tb] = prev_ap
            lp["apply_prev"] = apply_vv
        R = self.R
        R["psS_free"][0] = ("EV", lp["stats_ev_prev"])
        R["psS_free"][1] = ("VV", lp["apply_prev"])
        st.close()

    def attn_phase(self, seq, layer):
        P = self.P
        src = self.XN
        if layer == 1:
            self.attn_st = P.scope()
            self.XN2 = P.sb_in(self.attn_st, "XN2", [128, NCH, S], BF16)
            self.xn2_phase(seq)
            src = self.XN2
        R = self.R
        if layer == 0:
            st = P.scope()
            CTX = P.sb_in(st, "ctx", [128, NCH, S], BF16)
            stsb = P.scope()
            self.alloc_hg_sb(stsb, R)
            R["vrow"], R["vrow_dep"] = self.load_row(stsb, "L0_attn_b_v")
            self.emit_hg_set(seq, layer, src, list(range(NCH)), CTX, R)
            stsb.close()
            self.out_proj_l0(seq, CTX, st)
            P.flush(f"attn{seq}_{layer}")
            st.close()
        else:
            for half in range(2):
                st = P.scope()
                CTX = P.sb_in(st, "ctx", [128, 4, S], BF16)
                stsb = P.scope()
                self.alloc_hg_sb(stsb, R)
                R["vrow"], R["vrow_dep"] = self.load_row(
                    stsb, "L1_attn_in_proj_b", (2048, 3072))
                self.emit_hg_set(seq, layer, src,
                                 [half * 4 + lg for lg in range(4)], CTX, R)
                stsb.close()
                self.out_proj_l1(seq, CTX, half, st)
                P.flush(f"attn{seq}_{layer}_{half}")
                st.close()

    def cnt_guard(self):
        P = self.P
        return [(s, P.cnt[s]) for s in ("MM", "EV", "VV", "PP")]

    def alloc_hg(self, st):
        P = self.P
        R = dict(
            psS=P.ps_in(st, "psS", [128, 2, 2, 512], F32),
            psC=[P.ps_in(st, "psC", [128, 512], F32) for _ in range(2)],
            psP=[P.ps_in(st, "psP", [128, 512], F32) for _ in range(2)],
            wnat_free=[0, 0],
            wt_free=0,
            kt_free=0, qt_free=0,
            v_free=0,
            pt_free=[0, 0],
            psS_free=[None, None],
            psC_free=[None, None],
            psP_free=[None, None],
            sq_free=[0, 0],
            stat_free=[0, 0],
            rl_free=0,
            ct32_free=0,
            ones_pp=0,
        )
        return R

    def alloc_hg_sb(self, st, R):
        P = self.P
        R.update(
            wnat=[P.sb_in(st, "wnat", [128, D], F32)],
            WT=[P.sb_in(st, "wt", [128, 3, NCH, 128], BF16) for _ in range(2)],
            KT=[P.sb_in(st, "kt", [128, S], BF16) for _ in range(2)],
            QT=[P.sb_in(st, "qt", [128, S], BF16) for _ in range(2)],
            V=[P.sb_in(st, "v", [128, NTT, 130], BF16) for _ in range(2)],
            PT=P.sb_in(st, "pt", [128, 2, 2, 512], BF16),
            NEGC=[P.sb_in(st, "negc", [128, 2, 4], F32) for _ in range(2)],
            CT32=P.sb_in(st, "ct32", [64, 512], F32),
            RLROW=P.sb_in(st, "rl", [1, 512], F32),
            SQ=P.sb_in(st, "sqn", [128, 2, 512], BF16),
            statq=[[P.sb_in(st, "stq", [1, 8], F32) for _ in range(2)]
                   for _ in range(2)],
            statk=[[P.sb_in(st, "stk", [1, 8], F32) for _ in range(2)]
                   for _ in range(2)],
        )
        g = self.cnt_guard()
        pp = P.op("gpsimd", lambda e: e.memset(R["V"][0][:, :, 64:65], 1.0),
                  waits=g, tick="PP")
        pp = P.op("gpsimd", lambda e: e.memset(R["V"][0][:, :, 129:130], 1.0),
                  tick="PP")
        pp = P.op("gpsimd", lambda e: e.memset(R["V"][1][:, :, 64:65], 1.0),
                  tick="PP")
        R["ones_pp"] = P.op("gpsimd", lambda e: e.memset(R["V"][1][:, :, 129:130], 1.0),
                            tick="PP")
        R["wnat_free"] = [0]
        R["wt_free"] = [0, 0]
        R["kt_free"] = [0, 0]
        R["qt_free"] = [0, 0]
        R["v_free"] = [0, 0]
        R["sq_free"] = [0, 0]
        R["stat_free"] = [[0, 0], [0, 0]]
        return R

    def emit_hg_proj(self, seq, layer, src, ch, par, R, out):
        P = self.P
        d = self.dram
        vrow, vrow_dep = R["vrow"], R["vrow_dep"]
        xn_deps = self.xn2_ev if layer == 1 else self.xn_ev
        WT, KT, QT, V = R["WT"][par], R["KT"][par], R["QT"][par], R["V"][par]
        NEGC, SQ = R["NEGC"][par], R["SQ"]
        statq, statk = R["statq"][par], R["statk"][par]
        psP = R["psP"]
        wkeys = {
            0: [("L0_attn_w_q", 0, 0.125), ("L0_attn_w_k", 0, None),
                ("L0_attn_w_v", 0, None)],
            1: [("L1_attn_in_proj_w", 0, 0.125), ("L1_attn_in_proj_w", 1024, None),
                ("L1_attn_in_proj_w", 2048, None)],
        }[layer]
        wt_ready = 0
        for mi, (key, roff, scale) in enumerate(wkeys):
            wb = 0
            wnat = R["wnat"][wb]
            dma_t = P.dma("sync", wnat[:],
                          d[key][roff + ch * 128: roff + (ch + 1) * 128, :],
                          waits=[("MM", R["wnat_free"][wb])])
            for g in range(2):
                ps = psP[g]
                mt = 0
                for k in range(4):
                    kc = g * 4 + k
                    mt = P.op("tensor", lambda e, ps=ps, k=k, kc=kc, wnat=wnat: e.transpose(
                        ps[:, k * 128:(k + 1) * 128],
                        wnat[:, kc * 128:(kc + 1) * 128], self.ID32[:]),
                        waits=[dma_t, psP_w(R, g)], tick="MM")
                if g == 1:
                    R["wnat_free"][wb] = mt
                if scale is None:
                    wt_ready = P.op("vector", lambda e, ps=ps, mi=mi, g=g: e.tensor_copy(
                        WT[:, mi, g * 4:g * 4 + 4, :],
                        ps[:].rearrange("p (c t) -> p c t", t=128)),
                        waits=[("MM", max(mt, R["wt_free"][par]))], tick="VV")
                else:
                    wt_ready = P.op("vector", lambda e, ps=ps, mi=mi, g=g, s=scale: e.tensor_scalar(
                        WT[:, mi, g * 4:g * 4 + 4, :],
                        ps[:].rearrange("p (c t) -> p c t", t=128),
                        s, None, ALU.mult),
                        waits=[("MM", max(mt, R["wt_free"][par]))], tick="VV")
                R["psP_free"][g] = ("VV", wt_ready)
                yield

        bq, bk = ("b_q8", "b_k") if layer == 0 else ("inp_b8", "inp_bk")
        kq_ev = 0
        for mi, (dst, bias, rd_free) in enumerate(
                ((QT, bq, R["qt_free"][par]), (KT, bk, R["kt_free"][par]))):
            for tb in range(NTB):
                tsl = slice(tb * TB, (tb + 1) * TB)
                b = (mi * NTB + tb) % 2
                ps = psP[b]
                mt = 0
                for kc in range(NCH):
                    mt = P.op("tensor", lambda e, ps=ps, mi=mi, kc=kc, tsl=tsl: e.matmul(
                        ps[:], WT[:, mi, kc, :], src[:, kc, tsl],
                        start=(kc == 0), stop=(kc == NCH - 1)),
                        waits=[("VV", wt_ready), ("EV", xn_deps[tb]),
                               psP_w(R, b)], tick="MM")
                kq_ev = P.op("vector", lambda e, ps=ps, dst=dst, bias=bias, tsl=tsl: e.tensor_scalar(
                    dst[:, tsl], ps[:], self.bias_col(bias, ch), None, ALU.add),
                    waits=[("MM", max(mt, rd_free))], tick="VV")
                R["psP_free"][b] = ("VV", kq_ev)
                yield
        v_ev = 0
        last_proj_mm = 0
        for tt in range(NTT):
            b = tt % 2
            ps = psP[b]
            for kc in range(NCH):
                P.op("tensor", lambda e, ps=ps, kc=kc, tt=tt: e.matmul(
                    ps[:, 0:128], src[:, kc, tt * 128:(tt + 1) * 128],
                    WT[:, 2, kc, :], start=(kc == 0), stop=False),
                    waits=[("VV", wt_ready), ("EV", xn_deps[tt // 4]),
                           psP_w(R, b)], tick="MM")
            mt = P.op("tensor", lambda e, ps=ps: e.matmul(
                ps[:, 0:128], self.ONES32[0:1, 0:128],
                vrow[0:1, ch * 128:(ch + 1) * 128], start=False, stop=True),
                waits=[vrow_dep], tick="MM")
            last_proj_mm = mt
            P.op("vector", lambda e, ps=ps, tt=tt: e.tensor_copy(
                V[:, tt, 0:64], ps[:, 0:64]),
                waits=[("MM", max(mt, R["v_free"][par]))], tick="VV")
            v_ev = P.op("vector", lambda e, ps=ps, tt=tt: e.tensor_copy(
                V[:, tt, 65:129], ps[:, 64:128]), waits=[("MM", mt)], tick="VV")
            R["psP_free"][b] = ("VV", v_ev)
            if tt % 2 == 1:
                yield

        sqf = R["sq_free"]
        negc_ev = 0
        sidx = 0
        for hj in range(2):
            last_red = 0
            for (dstt, stat) in ((QT, statq[hj]), (KT, statk[hj])):
                for tb in range(NTB):
                    tsl = slice(tb * TB, (tb + 1) * TB)
                    b = sidx % 2
                    sidx += 1
                    evq = P.op("gpsimd", lambda e, dstt=dstt, tsl=tsl, b=b: e.tensor_mul(
                        SQ[:, b, :], dstt[:, tsl], dstt[:, tsl]),
                        waits=[("VV", max(kq_ev, v_ev)), ("MM", sqf[b])], tick="PP")
                    mmb = P.op("tensor", lambda e, hj=hj, b=b: e.matmul(
                        psP[b][0:1, :], self.MASKH[:, hj:hj + 1], SQ[:, b, :],
                        start=True, stop=True),
                        waits=[("PP", evq), psP_w(R, b)], tick="MM")
                    sqf[b] = mmb
                    last_red = P.op("vector", lambda e, stat=stat, tb=tb, b=b: e.tensor_reduce(
                        stat[0:1, tb:tb + 1], psP[b][0:1, :], AXL.X, ALU.max),
                        waits=[("MM", max(mmb, R["stat_free"][par][hj]))], tick="VV")
                    R["psP_free"][b] = ("VV", last_red)
                    if tb % 2 == 1:
                        yield
            vk = P.op("vector", lambda e, hj=hj, statk=statk: e.tensor_reduce(
                statk[hj][0:1, 4:5], statk[hj][0:1, 0:4], AXL.X, ALU.max),
                waits=[("VV", last_red)], tick="VV")
            vq = P.op("vector", lambda e, hj=hj, statq=statq, statk=statk: e.tensor_scalar(
                statq[hj][0:1, 0:4], statq[hj][0:1, 0:4],
                statk[hj][0:1, 4:5], None, ALU.mult),
                waits=[("VV", vk)], tick="VV")
            el = P.op("scalar", lambda e, hj=hj, statq=statq: e.activation(
                statq[hj][0:1, 0:4], statq[hj][0:1, 0:4], AF.Ln),
                waits=[("VV", vq)], tick="EV")
            ee = P.op("scalar", lambda e, hj=hj, statq=statq: e.activation(
                statq[hj][0:1, 0:4], statq[hj][0:1, 0:4], AF.Exp, scale=0.5),
                waits=[("EV", el)], tick="EV")
            vm = P.op("vector", lambda e, hj=hj, statq=statq: e.tensor_scalar_mul(
                statq[hj][0:1, 0:4], statq[hj][0:1, 0:4], -1.05),
                waits=[("EV", ee)], tick="VV")
            mtb = P.op("tensor", lambda e, hj=hj, statq=statq: e.matmul(
                psP[0][:, 0:4], self.ONES32[0:1, 0:128], statq[hj][0:1, 0:4],
                start=True, stop=True),
                waits=[("VV", vm), psP_w(R, 0)], tick="MM")
            R["stat_free"][par][hj] = mtb
            negc_ev = P.op("vector", lambda e, hj=hj, NEGC=NEGC: e.tensor_copy(
                NEGC[:, hj, :], psP[0][:, 0:4]),
                waits=[("MM", mtb)], tick="VV")
            R["psP_free"][0] = ("VV", negc_ev)
        out.update(kq_ev=kq_ev, v_ev=v_ev, negc_ev=negc_ev,
                   last_proj_mm=last_proj_mm)
        yield

    def emit_hg_scores(self, seq, layer, par, CTX, ctxcol, R, out, feeder):
        P = self.P
        KT, QT, V, PT = R["KT"][par], R["QT"][par], R["V"][par], R["PT"]
        NEGC, CT32, RLROW = R["NEGC"][par], R["CT32"], R["RLROW"]
        psS, psC = R["psS"], R["psC"]
        kq_ev = max(out["kq_ev"], out["v_ev"])
        negc_ev = out["negc_ev"]
        last_score_mm = 0
        last_ctx_mm = 0
        pend = None
        for qb in range(NTB):
            qsl = slice(qb * TB, (qb + 1) * TB)
            ctx_mms = [0, 0]
            for ktp in range(NTT // 2):
                exp_evs = [0, 0]
                for hj in range(2):
                    base = 64 * hj
                    smm = 0
                    for pr in range(2):
                        kt = 2 * ktp + pr
                        smm = P.op(
                            "tensor", lambda e, hj=hj, pr=pr, base=base, kt=kt, qsl=qsl: e.matmul(
                                psS[:, hj, pr, :],
                                KT[base:base + 64, kt * 128:(kt + 1) * 128],
                                QT[base:base + 64, qsl], start=True, stop=True),
                            waits=[R["psS_free"][hj],
                                   ("VV", max(negc_ev, kq_ev))],
                            tick="MM")
                    last_score_mm = smm
                    ev = P.op("scalar", lambda e, hj=hj, qb=qb, NEGC=NEGC: e.activation(
                        PT[:, hj, :, :], psS[:, hj, :, :], AF.Exp,
                        bias=NEGC[:, hj, qb:qb + 1]),
                        waits=[("MM", max(smm, R["pt_free"][hj]))],
                        tick="EV")
                    R["psS_free"][hj] = ("EV", ev)
                    exp_evs[hj] = ev
                if ktp == 0 and pend is not None:
                    pend()
                    pend = None
                next(feeder, None)
                for hj in range(2):
                    for pr in range(2):
                        kt = 2 * ktp + pr
                        cm = P.op("tensor", lambda e, hj=hj, pr=pr, kt=kt: e.matmul(
                            psC[hj][0:65, :], V[:, kt, hj * 65:hj * 65 + 65],
                            PT[:, hj, pr, :], start=(kt == 0), stop=(kt == NTT - 1)),
                            waits=[("EV", exp_evs[hj]),
                                   (R["psC_free"][hj] if kt == 0 else None),
                                   ("VV", out["v_ev"] if kt == 0 else 0),
                                   ("PP", R["ones_pp"])], tick="MM")
                        R["pt_free"][hj] = cm
                        ctx_mms[hj] = cm
                        last_ctx_mm = cm

            def fin(ctx_mms=ctx_mms, qsl=qsl):
                for hj in range(2):
                    e_ln = P.op("scalar", lambda e, hj=hj: e.activation(
                        RLROW[:], psC[hj][64:65, :], AF.Ln),
                        waits=[("MM", max(ctx_mms[hj], R["rl_free"]))], tick="EV")
                    e_rl = P.op("scalar", lambda e: e.activation(
                        RLROW[:], RLROW[:], AF.Exp, scale=-1.0),
                        waits=[("EV", e_ln)], tick="EV")
                    ev_ct = P.op("vector", lambda e, hj=hj: e.tensor_copy(
                        CT32[:], psC[hj][0:64, :]),
                        waits=[("MM", ctx_mms[hj]), ("VV", R["ct32_free"])], tick="VV")
                    mt_rb = P.op("tensor", lambda e, hj=hj: e.matmul(
                        psC[hj][0:64, :], self.ONES32[0:1, 0:64], RLROW[:],
                        start=True, stop=True),
                        waits=[("EV", e_rl), ("VV", ev_ct)], tick="MM")
                    R["rl_free"] = mt_rb
                    v_fin = P.op("vector", lambda e, hj=hj, qsl=qsl: e.tensor_tensor(
                        CTX[hj * 64:hj * 64 + 64, ctxcol, qsl], CT32[:],
                        psC[hj][0:64, :], ALU.mult),
                        waits=[("MM", mt_rb), ("VV", ev_ct)], tick="VV")
                    R["psC_free"][hj] = ("VV", v_fin)
                    R["ct32_free"] = v_fin
            pend = fin
            next(feeder, None)
        if pend is not None:
            pend()
        for _ in feeder:
            pass
        R["kt_free"][par] = last_score_mm
        R["qt_free"][par] = last_score_mm
        R["v_free"][par] = last_ctx_mm
        R["wt_free"][par] = out["last_proj_mm"]
        self.ctx_ready = P.cnt["VV"]

    def emit_hg_set(self, seq, layer, src, chs, CTX, R):
        outs = [dict() for _ in chs]
        g = self.emit_hg_proj(seq, layer, src, chs[0], 0, R, outs[0])
        for _ in g:
            pass
        for i in range(len(chs)):
            if i + 1 < len(chs):
                feeder = self.emit_hg_proj(seq, layer, src, chs[i + 1],
                                           (i + 1) % 2, R, outs[i + 1])
            else:
                feeder = iter(())
            self.emit_hg_scores(seq, layer, i % 2, CTX,
                                chs[i] if layer == 0 else chs[i] % 4,
                                R, outs[i], feeder)

    def prep_wot(self, st, key, kclist, guard=()):
        P = self.P
        d = self.dram
        nk = len(kclist)
        WOT = P.sb_in(st, "wot", [128, nk, D], BF16)
        wnat = P.sb_in(st, "wnat2", [128, D], F32)
        ps_tr = [self.R["psP"][0], self.R["psP"][1]]
        tr_free = [0, 0]
        prev_use = 0
        for co in range(8):
            dma_t = P.dma("sync", wnat[:], d[key][co * 128:(co + 1) * 128, :],
                          waits=list(guard) + [("MM", prev_use)])
            for g in range(0, nk, 4):
                gn = min(4, nk - g)
                b = (g // 4) % 2
                ps = ps_tr[b]
                mt = 0
                for j in range(gn):
                    kc = kclist[g + j]
                    mt = P.op("tensor", lambda e, ps=ps, j=j, kc=kc: e.transpose(
                        ps[:, j * 128:(j + 1) * 128],
                        wnat[:, kc * 128:(kc + 1) * 128], self.ID32[:]),
                        waits=list(guard) + [dma_t, ("EV", tr_free[b])], tick="MM")
                prev_use = mt
                tr_free[b] = P.op("scalar", lambda e, ps=ps, g=g, gn=gn, co=co: e.activation(
                    WOT[:, g:g + gn, co * 128:(co + 1) * 128],
                    ps[:, 0:gn * 128].rearrange("p (c t) -> p c t", t=128), AF.Copy),
                    waits=[("MM", mt)], tick="EV")
        return WOT, max(tr_free)

    def out_proj_l0(self, seq, CTX, st):
        P = self.P
        guard = self.cnt_guard()
        WOT, wot_ev = self.prep_wot(st, "L0_attn_w_o", list(range(8)), guard)
        borow, bo_dep = self.load_row(st, "L0_attn_b_o", waits=guard)
        Y32 = P.sb_in(st, "y32", [128, NCH, 512], F32)
        T32 = P.sb_in(st, "t32o", [128, 512], F32)
        R = self.R
        lp = self.alloc_ln_views(st, R)
        if R["psS_free"][0] is not None:
            lp["stats_ev_prev"] = R["psS_free"][0][1]
        ps_o = [R["psC"][0], R["psC"][1]]
        pso_free = [0, 0]
        prev_x_add = 0
        prev_src_mm = 0
        prev_src_ev = 0
        for tb in range(NTB):
            first = guard if tb == 0 else []
            tsl = slice(tb * TB, (tb + 1) * TB)
            yvv = 0
            for oc in range(NCH):
                b = oc % 2
                ps = ps_o[b]
                for kc in range(NCH):
                    P.op("tensor", lambda e, ps=ps, kc=kc, oc=oc, tsl=tsl: e.matmul(
                        ps[:], WOT[:, kc, oc * 128:(oc + 1) * 128], CTX[:, kc, tsl],
                        start=(kc == 0), stop=False),
                        waits=first + [("EV", wot_ev),
                                       ("VV", max(pso_free[b], self.ctx_ready))],
                        tick="MM")
                mt = P.op("tensor", lambda e, ps=ps, oc=oc: e.matmul(
                    ps[:], borow[0:1, oc * 128:(oc + 1) * 128], self.ONES32[0:1, :],
                    start=False, stop=True), waits=[bo_dep], tick="MM")
                yvv = P.op("vector", lambda e, ps=ps, oc=oc, tsl=tsl: e.tensor_tensor(
                    Y32[:, oc, :], self.XN[:, oc, tsl], ps[:], ALU.add),
                    waits=first + [("MM", max(mt, prev_src_mm)),
                                   ("EV", prev_src_ev)], tick="VV")
                pso_free[b] = yvv
            r = self.emit_ln_tb(lp, lambda c: Y32[:, c, :], True,
                                first + [("VV", yvv)])
            prev_src_mm = r["src_mm"]
            prev_src_ev = r["src_ev"]
            apply_vv = 0
            for oc in range(NCH):
                vs = P.op("vector", lambda e, oc=oc: e.tensor_tensor(
                    T32[:], Y32[:, oc, :], lp["ps_mu"][:], ALU.subtract),
                    waits=first + [("MM", r["bmu"]), ("PP", prev_x_add)], tick="VV")
                vm = P.op("vector", lambda e: e.tensor_tensor(
                    T32[:], T32[:], lp["ps_rs"][:], ALU.mult),
                    waits=[("MM", r["brs"]), ("VV", vs)], tick="VV")
                apply_vv = vm
                vg = P.op("vector", lambda e, oc=oc: e.tensor_scalar(
                    T32[:], T32[:], self.bias_col("ln_g", oc),
                    self.bias_col("ln_b", oc), ALU.mult, ALU.add),
                    waits=[("VV", vm)], tick="VV")
                prev_x_add = P.op("gpsimd", lambda e, oc=oc, tsl=tsl: e.tensor_tensor(
                    self.X[:, oc, tsl], self.X[:, oc, tsl], T32[:], ALU.add),
                    waits=[("VV", vg)], tick="PP")
            lp["apply_prev"] = apply_vv

    def out_proj_l1(self, seq, CTX, half, st):
        P = self.P
        guard = self.cnt_guard()
        kclist = [half * 4 + j for j in range(4)]
        WOT, wot_ev = self.prep_wot(st, "L1_attn_out_proj_w", kclist, guard)
        borow, bo_dep = (None, None)
        if half == 0:
            borow, bo_dep = self.load_row(st, "L1_attn_out_proj_b", waits=guard)
        ps_o = [self.R["psC"][0], self.R["psC"][1]]
        pso_free = [0, 0]
        for tb in range(NTB):
            first = guard if tb == 0 else []
            tsl = slice(tb * TB, (tb + 1) * TB)
            for oc in range(NCH):
                b = (tb * NCH + oc) % 2
                ps = ps_o[b]
                mt = 0
                for j in range(4):
                    mt = P.op("tensor", lambda e, ps=ps, j=j, oc=oc, tsl=tsl: e.matmul(
                        ps[:], WOT[:, j, oc * 128:(oc + 1) * 128], CTX[:, j, tsl],
                        start=(j == 0), stop=(j == 3 and half == 1)),
                        waits=first + [("EV", wot_ev),
                                       ("VV", max(pso_free[b], self.ctx_ready))],
                        tick="MM")
                if half == 0:
                    mt = P.op("tensor", lambda e, ps=ps, oc=oc: e.matmul(
                        ps[:], borow[0:1, oc * 128:(oc + 1) * 128],
                        self.ONES32[0:1, :], start=False, stop=True),
                        waits=[bo_dep], tick="MM")
                pso_free[b] = P.op("vector", lambda e, ps=ps, oc=oc, tsl=tsl: e.scalar_tensor_tensor(
                    out=self.X[:, oc, tsl], in0=ps[:], scalar=1.0,
                    in1=self.X[:, oc, tsl], op0=ALU.mult, op1=ALU.add),
                    waits=[("MM", mt)], tick="VV")

    def lact_phase(self, seq):
        P = self.P
        d = self.dram
        st = P.scope()
        W1NAT = P.sb_in(st, "w1n", [128, 2, 256], BF16)
        W0T = P.sb_in(st, "w0t", [128, 2, 256], BF16)
        W2T = P.sb_in(st, "w2t", [128, 2, 256], BF16)
        W1T = P.sb_in(st, "w1t", [128, 2, 256], BF16)
        wsm = P.sb_in(st, "wsm", [128, 2, 256], F32)
        ps_w = P.ps_in(st, "psw", [128, 256], F32)
        B = dict(
            KVT=P.sb_in(st, "kvt", [128, 8, 256], BF16),
            HID=P.sb_in(st, "hid", [128, 8, 256], BF16),
            DG=P.sb_in(st, "dgb", [128, 8, 256], BF16),
            DP=P.sb_in(st, "dpb", [128, 8, 256], BF16),
            XXT=P.sb_in(st, "xxt", [128, 2, 8, 128], BF16),
            SGf=P.sb_in(st, "sgf", [128, 256], F32),
            Hf=P.sb_in(st, "hf", [128, 256], F32),
            Gf=P.sb_in(st, "gf", [128, 256], F32),
            DHf=P.sb_in(st, "dhf", [128, 256], F32),
            T1f=P.sb_in(st, "t1f", [128, 256], F32),
            DHPf=P.sb_in(st, "dhpf", [128, 256], F32),
            W0N=P.sb_in(st, "w0nb", [128, 2, 256], BF16),
            W2N=P.sb_in(st, "w2nb", [128, 2, 256], BF16),
            W1N=P.sb_in(st, "w1nb", [128, 2, 256], BF16),
            HQf=P.sb_in(st, "hqf", [128, 256], F32),
            XXb=P.sb_in(st, "xxb", [128, 256], BF16),
            ps_a=P.ps_in(st, "psa", [128, 256], F32),
            ps_b=P.ps_in(st, "psb", [128, 256], F32),
            ps_c=P.ps_in(st, "psc", [128, 256], F32),
            ps_t=P.ps_in(st, "pst", [128, 256], BF16),
            ps_u=P.ps_in(st, "psu", [128, 256], F32),
            ps_o=P.ps_in(st, "pso2", [128, 512], F32),
        )
        statics_ev = 0
        prev_wsm_mm = 0
        for h in range(4):
            mm_gate = P.cnt["MM"]
            w1nat_dep = P.dma("gpsimd", W1NAT[:],
                              d["L1_attn_w1"][h].rearrange("(c p) e -> p c e", p=128),
                              waits=[("MM", mm_gate)])
            for (dst, key) in ((W0T, "L1_attn_w0"), (W2T, "L1_attn_w2"),
                               (W1T, "L1_attn_w1")):
                dma_t = P.dma("sync", wsm[:],
                              d[key][h].rearrange("(c p) e -> p c e", p=128),
                              waits=[("MM", max(mm_gate, prev_wsm_mm))])
                for dc in range(2):
                    mt = 0
                    for er in range(2):
                        mt = P.op("tensor", lambda e, dc=dc, er=er: e.transpose(
                            ps_w[:, er * 128:(er + 1) * 128],
                            wsm[:, er, dc * 128:(dc + 1) * 128], self.ID32[:]),
                            waits=[dma_t, ("EV", statics_ev)], tick="MM")
                    prev_wsm_mm = mt
                    statics_ev = P.op("scalar", lambda e, dst=dst, dc=dc: e.activation(
                        dst[:, dc, :], ps_w[:], AF.Copy), waits=[("MM", mt)], tick="EV")
            for n in range(2):
                gate = [("EV", P.cnt["EV"]), ("VV", P.cnt["VV"])]
                self.lact_unit(seq, h, n, B, W1NAT, W0T, W2T, W1T,
                               [("EV", statics_ev), w1nat_dep] + gate)
        st.close()

    def lact_unit(self, seq, h, n, B, W1NAT, W0T, W2T, W1T, wdeps):
        P = self.P
        KVT, HID, DG, DP, XXT = B["KVT"], B["HID"], B["DG"], B["DP"], B["XXT"]
        SGf, Hf, Gf, DHf, T1f = B["SGf"], B["Hf"], B["Gf"], B["DHf"], B["T1f"]
        DHPf, W0N, W2N, W1N = B["DHPf"], B["W0N"], B["W2N"], B["W1N"]
        HQf, XXb = B["HQf"], B["XXb"]
        ps_a, ps_b, ps_c = B["ps_a"], B["ps_b"], B["ps_c"]
        ps_t, ps_u, ps_o = B["ps_t"], B["ps_u"], B["ps_o"]
        base_t = n * 1024
        ch0 = 2 * h

        kv_ev = 0
        prev = 0
        for mi in range(8):
            mm = 0
            for kc in range(2):
                mm = P.op("tensor", lambda e, mi=mi, kc=kc: e.transpose(
                    ps_t[:, kc * 128:(kc + 1) * 128],
                    self.XN2[:, ch0 + kc,
                             base_t + mi * 128: base_t + (mi + 1) * 128],
                    self.ID16[:]),
                    waits=wdeps + [("EV", prev)], tick="MM")
            prev = P.op("scalar", lambda e, mi=mi: e.activation(
                KVT[:, mi, :], ps_t[:], AF.Copy), waits=[("MM", mm)], tick="EV")
            kv_ev = prev

        pp_done = 0
        psa_vv = 0
        ev_free = [0, 0, 0]
        for mi in range(8):
            csl = slice(base_t + mi * 128, base_t + (mi + 1) * 128)
            mmg = 0
            for kc in range(2):
                mmg = P.op("tensor", lambda e, kc=kc, csl=csl: e.matmul(
                    ps_a[:], self.XN2[:, ch0 + kc, csl], W0T[:, kc, :],
                    start=(kc == 0), stop=(kc == 1)),
                    waits=wdeps + [("VV", max(psa_vv, pp_done))], tick="MM")
            ev_sg = P.op("scalar", lambda e: e.activation(
                SGf[:], ps_a[:], AF.Sigmoid), waits=[("MM", mmg)], tick="EV")
            ev_h = P.op("vector", lambda e: e.tensor_tensor(
                Hf[:], SGf[:], ps_a[:], ALU.mult),
                waits=[("EV", ev_sg)], tick="VV")
            psa_vv = ev_h
            mm2 = 0
            for kc in range(2):
                mm2 = P.op("tensor", lambda e, kc=kc, csl=csl: e.matmul(
                    ps_b[:], self.XN2[:, ch0 + kc, csl], W2T[:, kc, :],
                    start=(kc == 0), stop=(kc == 1)),
                    waits=wdeps + [("EV", ev_free[1]), ("VV", pp_done)], tick="MM")
            ev_g = P.op("scalar", lambda e: e.activation(
                Gf[:], ps_b[:], AF.Copy), waits=[("MM", mm2)], tick="EV")
            ev_free[1] = ev_g
            mm3 = 0
            for kc in range(2):
                mm3 = P.op("tensor", lambda e, kc=kc, csl=csl: e.matmul(
                    ps_c[:], self.XN2[:, ch0 + kc, csl], W1NAT[:, kc, :],
                    start=(kc == 0), stop=(kc == 1)),
                    waits=wdeps + [("EV", ev_free[2]), ("VV", pp_done)], tick="MM")
            ev_dh = P.op("scalar", lambda e: e.activation(
                DHf[:], ps_c[:], AF.Copy), waits=[("MM", mm3)], tick="EV")
            ev_free[2] = ev_dh
            p1 = P.op("vector", lambda e, mi=mi: e.tensor_tensor(
                HID[:, mi, :], Hf[:], Gf[:], ALU.mult),
                waits=[("VV", ev_h), ("EV", ev_g)], tick="VV")
            p2 = P.op("vector", lambda e: e.tensor_tensor(
                DHPf[:], DHf[:], Gf[:], ALU.mult),
                waits=[("EV", max(ev_dh, ev_g))], tick="VV")
            p3 = P.op("vector", lambda e, mi=mi: e.tensor_tensor(
                DG[:, mi, :], DHf[:], Hf[:], ALU.mult),
                waits=[("EV", ev_dh), ("VV", ev_h)], tick="VV")
            ev_t1 = P.op("scalar", lambda e: e.activation(
                T1f[:], SGf[:], AF.Identity, scale=-1.0, bias=1.0),
                waits=[("EV", ev_sg), ("VV", pp_done)], tick="EV")
            p4 = P.op("vector", lambda e: e.tensor_tensor(
                T1f[:], Hf[:], T1f[:], ALU.mult),
                waits=[("EV", ev_t1), ("VV", p3)], tick="VV")
            p5 = P.op("vector", lambda e: e.tensor_tensor(
                T1f[:], SGf[:], T1f[:], ALU.add), waits=[("VV", p4)], tick="VV")
            pp_done = P.op("vector", lambda e, mi=mi: e.tensor_tensor(
                DP[:, mi, :], DHPf[:], T1f[:], ALU.mult),
                waits=[("VV", max(p5, p2))], tick="VV")

        upd_ev = 0
        for (OUT, LHS, WSTAT) in ((W1N, HID, W1T), (W0N, None, W0T), (W2N, None, W2T)):
            RHS = DP if OUT is W0N else (DG if OUT is W2N else None)
            for m in range(2):
                mm = 0
                for mi in range(8):
                    if OUT is W1N:
                        la = HID[:, mi, m * 128:(m + 1) * 128]
                        ra = KVT[:, mi, :]
                    else:
                        la = KVT[:, mi, m * 128:(m + 1) * 128]
                        ra = RHS[:, mi, :]
                    mm = P.op("tensor", lambda e, la=la, ra=ra, mi=mi: e.matmul(
                        ps_u[:], la, ra, start=(mi == 0), stop=(mi == 7)),
                        waits=[("VV", max(pp_done, upd_ev)), ("EV", kv_ev)],
                        tick="MM")
                upd_ev = P.op("vector", lambda e, OUT=OUT, WSTAT=WSTAT, m=m: e.tensor_tensor(
                    OUT[:, m, :], WSTAT[:, m, :], ps_u[:], ALU.add),
                    waits=[("MM", mm)], tick="VV")

        xxt_ev = 0
        psa_free = ("VV", psa_vv)
        psb_free = ("EV", ev_free[1])
        prev_vxx = 0
        prev_mmx = 0
        for mi in range(8):
            csl = slice(base_t + mi * 128, base_t + (mi + 1) * 128)
            mmq = 0
            for kc in range(2):
                mmq = P.op("tensor", lambda e, kc=kc, csl=csl: e.matmul(
                    ps_a[:], self.XN2[:, ch0 + kc, csl], W0N[:, kc, :],
                    start=(kc == 0), stop=(kc == 1)),
                    waits=[("VV", upd_ev), psa_free], tick="MM")
            ev_sgq = P.op("scalar", lambda e: e.activation(
                HQf[:], ps_a[:], AF.Sigmoid),
                waits=[("MM", mmq)], tick="EV")
            ev_hq = P.op("vector", lambda e: e.tensor_tensor(
                HQf[:], HQf[:], ps_a[:], ALU.mult),
                waits=[("EV", ev_sgq), ("VV", prev_vxx)], tick="VV")
            psa_free = ("VV", ev_hq)
            mmq2 = 0
            for kc in range(2):
                mmq2 = P.op("tensor", lambda e, kc=kc, csl=csl: e.matmul(
                    ps_b[:], self.XN2[:, ch0 + kc, csl], W2N[:, kc, :],
                    start=(kc == 0), stop=(kc == 1)),
                    waits=[("VV", upd_ev), psb_free], tick="MM")
            v_xx = P.op("vector", lambda e: e.tensor_tensor(
                XXb[:], HQf[:], ps_b[:], ALU.mult),
                waits=[("VV", ev_hq), ("MM", max(mmq2, prev_mmx))], tick="VV")
            psb_free = ("VV", v_xx)
            prev_vxx = v_xx
            mmx = 0
            for ec in range(2):
                mmx = P.op("tensor", lambda e, ec=ec: e.transpose(
                    ps_t[:, ec * 128:(ec + 1) * 128],
                    XXb[:, ec * 128:(ec + 1) * 128], self.ID16[:]),
                    waits=[("VV", v_xx), ("EV", max(xxt_ev, kv_ev))], tick="MM")
            prev_mmx = mmx
            xxt_ev = P.op("scalar", lambda e, mi=mi: e.activation(
                XXT[:, :, mi, :], ps_t[:].rearrange("p (c t) -> p c t", t=128),
                AF.Copy), waits=[("MM", mmx)], tick="EV")

        prev_add = 0
        for dm in range(2):
            for ncb in range(2):
                mm = 0
                for ec in range(2):
                    mm = P.op("tensor", lambda e, dm=dm, ec=ec, ncb=ncb: e.matmul(
                        ps_o[:], W1N[:, ec, dm * 128:(dm + 1) * 128],
                        XXT[:, ec, ncb * 4:(ncb + 1) * 4, :].rearrange(
                            "p a b -> p (a b)"),
                        start=(ec == 0), stop=(ec == 1)),
                        waits=[("EV", xxt_ev), ("VV", max(upd_ev, prev_add))],
                        tick="MM")
                tsl = slice(base_t + ncb * 512, base_t + (ncb + 1) * 512)
                prev_add = P.op("vector", lambda e, dm=dm, tsl=tsl: e.scalar_tensor_tensor(
                    out=self.X[:, ch0 + dm, tsl], in0=ps_o[:], scalar=1.0,
                    in1=self.X[:, ch0 + dm, tsl], op0=ALU.mult, op1=ALU.add),
                    waits=[("MM", mm)], tick="VV")

    def emit_cbs(self, st, ei):
        P = self.P
        CBS = P.sb_in(st, "cbsx", [128, S], F32)
        stp = P.scope()
        psb = [P.ps_in(stp, "pscb", [128, 512], F32) for _ in range(2)]
        prev_ev = [0, 0]
        ev = 0
        for tb in range(NTB):
            ps = psb[tb % 2]
            mtb = 0
            for k in range(4):
                tt = tb * 4 + k
                mtb = P.op("tensor", lambda e, ps=ps, tt=tt, k=k: e.matmul(
                    ps[:, k * 128:(k + 1) * 128],
                    self.CW_TOK[:, tt, ei:ei + 1].broadcast_to((128, 128)),
                    self.ID32[:], start=True, stop=True),
                    waits=[("VV", self.cw_ready), ("EV", prev_ev[tb % 2])], tick="MM")
            ev = P.op("scalar", lambda e, ps=ps, tb=tb: e.activation(
                CBS[:, tb * TB:(tb + 1) * TB], ps[:], AF.Copy),
                waits=[("MM", mtb)], tick="EV")
            prev_ev[tb % 2] = ev
        self.P.flush("cbs")
        stp.close()
        return CBS, ev

    def stage_proj(self, src, dst, wkey, bname, act, W, pspair, psfree,
                   w_free=(), src_dep=()):
        P = self.P
        d = self.dram
        wdep = P.dma("gpsimd", W[:],
                     d[wkey].ap().rearrange("(c p) o -> p c o", p=128),
                     waits=list(w_free))
        last = 0
        last_mm = 0
        for tb in range(NTB):
            tsl = slice(tb * TB, (tb + 1) * TB)
            for oc in range(NCH):
                b = (tb * NCH + oc) % 2
                ps = pspair[b]
                mt = 0
                for kc in range(NCH):
                    mt = P.op("tensor", lambda e, ps=ps, kc=kc, oc=oc, tsl=tsl: e.matmul(
                        ps[:], W[:, kc, oc * 128:(oc + 1) * 128], src[:, kc, tsl],
                        start=(kc == 0), stop=(kc == NCH - 1)),
                        waits=[wdep, ("EV", psfree[b])] + list(src_dep),
                        tick="MM")
                ev = P.op("scalar", lambda e, ps=ps, oc=oc, tsl=tsl: e.activation(
                    dst[:, oc, tsl], ps[:], act, bias=self.bias_col(bname, oc)),
                    waits=[("MM", mt)], tick="EV")
                psfree[b] = ev
                last = ev
                last_mm = mt
        return last, last_mm

    def alloc_2mat(self, st):
        P = self.P
        return dict(
            H=[P.sb_in(st, "h2m", [128, 8, 512], BF16) for _ in range(2)],
            TF=P.sb_in(st, "tf2m", [128, 512], F32),
            W1b=[P.sb_in(st, "w1g", [128, NCH, 1024], BF16) for _ in range(2)],
            W2b=[P.sb_in(st, "w2g", [128, 8, 1024], BF16) for _ in range(2)],
            CBS=[P.sb_in(st, "cbsx", [128, S], F32) for _ in range(2)],
            ps1=[P.ps_in(st, "ps1", [128, 512], F32) for _ in range(2)],
            ps2=[P.ps_in(st, "ps2", [128, 512], F32) for _ in range(2)],
            pcb=[P.ps_in(st, "pscb2", [128, 512], F32) for _ in range(2)],
            ROW=P.sb_in(st, "b2row2", [1, 1024], BF16),
            row_free=0,
            w_free=[(None, None), (None, None)],
            **(lambda t: dict(CBSB=[t, t]))(P.sb_in(st, "cbsb", [1, S], BF16)),
            ps1_free=[0, 0], ps2_free=[0, 0],
            h_free=[0, 0], tf_free=0,
            cbs_free=[0, 0],
            pcb_free=[0, 0],
            gidx=0,
        )

    def emit_cbs2(self, B, ei, epar):
        P = self.P
        CBS = B["CBS"][epar]
        ev = 0
        for tb in range(NTB):
            ps = B["pcb"][tb % 2]
            mtb = 0
            for k in range(4):
                tt = tb * 4 + k
                mtb = P.op("tensor", lambda e, ps=ps, tt=tt, k=k, ei=ei: e.matmul(
                    ps[:, k * 128:(k + 1) * 128],
                    self.CW_TOK[:, tt, ei:ei + 1].broadcast_to((128, 128)),
                    self.ID32[:], start=True, stop=True),
                    waits=[("VV", max(self.cw_ready, B.get("gvv", 0))),
                           ("EV", B["pcb_free"][tb % 2]),
                           ("MM", B["cbs_free"][epar])], tick="MM")
            ev = P.op("scalar", lambda e, ps=ps, CBS=CBS, tb=tb: e.activation(
                CBS[:, tb * TB:(tb + 1) * TB], ps[:], AF.Copy),
                waits=[("MM", mtb)], tick="EV")
            B["pcb_free"][tb % 2] = ev
        if "CBSB" in B:
            ev = P.op("scalar", lambda e, CBS=CBS, epar=epar: e.activation(
                B["CBSB"][epar][0:1, :], CBS[0:1, :], AF.Copy),
                waits=[("EV", ev), ("MM", max(B["cbs_free"])),
                       ("VV", B.get("gvv", 0))], tick="EV")
        return CBS, ev

    def expert_2mat(self, seq, tag, B, w1src, w2src, I, act, b1pfx, b2row_key,
                    ei, strow):
        P = self.P
        epar = ei % 2
        CBS, cbs_ev = self.emit_cbs2(B, ei, epar)
        key, bsl = b2row_key
        b2src = self.dram[key].ap() if bsl is None else self.dram[key][bsl[1], :]
        b2row = B["ROW"]
        b2dep = P.dma("gpsimd", b2row[0:1, :], b2src,
                      waits=([("MM", B["row_free"])] if B["row_free"] else [])
                      + B.get("gates", []))
        CBSB = B["CBSB"][epar]
        NG = I // 1024
        wsrc = []
        for g in range(NG):
            wsrc.append((
                w1src[:, g * 1024:(g + 1) * 1024].rearrange("(c p) o -> p c o", p=128),
                w2src[g * 1024:(g + 1) * 1024, :].rearrange("(c p) o -> p c o", p=128),
            ))
        w_dep = [None, None]

        def load_group(g):
            gp = B["gidx"] % 2
            B["gidx"] += 1
            fr = B["w_free"][gp]
            gts = B.get("gates", [])
            w_dep[gp] = (
                P.dma("gpsimd", B["W1b"][gp][:], wsrc[g][0],
                      waits=([fr[0]] if fr[0] else []) + gts),
                P.dma("gpsimd", B["W2b"][gp][:], wsrc[g][1],
                      waits=([fr[1]] if fr[1] else []) + gts),
            )
            return gp
        gps = [load_group(0)]
        if NG > 1:
            gps.append(load_group(1))
        last_cbs_mm = B["cbs_free"][epar]
        for g in range(NG):
            gp = gps[g]
            s1_last = 0
            s2_last = 0
            for tb in range(NTB):
                tsl = slice(tb * TB, (tb + 1) * TB)
                hp = tb % 2
                pp_hs = [0] * 8
                for ic in range(8):
                    b = ic % 2
                    mt = 0
                    for kc in range(NCH):
                        mt = P.op("tensor", lambda e, b=b, kc=kc, ic=ic, gp=gp, tsl=tsl: e.matmul(
                            B["ps1"][b][:], B["W1b"][gp][:, kc, ic * 128:(ic + 1) * 128],
                            self.XN[:, kc, tsl], start=(kc == 0), stop=(kc == NCH - 1)),
                            waits=[w_dep[gp][0], ("EV", B["ps1_free"][b]),
                                   ("EV", self.xn_ev[tb])],
                            tick="MM")
                    s1_last = mt
                    ev = P.op("scalar", lambda e, b=b, g=g, ic=ic: e.activation(
                        B["TF"][:], B["ps1"][b][:], act,
                        bias=self.bias_col(b1pfx, g * 8 + ic)),
                        waits=[("MM", mt), ("VV", B["tf_free"])], tick="EV")
                    B["ps1_free"][b] = ev
                    pp_hs[ic] = P.op("vector", lambda e, hp=hp, ic=ic, tsl=tsl, CBS=CBS: e.tensor_tensor(
                        B["H"][hp][:, ic, :], B["TF"][:], CBS[:, tsl], ALU.mult),
                        waits=[("EV", max(ev, cbs_ev)), ("MM", B["h_free"][hp])],
                        tick="VV")
                    B["tf_free"] = pp_hs[ic]
                for oc in range(NCH):
                    b = oc % 2
                    mt = 0
                    for ic in range(8):
                        mt = P.op("tensor", lambda e, b=b, ic=ic, oc=oc, gp=gp, hp=hp, g=g: e.matmul(
                            B["ps2"][b][:], B["W2b"][gp][:, ic, oc * 128:(oc + 1) * 128],
                            B["H"][hp][:, ic, :], start=(ic == 0),
                            stop=(ic == 7 and g > 0)),
                            waits=[w_dep[gp][1],
                                   ("VV", max(B["ps2_free"][b], pp_hs[ic]))],
                            tick="MM")
                    if g == 0:
                        mt = P.op("tensor", lambda e, b=b, oc=oc, tsl=tsl, CBSB=CBSB: e.matmul(
                            B["ps2"][b][:], b2row[0:1, oc * 128:(oc + 1) * 128],
                            CBSB[0:1, tsl], start=False, stop=True),
                            waits=[b2dep, ("EV", cbs_ev)], tick="MM")
                        last_cbs_mm = mt
                    s2_last = mt
                    if oc == NCH - 1:
                        B["h_free"][hp] = mt
                    B["ps2_free"][b] = P.op("vector", lambda e, b=b, oc=oc, tsl=tsl: e.scalar_tensor_tensor(
                        out=self.X[:, oc, tsl], in0=B["ps2"][b][:], scalar=1.0,
                        in1=self.X[:, oc, tsl], op0=ALU.mult, op1=ALU.add),
                        waits=[("MM", mt)], tick="VV")
            B["w_free"][gp] = (("MM", s1_last), ("MM", s2_last))
            if g + 2 < NG:
                gps.append(load_group(g + 2))
        B["cbs_free"][epar] = max(last_cbs_mm, s2_last)
        B["row_free"] = last_cbs_mm

    def emit_e0(self, seq):
        P = self.P
        d = self.dram
        gmm, gev, gvv = P.cnt["MM"], P.cnt["EV"], P.cnt["VV"]
        gts = [("MM", gmm), ("EV", gev), ("VV", gvv)]
        st = P.scope()
        Bc = dict(CBS=[P.sb_in(st, "cbsx", [128, S], F32)],
                  pcb=[P.ps_in(st, "pscb", [128, 512], F32) for _ in range(2)],
                  pcb_free=[gev, gev], cbs_free=[gmm, gmm], gvv=gvv)
        CBS, cbs_ev = self.emit_cbs2(Bc, 0, 0)
        H = [P.sb_in(st, "h0", [128, 4, 512], BF16) for _ in range(2)]
        TF2 = P.sb_in(st, "tf0", [128, 2, 512], F32)
        W1b = [P.sb_in(st, "w1g0", [128, NCH, 512], BF16) for _ in range(2)]
        W3b = [P.sb_in(st, "w3g0", [128, NCH, 512], BF16) for _ in range(2)]
        W2b = [P.sb_in(st, "w2g0", [128, 4, 1024], BF16) for _ in range(2)]
        ps1 = [P.ps_in(st, "ps10", [128, 512], F32) for _ in range(2)]
        ps3 = [P.ps_in(st, "ps30", [128, 512], F32) for _ in range(2)]
        ps2 = [P.ps_in(st, "ps20", [128, 512], F32) for _ in range(2)]
        NG = 8
        IG = 512
        w_dep = [None, None]

        def load_group(g, waits):
            waits = list(waits) + gts
            gp = g % 2
            gsl = slice(g * IG, (g + 1) * IG)
            w_dep[gp] = (
                P.dma("gpsimd", W1b[gp][:],
                      d["L0_ffn_sw_w1"][:, gsl].rearrange("(c p) o -> p c o", p=128),
                      waits=waits),
                P.dma("gpsimd", W3b[gp][:],
                      d["L0_ffn_sw_w3"][:, gsl].rearrange("(c p) o -> p c o", p=128),
                      waits=waits),
                P.dma("gpsimd", W2b[gp][:],
                      d["L0_ffn_sw_w2"][gsl, :].rearrange("(c p) o -> p c o", p=128),
                      waits=waits),
            )
        load_group(0, [])
        load_group(1, [])
        ps1_free = [0, 0]
        ps3_free = [0, 0]
        ps2_free = [0, 0]
        h_free = [0, 0]
        tff = [0, 0]
        tfp = [0, 0]
        for g in range(NG):
            gp = g % 2
            s_last = 0
            for tb in range(NTB):
                tsl = slice(tb * TB, (tb + 1) * TB)
                hp = tb % 2
                pp_hs = [0] * 4
                for ic in range(4):
                    b = ic % 2
                    mt1 = 0
                    mt3 = 0
                    for kc in range(NCH):
                        mt1 = P.op("tensor", lambda e, b=b, kc=kc, ic=ic, gp=gp, tsl=tsl: e.matmul(
                            ps1[b][:], W1b[gp][:, kc, ic * 128:(ic + 1) * 128],
                            self.XN[:, kc, tsl], start=(kc == 0), stop=(kc == NCH - 1)),
                            waits=[w_dep[gp][0], ("EV", ps1_free[b])],
                            tick="MM")
                    for kc in range(NCH):
                        mt3 = P.op("tensor", lambda e, b=b, kc=kc, ic=ic, gp=gp, tsl=tsl: e.matmul(
                            ps3[b][:], W3b[gp][:, kc, ic * 128:(ic + 1) * 128],
                            self.XN[:, kc, tsl], start=(kc == 0), stop=(kc == NCH - 1)),
                            waits=[w_dep[gp][1], ("VV", ps3_free[b])],
                            tick="MM")
                    TF = TF2[:, b, :]
                    ev = P.op("scalar", lambda e, b=b, TF=TF: e.activation(
                        TF[:], ps1[b][:], AF.Silu),
                        waits=[("MM", mt1), ("VV", max(tff[b], tfp[b]))],
                        tick="EV")
                    ps1_free[b] = ev
                    vv = P.op("vector", lambda e, b=b, TF=TF: e.tensor_tensor(
                        TF[:], TF[:], ps3[b][:], ALU.mult),
                        waits=[("EV", ev), ("MM", mt3)], tick="VV")
                    ps3_free[b] = vv
                    tff[b] = vv
                    pp_hs[ic] = P.op("vector", lambda e, hp=hp, ic=ic, tsl=tsl, TF=TF: e.tensor_tensor(
                        H[hp][:, ic, :], TF[:], CBS[:, tsl], ALU.mult),
                        waits=[("VV", vv), ("EV", cbs_ev), ("MM", h_free[hp])],
                        tick="VV")
                    tfp[b] = pp_hs[ic]
                for oc in range(NCH):
                    b = oc % 2
                    mt = 0
                    for ic in range(4):
                        mt = P.op("tensor", lambda e, b=b, ic=ic, oc=oc, gp=gp, hp=hp: e.matmul(
                            ps2[b][:], W2b[gp][:, ic, oc * 128:(oc + 1) * 128],
                            H[hp][:, ic, :], start=(ic == 0), stop=(ic == 3)),
                            waits=[w_dep[gp][2], ("VV", max(ps2_free[b], pp_hs[ic]))],
                            tick="MM")
                    s_last = mt
                    if oc == NCH - 1:
                        h_free[hp] = mt
                    ps2_free[b] = P.op("vector", lambda e, b=b, oc=oc, tsl=tsl: e.scalar_tensor_tensor(
                        out=self.X[:, oc, tsl], in0=ps2[b][:], scalar=1.0,
                        in1=self.X[:, oc, tsl], op0=ALU.mult, op1=ALU.add),
                        waits=[("MM", mt)], tick="VV")
            if g + 2 < NG:
                load_group(g + 2, [("MM", s_last)])
        st.close()
        P.flush(f"l0ffn_{seq}")

    def emit_e1(self, seq):
        P = self.P
        d = self.dram
        gmm, gev, gvv = P.cnt["MM"], P.cnt["EV"], P.cnt["VV"]
        gts = [("MM", gmm), ("EV", gev), ("VV", gvv)]
        st = P.scope()
        Bc = dict(CBS=[P.sb_in(st, "cbsx", [128, S], F32)],
                  pcb=[P.ps_in(st, "pscb", [128, 512], F32) for _ in range(2)],
                  pcb_free=[gev, gev], cbs_free=[gmm, gmm], gvv=gvv)
        Bc["CBSB"] = [P.sb_in(st, "cbsb", [1, S], BF16)]
        Bc["CBS"] = Bc["CBS"] * 1
        CBS, cbs_ev = self.emit_cbs2(Bc, 1, 0)
        CBSB = Bc["CBSB"][0]
        c2row = P.sb_in(st, "c2rb", [1, 1024], BF16)
        c2dep = P.dma("gpsimd", c2row[0:1, :], self.dram["L0_ffn_me_c2b"].ap(),
                      waits=gts)
        BUF1 = P.sb_in(st, "e1b1", [128, NCH, S], BF16)
        SPW = P.sb_in(st, "spw", [128, NCH, D], BF16)
        sp_ps = [P.ps_in(st, "psp", [128, 512], F32) for _ in range(2)]
        spfree = [gev, gev]
        stb = P.scope()
        BUF2 = P.sb_in(stb, "e1b2", [128, NCH, S], BF16)
        d1, m1 = self.stage_proj(self.XN, BUF1, "L0_ffn_me_eq_w", "me_eq_b",
                                 AF.Identity, SPW, sp_ps, spfree,
                                 w_free=gts, src_dep=[("VV", gvv)])
        d2, m2 = self.stage_proj(BUF1, BUF2, "L0_ffn_me_wv", "me_bv",
                                 AF.Identity, SPW, sp_ps, spfree,
                                 w_free=[("MM", m1)], src_dep=[("EV", d1)])
        d3, m3 = self.stage_proj(BUF2, BUF1, "L0_ffn_me_wo", "me_bo",
                                 AF.Identity, SPW, sp_ps, spfree,
                                 w_free=[("MM", m2)], src_dep=[("EV", d2)])
        stb.close()
        U = P.sb_in(st, "e1u", [128, 8, 512], BF16)
        TF = P.sb_in(st, "e1tf", [128, 512], F32)
        C1 = P.sb_in(st, "c1h", [128, NCH, 1024], BF16)
        C2 = P.sb_in(st, "c2h", [128, 8, 1024], BF16)
        ps1 = [P.ps_in(st, "e1p1", [128, 512], F32) for _ in range(2)]
        ps2 = [P.ps_in(st, "e1p2", [128, 512], F32) for _ in range(2)]
        last2 = 0
        last1 = 0
        for g in range(2):
            gsl = slice(g * 1024, (g + 1) * 1024)
            wd1 = P.dma("gpsimd", C1[:],
                        d["L0_ffn_me_c1w"][:, gsl].rearrange("(c p) o -> p c o", p=128),
                        waits=[("EV", d3)] if g == 0 else [("MM", last1)])
            wd2 = P.dma("gpsimd", C2[:],
                        d["L0_ffn_me_c2w"][gsl, :].rearrange("(c p) o -> p c o", p=128),
                        waits=[("EV", d3)] if g == 0 else [("MM", last2)])
            ps1_free = [0, 0]
            ps2_free = [0, 0]
            u_free = 0
            tf_free = 0
            for tb in range(NTB):
                tsl = slice(tb * TB, (tb + 1) * TB)
                pp_us = [0] * 8
                for ic in range(8):
                    b = ic % 2
                    mt = 0
                    for kc in range(NCH):
                        mt = P.op("tensor", lambda e, b=b, kc=kc, ic=ic, tsl=tsl: e.matmul(
                            ps1[b][:], C1[:, kc, ic * 128:(ic + 1) * 128],
                            BUF1[:, kc, tsl], start=(kc == 0), stop=(kc == NCH - 1)),
                            waits=[wd1, ("EV", max(ps1_free[b], d3)),
                                   ("MM", u_free)], tick="MM")
                    last1 = mt
                    ev = P.op("scalar", lambda e, b=b, g=g, ic=ic: e.activation(
                        TF[:], ps1[b][:], AF.Gelu,
                        bias=self.bias_col("me_c1b", g * 8 + ic)),
                        waits=[("MM", mt), ("VV", tf_free)], tick="EV")
                    ps1_free[b] = ev
                    pp_us[ic] = P.op("vector", lambda e, ic=ic, tsl=tsl: e.tensor_tensor(
                        U[:, ic, :], TF[:], CBS[:, tsl], ALU.mult),
                        waits=[("EV", max(ev, cbs_ev))], tick="VV")
                    tf_free = pp_us[ic]
                for oc in range(NCH):
                    b = oc % 2
                    mt = 0
                    for ic in range(8):
                        mt = P.op("tensor", lambda e, b=b, ic=ic, oc=oc, g=g: e.matmul(
                            ps2[b][:], C2[:, ic, oc * 128:(oc + 1) * 128],
                            U[:, ic, :], start=(ic == 0), stop=(ic == 7 and g == 1)),
                            waits=[wd2, ("VV", max(ps2_free[b], pp_us[ic]))], tick="MM")
                    if g == 0:
                        mt = P.op("tensor", lambda e, b=b, oc=oc, tsl=tsl: e.matmul(
                            ps2[b][:], c2row[0:1, oc * 128:(oc + 1) * 128],
                            CBSB[0:1, tsl], start=False, stop=True),
                            waits=[c2dep, ("EV", cbs_ev)], tick="MM")
                    last2 = mt
                    if oc == NCH - 1:
                        u_free = mt
                    ps2_free[b] = P.op("vector", lambda e, b=b, oc=oc, tsl=tsl: e.scalar_tensor_tensor(
                        out=self.X[:, oc, tsl], in0=ps2[b][:], scalar=1.0,
                        in1=self.X[:, oc, tsl], op0=ALU.mult, op1=ALU.add),
                        waits=[("MM", mt)], tick="VV")
        st.close()

    def emit_e2(self, seq):
        P = self.P
        d = self.dram
        st = P.scope()
        stp = P.scope()
        Bc = dict(CBS=[P.sb_in(st, "cbsx", [128, S], F32)],
                  CBSB=[P.sb_in(st, "cbsb", [1, S], BF16)],
                  pcb=[P.ps_in(stp, "pscb", [128, 512], F32) for _ in range(2)],
                  pcb_free=[0, 0], cbs_free=[0, 0])
        CBS, cbs_ev = self.emit_cbs2(Bc, 2, 0)
        stp.close()
        CBSB = Bc["CBSB"][0]
        genrow = P.sb_in(st, "genrb", [1, 1024], BF16)
        gendep = P.dma("gpsimd", genrow[0:1, :], d["L0_ffn_ce_gen_b"].ap())
        borow = P.sb_in(st, "borb", [1, 1024], BF16)
        bodep = P.dma("gpsimd", borow[0:1, :], d["L0_ffn_ce_bo"].ap())
        HS = S // 2
        BUF1 = P.sb_in(st, "e2b1", [128, NCH, HS], BF16)
        BUF2 = P.sb_in(st, "e2b2", [128, NCH, HS], BF16)
        WX = P.sb_in(st, "e2wx", [128, 16, 1024], BF16)
        Wp = WX[:, 0:8, :]
        F2v = WX[:, 8:16, :]
        Y32 = P.sb_in(st, "e2y", [128, NCH, 256], F32)
        T32P = P.sb_in(st, "e2t", [128, 2, 256], F32)
        U = P.sb_in(st, "e2u", [128, 8, 256], BF16)
        lp = self.alloc_ln(st, 256)
        lp["sq_dve"] = True
        psA = [P.ps_in(st, "e2pa", [128, 512], F32) for _ in range(2)]
        psB = [P.ps_in(st, "e2pb", [128, 512], F32) for _ in range(2)]
        lp["stats_ev_prev"] = cbs_ev
        pAf = [("EV", cbs_ev), ("EV", cbs_ev)]
        pBf = [("EV", cbs_ev), ("EV", cbs_ev)]
        ctx = dict(wx_a=0, wx_b=0, u_free=0, y32_mm=0, y32_ev=0)

        def proj_half(src, srcoff, dst, wkey, bname, src_dep=()):
            wdep = P.dma("gpsimd", Wp,
                         d[wkey].ap().rearrange("(c p) o -> p c o", p=128),
                         waits=[("MM", ctx["wx_a"])] if ctx["wx_a"] else [])
            lastv = 0
            for tb in range(HS // 512):
                ssl = slice(srcoff + tb * 512, srcoff + (tb + 1) * 512)
                dsl = slice(tb * 512, (tb + 1) * 512)
                for oc in range(NCH):
                    b = (tb * NCH + oc) % 2
                    ps = psA[b]
                    mt = 0
                    for kc in range(NCH):
                        mt = P.op("tensor", lambda e, ps=ps, kc=kc, oc=oc, ssl=ssl: e.matmul(
                            ps[:], Wp[:, kc, oc * 128:(oc + 1) * 128],
                            src[:, kc, ssl],
                            start=(kc == 0), stop=(kc == NCH - 1)),
                            waits=[wdep, pAf[b]] + list(src_dep), tick="MM")
                    ev = P.op("scalar", lambda e, ps=ps, oc=oc, dsl=dsl: e.activation(
                        dst[:, oc, dsl], ps[:], AF.Identity,
                        bias=self.bias_col(bname, oc)),
                        waits=[("MM", mt)], tick="EV")
                    pAf[b] = ("EV", ev)
                    lastv = ev
                    ctx["wx_a"] = mt
            return lastv

        for hseq in range(2):
            off = hseq * HS
            s1 = proj_half(self.XN, off, BUF1, "L0_ffn_ce_syn_w", "ce_syn_b")
            s2 = proj_half(BUF1, 0, BUF2, "L0_ffn_ce_wv", "ce_bv",
                           src_dep=[("EV", s1)])
            wdep = P.dma("gpsimd", Wp,
                         d["L0_ffn_ce_wo"].ap().rearrange("(c p) o -> p c o", p=128),
                         waits=[("MM", ctx["wx_a"])])
            h2_deps = [("EV", 0)] * 4
            prev_src_mm = ctx["y32_mm"]
            prev_src_ev = ctx["y32_ev"]
            prev_src_vv = ctx.get("y32_vv", 0)
            t32f = [("VV", 0), ("VV", 0)]
            for tb in range(4):
                tsl = slice(tb * 256, (tb + 1) * 256)
                yvv = 0
                for oc in range(NCH):
                    b = oc % 2
                    ps = psB[b][:, 0:256]
                    mt = 0
                    for kc in range(NCH):
                        mt = P.op("tensor", lambda e, ps=ps, kc=kc, oc=oc, tsl=tsl: e.matmul(
                            ps[:], Wp[:, kc, oc * 128:(oc + 1) * 128],
                            BUF2[:, kc, tsl],
                            start=(kc == 0), stop=False),
                            waits=[wdep, pBf[b], ("EV", s2)],
                            tick="MM")
                    mt = P.op("tensor", lambda e, ps=ps, oc=oc: e.matmul(
                        ps[:], borow[0:1, oc * 128:(oc + 1) * 128],
                        self.ONESR16[0:1, 0:256], start=False, stop=True),
                        waits=[bodep], tick="MM")
                    ctx["wx_a"] = mt
                    yvv = P.op("vector", lambda e, ps=ps, oc=oc, tsl=tsl: e.tensor_tensor(
                        Y32[:, oc, :], BUF1[:, oc, tsl], ps[:], ALU.add),
                        waits=[("MM", max(mt, prev_src_mm)),
                               ("EV", prev_src_ev), ("VV", prev_src_vv)],
                        tick="VV")
                    pBf[b] = ("VV", yvv)
                r = self.emit_ln_tb(lp, lambda c: Y32[:, c, :], True, [("VV", yvv)])
                prev_src_mm = r["src_mm"]
                prev_src_ev = r["src_ev"]
                prev_src_vv = r["src_vv"]
                apply_vv = 0
                for oc in range(NCH):
                    b = oc % 2
                    T32 = T32P[:, b, :]
                    vs = P.op("vector", lambda e, oc=oc, T32=T32: e.tensor_tensor(
                        T32[:], Y32[:, oc, :], lp["ps_mu"][:], ALU.subtract),
                        waits=[("MM", r["bmu"]), t32f[b]], tick="VV")
                    vm = P.op("vector", lambda e, T32=T32: e.tensor_tensor(
                        T32[:], T32[:], lp["ps_rs"][:], ALU.mult),
                        waits=[("MM", r["brs"]), ("VV", vs)], tick="VV")
                    apply_vv = vm
                    h2_ev = P.op("scalar", lambda e, oc=oc, tsl=tsl, T32=T32: e.activation(
                        BUF2[:, oc, tsl], T32[:], AF.Identity,
                        scale=self.bias_col("ce_n1g", oc),
                        bias=self.bias_col("ce_n1b", oc)),
                        waits=[("VV", vm)], tick="EV")
                    h2_deps[tb] = ("EV", h2_ev)
                    t32f[b] = ("EV", h2_ev)
                lp["apply_prev"] = apply_vv
            wo_yvv = yvv
            buf1_ev = 0
            out_pp = 0
            out_pp_tb = [0] * 4
            for g in range(2):
                gsl = slice(g * 1024, (g + 1) * 1024)
                wd1 = P.dma("gpsimd", Wp,
                            d["L0_ffn_ce_f1w"][:, gsl].rearrange("(c p) o -> p c o", p=128),
                            waits=[("MM", ctx["wx_a"])])
                wd2 = P.dma("gpsimd", F2v,
                            d["L0_ffn_ce_f2w"][gsl, :].rearrange("(c p) o -> p c o", p=128),
                            waits=[("MM", ctx["wx_b"])] if ctx["wx_b"] else [])
                last_v2 = 0
                for tb in range(4):
                    tsl = slice(tb * 256, (tb + 1) * 256)
                    ev_us = [0] * 8
                    mt1 = 0
                    for ic in range(8):
                        b = ic % 2
                        mt = 0
                        for kc in range(NCH):
                            mt = P.op("tensor", lambda e, b=b, kc=kc, ic=ic, tsl=tsl: e.matmul(
                                psA[b][:, 0:256], Wp[:, kc, ic * 128:(ic + 1) * 128],
                                BUF2[:, kc, tsl], start=(kc == 0), stop=(kc == NCH - 1)),
                                waits=[wd1, pAf[b], ("MM", ctx["u_free"]),
                                       h2_deps[tb]], tick="MM")
                        mt1 = mt
                        ev_us[ic] = P.op("scalar", lambda e, b=b, g=g, ic=ic: e.activation(
                            U[:, ic, :], psA[b][:, 0:256], AF.Relu,
                            bias=self.bias_col("ce_f1b", g * 8 + ic)),
                            waits=[("MM", mt)], tick="EV")
                        pAf[b] = ("EV", ev_us[ic])
                    vvs = []
                    mt2 = 0
                    for oc in range(NCH):
                        b = oc % 2
                        mt = 0
                        for ic in range(8):
                            mt = P.op("tensor", lambda e, b=b, ic=ic, oc=oc: e.matmul(
                                psB[b][:, 0:256], F2v[:, ic, oc * 128:(oc + 1) * 128],
                                U[:, ic, :], start=(ic == 0), stop=(ic == 7)),
                                waits=[wd2, ("EV", ev_us[ic]), pBf[b],
                                       ("VV", out_pp)], tick="MM")
                        mt2 = mt
                        if oc == NCH - 1:
                            ctx["u_free"] = mt
                        if g == 0:
                            buf1_ev = P.op("scalar", lambda e, b=b, oc=oc, tsl=tsl: e.activation(
                                BUF1[:, oc, tsl], psB[b][:, 0:256], AF.Identity,
                                bias=self.bias_col("ce_f2b", oc)),
                                waits=[("MM", mt), ("VV", wo_yvv)], tick="EV")
                            pBf[b] = ("EV", buf1_ev)
                        else:
                            T32 = T32P[:, b, :]
                            v1 = P.op("vector", lambda e, b=b, oc=oc, tsl=tsl, T32=T32: e.scalar_tensor_tensor(
                                out=T32[:], in0=psB[b][:, 0:256], scalar=1.0,
                                in1=BUF1[:, oc, tsl], op0=ALU.mult, op1=ALU.add),
                                waits=[("MM", max(mt, prev_src_mm)),
                                       ("EV", max(prev_src_ev, buf1_ev)),
                                       ("VV", max(out_pp, last_v2, prev_src_vv)),
                                       t32f[b]],
                                tick="VV")
                            v2 = P.op("vector", lambda e, oc=oc, tsl=tsl, T32=T32: e.tensor_tensor(
                                Y32[:, oc, :], BUF2[:, oc, tsl], T32[:], ALU.add),
                                waits=[("VV", v1)], tick="VV")
                            t32f[b] = ("VV", v2)
                            last_v2 = v2
                            pBf[b] = ("VV", v2)
                            vvs.append(v2)
                    if g == 1:
                        r = self.emit_ln_tb(lp, lambda c: Y32[:, c, :], True,
                                            [("VV", vvs[-1] if vvs else 0)])
                        prev_src_mm = r["src_mm"]
                        prev_src_ev = r["src_ev"]
                        prev_src_vv = r["src_vv"]
                        apply_vv = 0
                        for oc in range(NCH):
                            b2 = oc % 2
                            T32 = T32P[:, b2, :]
                            vs = P.op("vector", lambda e, oc=oc, T32=T32: e.tensor_tensor(
                                T32[:], Y32[:, oc, :], lp["ps_mu"][:], ALU.subtract),
                                waits=[("MM", r["bmu"]), ("VV", out_pp), t32f[b2]],
                                tick="VV")
                            vm = P.op("vector", lambda e, T32=T32: e.tensor_tensor(
                                T32[:], T32[:], lp["ps_rs"][:], ALU.mult),
                                waits=[("MM", r["brs"]), ("VV", vs)], tick="VV")
                            apply_vv = vm
                            ev3 = P.op("scalar", lambda e, oc=oc, T32=T32: e.activation(
                                T32[:], T32[:], AF.Identity,
                                scale=self.bias_col("ce_n2g", oc),
                                bias=self.bias_col("ce_n2b", oc)),
                                waits=[("VV", vm)], tick="EV")
                            out_pp = P.op("vector", lambda e, oc=oc, tsl=tsl, off=off, T32=T32: e.tensor_tensor(
                                BUF1[:, oc, tsl], T32[:],
                                CBS[:, off + tsl.start:off + tsl.stop], ALU.mult),
                                waits=[("EV", max(ev3, cbs_ev))], tick="VV")
                            t32f[b2] = ("VV", out_pp)
                            out_pp_tb[tb] = out_pp
                        lp["apply_prev"] = apply_vv
                ctx["wx_a"] = mt1
                ctx["wx_b"] = mt2
            ctx["y32_mm"] = prev_src_mm
            ctx["y32_ev"] = prev_src_ev
            ctx["y32_vv"] = prev_src_vv
            wdep = P.dma("gpsimd", Wp,
                         d["L0_ffn_ce_gen_w"].ap().rearrange("(c p) o -> p c o", p=128),
                         waits=[("MM", ctx["wx_a"])])
            for tb in range(HS // 512):
                lsl = slice(tb * 512, (tb + 1) * 512)
                xsl = slice(off + tb * 512, off + (tb + 1) * 512)
                for oc in range(NCH):
                    b = (tb * NCH + oc) % 2
                    ps = psA[b]
                    mt = 0
                    for kc in range(NCH):
                        mt = P.op("tensor", lambda e, ps=ps, kc=kc, oc=oc, lsl=lsl: e.matmul(
                            ps[:], Wp[:, kc, oc * 128:(oc + 1) * 128], BUF1[:, kc, lsl],
                            start=(kc == 0), stop=False),
                            waits=[wdep, pAf[b],
                                   ("VV", max(out_pp_tb[2 * tb], out_pp_tb[2 * tb + 1]))],
                            tick="MM")
                    mt = P.op("tensor", lambda e, ps=ps, oc=oc, xsl=xsl: e.matmul(
                        ps[:], genrow[0:1, oc * 128:(oc + 1) * 128], CBSB[0:1, xsl],
                        start=False, stop=True),
                        waits=[gendep, ("EV", cbs_ev)], tick="MM")
                    ctx["wx_a"] = mt
                    stt = P.op("vector", lambda e, ps=ps, oc=oc, xsl=xsl: e.scalar_tensor_tensor(
                        out=self.X[:, oc, xsl], in0=ps[:], scalar=1.0,
                        in1=self.X[:, oc, xsl], op0=ALU.mult, op1=ALU.add),
                        waits=[("MM", mt)], tick="VV")
                    pAf[b] = ("VV", stt)
        st.close()

    def ffn_phase(self, seq, layer):
        P = self.P
        d = self.dram
        if layer == 0:
            sel = self.stage.split(":")[1] if ":" in self.stage else "e0e1e2e3"
            if "e2" in sel:
                self.emit_e2(seq)
            if "e1" in sel:
                self.emit_e1(seq)
            if "e3" in sel:
                st = P.scope()
                B = self.alloc_2mat(st)
                B["gates"] = [("MM", P.cnt["MM"]), ("EV", P.cnt["EV"]),
                              ("VV", P.cnt["VV"])]
                B["pcb_free"] = [P.cnt["EV"]] * 2
                B["cbs_free"] = [P.cnt["MM"]] * 2
                B["gvv"] = P.cnt["VV"]
                self.expert_2mat(seq, "e3", B, d["L0_ffn_ml_w1"].ap(),
                                 d["L0_ffn_ml_w2"].ap(), 4096, AF.Gelu, "ml_b1",
                                 ("L0_ffn_ml_b2", None), 3, st)
                st.close()
            if "e0" in sel:
                self.emit_e0(seq)
        else:
            st = P.scope()
            B = self.alloc_2mat(st)
            B["gates"] = [("MM", P.cnt["MM"]), ("EV", P.cnt["EV"]),
                          ("VV", P.cnt["VV"]), ("PP", P.cnt["PP"])]
            B["pcb_free"] = [P.cnt["EV"]] * 2
            B["cbs_free"] = [P.cnt["MM"]] * 2
            B["gvv"] = P.cnt["VV"]
            for e in range(8):
                self.expert_2mat(seq, f"x{e}", B, d["L1_ffn_W1"][e],
                                 d["L1_ffn_W2"][e], 2048, AF.Relu, f"l1_b1_{e}",
                                 ("L1_ffn_b2", ("row", e)), e, st)
            P.flush(f"l1ffn_{seq}")
            st.close()


def kernel(**inputs):
    x = np.ascontiguousarray(inputs["L0_attn_x"])
    nc = build_nc("full")
    in_maps = []
    for core in range(8):
        m = {"x": x[2 * core:2 * core + 2]}
        for name in INPUT_SPECS:
            if name == "L0_attn_x":
                continue
            m[name] = np.ascontiguousarray(inputs[name])
        in_maps.append(m)
    res = run_bass_kernel_spmd(nc, in_maps, core_ids=list(range(8)))
    return np.concatenate([r["out"] for r in res.results], axis=0)


# version 25 (final)
# speedup vs baseline: 1.0807x; 1.0416x over previous
import contextlib
import numpy as np
import concourse.bass as bass
import concourse.mybir as mybir
from concourse.bass_utils import run_bass_kernel_spmd

F32 = mybir.dt.float32
BF16 = mybir.dt.bfloat16
AF = mybir.ActivationFunctionType
ALU = mybir.AluOpType
AXL = mybir.AxisListType

D = 1024
S = 2048
NCH = 8
NSEQ = 2
TB = 512
NTB = S // TB
NTT = S // 128
EPS = 1e-5

ENGINES = ("tensor", "scalar", "vector", "gpsimd", "sync")


def psP_w(R, i):
    d = R["psP_free"][i]
    return d if d is not None else ("MM", 0)





class Op:
    __slots__ = ("eng", "fn", "waits", "tick")

    def __init__(self, eng, fn, waits, tick):
        self.eng = eng
        self.fn = fn
        self.waits = waits
        self.tick = tick


class Prog:

    def __init__(self, nc):
        self.nc = nc
        self.ops = []
        self.sems = {}
        self.cnt = {}
        self.stack = contextlib.ExitStack()
        self.uid = 0
        for name in ("MM", "EV", "VV", "PP",
                     "WD0", "WD1", "WD2", "WD3",
                     "XD0", "XD1", "XD2", "XD3"):
            self.sems[name] = self.stack.enter_context(nc.semaphore(name))
            self.cnt[name] = 0
        self.rot = {"gpsimd": 0, "sync": 0}
        self.pool = {"gpsimd": ["WD0", "WD1", "WD2", "WD3"],
                     "sync": ["XD0", "XD1", "XD2", "XD3"]}

    def fresh(self, tag):
        self.uid += 1
        return f"{tag}_{self.uid}"

    def sb(self, tag, shape, dtype):
        return self.stack.enter_context(self.nc.sbuf_tensor(self.fresh(tag), shape, dtype))

    def scope(self):
        return contextlib.ExitStack()

    def sb_in(self, st, tag, shape, dtype):
        return st.enter_context(self.nc.sbuf_tensor(self.fresh(tag), shape, dtype))

    def ps_in(self, st, tag, shape, dtype=F32):
        return st.enter_context(self.nc.psum_tensor(self.fresh(tag), shape, dtype))

    def op(self, eng, fn, waits=(), tick=None):
        w = [t for t in waits if isinstance(t, tuple) and t[1] and t[1] > 0]
        t = None
        val = None
        if tick is not None:
            if isinstance(tick, str):
                tick = (tick, 1)
            s, dd = tick
            self.cnt[s] += dd
            val = self.cnt[s]
            t = (s, dd)
        self.ops.append(Op(eng, fn, w, t))
        return val

    def dma(self, queue_eng, out, in_, waits=()):
        sem = self.pool[queue_eng][self.rot[queue_eng] % 4]
        self.rot[queue_eng] += 1
        w = list(waits) + [(sem, self.cnt[sem])]

        def fn(eng, out=out, in_=in_):
            return eng.dma_start(out=out, in_=in_)
        v = self.op(queue_eng, fn, waits=w, tick=(sem, 16))
        return (sem, v)

    def qdeps(self, queue):
        return [(s, self.cnt[s]) for s in self.pool[queue]]

    def flush(self, name=None):
        ops = self.ops
        self.ops = []
        if not ops:
            return
        nc = self.nc
        sems = self.sems
        with nc.Block(self.fresh(name or "blk")) as blk:
            for ename in ENGINES:
                mine = [o for o in ops if o.eng == ename]
                if not mine:
                    continue
                dec = getattr(blk, ename)

                def body(eng, mine=mine):
                    last_wait = {}
                    for o in mine:
                        for (s, v) in o.waits:
                            if last_wait.get(s, -1) >= v:
                                continue
                            last_wait[s] = v
                            eng.wait_ge(sems[s], v)
                        instr = o.fn(eng)
                        if o.tick is not None:
                            s, dd = o.tick
                            instr.then_inc(sems[s], dd)
                dec(body)

    def close(self):
        self.stack.close()


INPUT_SPECS = {
    "L0_attn_x": (16, 2048, 1024),
    "L0_attn_w_q": (1024, 1024), "L0_attn_b_q": (1024,),
    "L0_attn_w_k": (1024, 1024), "L0_attn_b_k": (1024,),
    "L0_attn_w_v": (1024, 1024), "L0_attn_b_v": (1024,),
    "L0_attn_w_o": (1024, 1024), "L0_attn_b_o": (1024,),
    "L0_attn_ln_gamma": (1024,), "L0_attn_ln_beta": (1024,),
    "L0_ffn_router_w": (1024, 4), "L0_ffn_router_b": (4,),
    "L0_ffn_load_balancer": (4,),
    "L0_ffn_sw_w1": (1024, 4096), "L0_ffn_sw_w3": (1024, 4096),
    "L0_ffn_sw_w2": (4096, 1024),
    "L0_ffn_me_eq_w": (1024, 1024), "L0_ffn_me_eq_b": (1024,),
    "L0_ffn_me_wv": (1024, 1024), "L0_ffn_me_bv": (1024,),
    "L0_ffn_me_wo": (1024, 1024), "L0_ffn_me_bo": (1024,),
    "L0_ffn_me_c1w": (1024, 2048), "L0_ffn_me_c1b": (2048,),
    "L0_ffn_me_c2w": (2048, 1024), "L0_ffn_me_c2b": (1024,),
    "L0_ffn_ce_syn_w": (1024, 1024), "L0_ffn_ce_syn_b": (1024,),
    "L0_ffn_ce_wv": (1024, 1024), "L0_ffn_ce_bv": (1024,),
    "L0_ffn_ce_wo": (1024, 1024), "L0_ffn_ce_bo": (1024,),
    "L0_ffn_ce_n1g": (1024,), "L0_ffn_ce_n1b": (1024,),
    "L0_ffn_ce_f1w": (1024, 2048), "L0_ffn_ce_f1b": (2048,),
    "L0_ffn_ce_f2w": (2048, 1024), "L0_ffn_ce_f2b": (1024,),
    "L0_ffn_ce_n2g": (1024,), "L0_ffn_ce_n2b": (1024,),
    "L0_ffn_ce_gen_w": (1024, 1024), "L0_ffn_ce_gen_b": (1024,),
    "L0_ffn_ml_w1": (1024, 4096), "L0_ffn_ml_b1": (4096,),
    "L0_ffn_ml_w2": (4096, 1024), "L0_ffn_ml_b2": (1024,),
    "L1_attn_ln_w": (1024,), "L1_attn_ln_b": (1024,),
    "L1_attn_in_proj_w": (3072, 1024), "L1_attn_in_proj_b": (3072,),
    "L1_attn_out_proj_w": (1024, 1024), "L1_attn_out_proj_b": (1024,),
    "L1_attn_w0": (4, 256, 256), "L1_attn_w1": (4, 256, 256),
    "L1_attn_w2": (4, 256, 256),
    "L1_ffn_Wr": (1024, 8), "L1_ffn_br": (8,),
    "L1_ffn_W1": (8, 1024, 2048), "L1_ffn_b1": (8, 2048),
    "L1_ffn_W2": (8, 2048, 1024), "L1_ffn_b2": (8, 1024),
}

BT_ROWS = [
    ("b_q8", "L0_attn_b_q", None, 0.125),
    ("b_k", "L0_attn_b_k", None, None),
    ("ln_g", "L0_attn_ln_gamma", None, None),
    ("ln_b", "L0_attn_ln_beta", None, None),
    ("me_eq_b", "L0_ffn_me_eq_b", None, None),
    ("me_bv", "L0_ffn_me_bv", None, None),
    ("me_bo", "L0_ffn_me_bo", None, None),
    ("ce_syn_b", "L0_ffn_ce_syn_b", None, None),
    ("ce_bv", "L0_ffn_ce_bv", None, None),
    ("ce_bo", "L0_ffn_ce_bo", None, None),
    ("ce_n1g", "L0_ffn_ce_n1g", None, None),
    ("ce_n1b", "L0_ffn_ce_n1b", None, None),
    ("ce_n2g", "L0_ffn_ce_n2g", None, None),
    ("ce_n2b", "L0_ffn_ce_n2b", None, None),
    ("ce_f2b", "L0_ffn_ce_f2b", None, None),
    ("l1_ln_w", "L1_attn_ln_w", None, None),
    ("l1_ln_b", "L1_attn_ln_b", None, None),
    ("me_c1b", "L0_ffn_me_c1b", None, None),
    ("ce_f1b", "L0_ffn_ce_f1b", None, None),
    ("ml_b1", "L0_ffn_ml_b1", None, None),
    ("inp_b8", "L1_attn_in_proj_b", (0, 1024), 0.125),
    ("inp_bk", "L1_attn_in_proj_b", (1024, 2048), None),
] + [(f"l1_b1_{e}", "L1_ffn_b1", ("row", e), None) for e in range(8)]
NROWS = len(BT_ROWS)


def build_nc(stage="full"):
    nc = bass.Bass("TRN2", target_bir_lowering=False)
    P = Prog(nc)
    dram = {}
    for name, shape in INPUT_SPECS.items():
        if name == "L0_attn_x":
            dram["x"] = nc.dram_tensor("x", [NSEQ, S, D], F32, kind="ExternalInput")
        else:
            dram[name] = nc.dram_tensor(name, list(shape), F32, kind="ExternalInput")
    out = nc.dram_tensor("out", [NSEQ, S, D], F32, kind="ExternalOutput")
    k = Kern(P, dram, out, stage)
    k.build()
    P.close()
    return nc


class Kern:
    def __init__(self, P, dram, out, stage):
        self.P = P
        self.dram = dram
        self.out = out
        self.stage = stage

    def build(self):
        P = self.P
        self.X = P.sb("X", [128, NCH, S], F32)
        self.XN = P.sb("XN", [128, NCH, S], BF16)
        self.ID32 = P.sb("id32", [128, 128], F32)
        self.ID16 = P.sb("id16", [128, 128], BF16)
        self.ONES32 = P.sb("ones32", [1, 128], F32)
        self.ONESC = P.sb("onesc", [128, 1], F32)
        self.ONESC16 = P.sb("onesc16", [128, 1], BF16)
        self.ONESR16 = P.sb("onesr16", [1, 512], BF16)
        self.MASKH = P.sb("maskh", [128, 2], BF16)

        self.prep_const_phase()
        self.prep_bias_phase()

        for seq in range(NSEQ):
            self.load_x_phase(seq)
            if self.stage == "io":
                self.store_phase(seq, self.X, F32)
                continue
            for layer in range(2):
                self.layer_st = P.scope()
                self.R_st = P.scope()
                self.R = self.alloc_hg(self.R_st)
                self.ln_router_phase(seq, layer)
                if self.stage == "ln" and layer == 0:
                    self.R_st.close()
                    self.layer_st.close()
                    break
                self.attn_phase(seq, layer)
                self.R_st.close()
                if layer == 1:
                    self.lact_phase(seq)
                    self.attn_st.close()
                if self.stage == f"l{layer}attn":
                    self.layer_st.close()
                    break
                self.ffn_phase(seq, layer)
                self.layer_st.close()
                if self.stage.split(":")[0] == f"l{layer}":
                    break
            if self.stage == "ln":
                self.store_phase(seq, self.XN, BF16)
            else:
                self.store_phase(seq, self.X, F32)
        P.op("sync", lambda e: e.nop(), waits=[(s, P.cnt[s]) for s in ("XD0", "XD1", "XD2", "XD3")])
        P.flush("fin")

    def prep_const_phase(self):
        P = self.P
        for ident in (self.ID32, self.ID16):
            pm = P.op("gpsimd", lambda e, t=ident: e.memset(t[:], 0.0), tick="PP")
            P.op("gpsimd", lambda e, t=ident: e.affine_select(
                out=t[:], in_=t[:], compare_op=ALU.not_equal, fill=1.0,
                base=0, pattern=[[-1, 128]], channel_multiplier=1),
                waits=[("PP", pm)], tick="PP")
        P.op("gpsimd", lambda e: e.memset(self.ONES32[:], 1.0), tick="PP")
        P.op("gpsimd", lambda e: e.memset(self.ONESC[:], 1.0), tick="PP")
        P.op("gpsimd", lambda e: e.memset(self.ONESC16[:], 1.0), tick="PP")
        P.op("gpsimd", lambda e: e.memset(self.ONESR16[:], 1.0), tick="PP")
        pm = P.op("gpsimd", lambda e: e.memset(self.MASKH[:], 0.0), tick="PP")
        pm = P.op("gpsimd", lambda e: e.memset(self.MASKH[0:64, 0:1], 1.0),
                  waits=[("PP", pm)], tick="PP")
        P.op("gpsimd", lambda e: e.memset(self.MASKH[64:128, 1:2], 1.0),
             waits=[("PP", pm)], tick="PP")
        P.flush("consts")

    def prep_bias_phase(self):
        P = self.P
        d = self.dram
        self.BT = P.sb("BT", [128, 32, NROWS], F32)
        stp = P.scope()
        rows = P.sb_in(stp, "brows", [128, 4096], F32)
        pm = P.op("gpsimd", lambda e: e.memset(rows[:], 0.0), tick="PP")
        self.row_of = {}
        dma_t = 0
        r = 0
        for (name, key, sl, scale) in BT_ROWS:
            self.row_of[name] = r
            if sl is None:
                src = d[key].ap()
                length = INPUT_SPECS[key][0]
            elif isinstance(sl, tuple) and sl[0] == "row":
                src = d[key][sl[1], :]
                length = INPUT_SPECS[key][1]
            else:
                src = d[key][sl[0]:sl[1]]
                length = sl[1] - sl[0]
            dma_t = P.dma("sync", rows[r:r + 1, 0:length], src,
                          waits=[("PP", pm)])
            r += 1
        row_deps = P.qdeps("sync")
        psb = [P.ps_in(stp, "btp", [128, 128], F32) for _ in range(2)]
        ps_free = [0, 0]
        ev = 0
        for cs in range(32):
            ps = psb[cs % 2]
            mt = P.op("tensor", lambda e, cs=cs, ps=ps: e.transpose(
                ps[:], rows[:, cs * 128:(cs + 1) * 128], self.ID32[:]),
                waits=row_deps + [("PP", P.cnt["PP"]), ("EV", ps_free[cs % 2])],
                tick="MM")
            ev = P.op("scalar", lambda e, cs=cs, ps=ps: e.activation(
                self.BT[:, cs, :], ps[:, 0:NROWS], AF.Copy), waits=[("MM", mt)], tick="EV")
            ps_free[cs % 2] = ev
        for (name, key, sl, scale) in BT_ROWS:
            if scale is None:
                continue
            for c in range(8):
                P.op("vector", lambda e, name=name, c=c, s=scale: e.tensor_scalar_mul(
                    self.bias_col(name, c), self.bias_col(name, c), s),
                    waits=[("EV", ev)], tick="VV")
        stp.close()
        P.flush("bias")

    def bias_col(self, name, chunk):
        r = self.row_of[name]
        return self.BT[:, chunk, r:r + 1]

    def load_row(self, st, key, sl=None, waits=()):
        P = self.P
        d = self.dram
        if sl is None:
            src = d[key].ap()
            length = INPUT_SPECS[key][0]
        elif isinstance(sl, tuple) and sl[0] == "row":
            src = d[key][sl[1], :]
            length = INPUT_SPECS[key][1]
        else:
            src = d[key][sl[0]:sl[1]]
            length = sl[1] - sl[0]
        t = P.sb_in(st, "row", [1, length], F32)
        dma_t = P.dma("sync", t[:], src, waits=waits)
        return t, dma_t

    def load_x_phase(self, seq):
        P = self.P
        x = self.dram["x"]
        stp = P.scope()
        STG = P.sb_in(stp, "stg", [128, 2, D], F32)
        psb = [P.ps_in(stp, "xl", [128, 512], F32) for _ in range(4)]
        ps_free = [0, 0, 0, 0]
        stg_free = [0, 0]
        for tt in range(NTT):
            buf = tt % 2
            dma_t = P.dma("sync", STG[:, buf, :], x[seq, tt * 128:(tt + 1) * 128, :],
                          waits=[("MM", stg_free[buf])])
            for half in range(2):
                b = (tt % 2) * 2 + half
                ps = psb[b]
                mt = 0
                for k in range(4):
                    c = half * 4 + k
                    mt = P.op("tensor", lambda e, ps=ps, k=k, c=c, buf=buf: e.transpose(
                        ps[:, k * 128:(k + 1) * 128],
                        STG[:, buf, c * 128:(c + 1) * 128], self.ID32[:]),
                        waits=[dma_t, ("EV", ps_free[b])], tick="MM")
                ps_free[b] = P.op("scalar", lambda e, ps=ps, half=half, tt=tt: e.activation(
                    self.X[:, half * 4:half * 4 + 4, tt * 128:(tt + 1) * 128],
                    ps[:].rearrange("p (c t) -> p c t", t=128),
                    AF.Copy), waits=[("MM", mt)], tick="EV")
                if half == 1:
                    stg_free[buf] = mt
        stp.close()
        P.flush(f"loadx{seq}")

    def store_phase(self, seq, srcten, dtype):
        P = self.P
        ident = self.ID32 if dtype == F32 else self.ID16
        stp = P.scope()
        STG = P.sb_in(stp, "stg", [128, 2, D], F32)
        psb = [P.ps_in(stp, "xs", [128, 512], dtype) for _ in range(4)]
        ps_free = [0, 0, 0, 0]
        stg_free = [0, 0]
        for tt in range(NTT):
            buf = tt % 2
            evs = []
            for half in range(2):
                b = (tt % 2) * 2 + half
                ps = psb[b]
                mt = 0
                for k in range(4):
                    c = half * 4 + k
                    mt = P.op("tensor", lambda e, ps=ps, k=k, c=c, tt=tt: e.transpose(
                        ps[:, k * 128:(k + 1) * 128],
                        srcten[:, c, tt * 128:(tt + 1) * 128], ident[:]),
                        waits=[("EV", ps_free[b])], tick="MM")
                ev = P.op("scalar", lambda e, ps=ps, half=half, buf=buf: e.activation(
                    STG[:, buf, half * 512:half * 512 + 512], ps[:], AF.Copy),
                    waits=[("MM", mt), stg_free[buf]], tick="EV")
                ps_free[b] = ev
                evs.append(ev)
            dma_t = P.dma("sync", self.out[seq, tt * 128:(tt + 1) * 128, :],
                          STG[:, buf, :], waits=[("EV", evs[-1])])
            stg_free[buf] = dma_t
        P.op("sync", lambda e: e.nop(), waits=P.qdeps("sync"))
        stp.close()
        P.flush(f"storex{seq}")

    def alloc_ln_views(self, st, R, n=512):
        P = self.P
        psS = R["psS"]
        lp = dict(
            ps_s=psS[0:1, 0, 0, 0:n],
            ps_q=psS[0:1, 0, 1, 0:n],
            ps_mu=psS[:, 1, 0, 0:n],
            ps_rs=psS[:, 1, 1, 0:n],
            sq=P.sb_in(st, "sq", [128, 2, n], F32),
            mu=P.sb_in(st, "mu", [1, n], F32),
            e2r=P.sb_in(st, "e2r", [1, n], F32),
            var=P.sb_in(st, "var", [1, n], F32),
            rstd=P.sb_in(st, "rstd", [1, n], F32),
            sq_mm_prev=[0, 0], stats_ev_prev=0, apply_prev=0,
        )
        return lp

    def alloc_ln(self, st, n=512):
        P = self.P
        lnps = dict(
            ps_s=P.ps_in(st, "lns", [1, n], F32),
            ps_q=P.ps_in(st, "lnq", [1, n], F32),
            ps_mu=P.ps_in(st, "mub", [128, n], F32),
            ps_rs=P.ps_in(st, "rsb", [128, n], F32),
            sq=P.sb_in(st, "sq", [128, 2, n], F32),
            mu=P.sb_in(st, "mu", [1, n], F32),
            e2r=P.sb_in(st, "e2r", [1, n], F32),
            var=P.sb_in(st, "var", [1, n], F32),
            rstd=P.sb_in(st, "rstd", [1, n], F32),
            sq_mm_prev=[0, 0], stats_ev_prev=0, apply_prev=0,
        )
        return lnps

    def emit_ln_tb(self, lp, get_x, fp32_src, waits_base):
        P = self.P
        w0 = list(waits_base)
        mts = 0
        onesc = self.ONESC if fp32_src else self.ONESC16
        for c in range(NCH):
            mts = P.op("tensor", lambda e, c=c: e.matmul(
                lp["ps_s"][:], onesc[:], get_x(c),
                start=(c == 0), stop=(c == NCH - 1)),
                waits=w0 + [("EV", lp["stats_ev_prev"])], tick="MM")
        sq_mm = 0
        sq_ev = 0
        sq_dve = lp.get("sq_dve", False)
        for c in range(NCH):
            b = c % 2
            if sq_dve:
                sq_ev = P.op("vector", lambda e, c=c, b=b: e.tensor_tensor(
                    lp["sq"][:, b, :], get_x(c), get_x(c), ALU.mult),
                    waits=w0 + [("MM", lp["sq_mm_prev"][b])], tick="VV")
                sq_mm = P.op("tensor", lambda e, c=c, b=b: e.matmul(
                    lp["ps_q"][:], self.ONESC[:], lp["sq"][:, b, :],
                    start=(c == 0), stop=(c == NCH - 1)),
                    waits=[("VV", sq_ev), ("EV", lp["stats_ev_prev"])], tick="MM")
            else:
                sq_ev = P.op("scalar", lambda e, c=c, b=b: e.activation(
                    lp["sq"][:, b, :], get_x(c), AF.Square),
                    waits=w0 + [("MM", lp["sq_mm_prev"][b])], tick="EV")
                sq_mm = P.op("tensor", lambda e, c=c, b=b: e.matmul(
                    lp["ps_q"][:], self.ONESC[:], lp["sq"][:, b, :],
                    start=(c == 0), stop=(c == NCH - 1)),
                    waits=[("EV", max(sq_ev, lp["stats_ev_prev"]))], tick="MM")
            lp["sq_mm_prev"][b] = sq_mm
        ev_mu = P.op("scalar", lambda e: e.activation(
            lp["mu"][:], lp["ps_s"][:], AF.Copy, scale=1.0 / D),
            waits=[("MM", mts)], tick="EV")
        ev_e2 = P.op("scalar", lambda e: e.activation(
            lp["e2r"][:], lp["ps_q"][:], AF.Copy, scale=1.0 / D),
            waits=[("MM", sq_mm)], tick="EV")
        lp["stats_ev_prev"] = ev_e2
        v1 = P.op("vector", lambda e: e.tensor_tensor(
            lp["var"][:], lp["mu"][:], lp["mu"][:], ALU.mult),
            waits=[("EV", ev_mu)], tick="VV")
        v2 = P.op("vector", lambda e: e.scalar_tensor_tensor(
            out=lp["var"][:], in0=lp["e2r"][:], scalar=EPS, in1=lp["var"][:],
            op0=ALU.add, op1=ALU.subtract),
            waits=[("EV", ev_e2), ("VV", v1)], tick="VV")
        e_ln = P.op("scalar", lambda e: e.activation(
            lp["rstd"][:], lp["var"][:], AF.Ln), waits=[("VV", v2)], tick="EV")
        e_rs = P.op("scalar", lambda e: e.activation(
            lp["rstd"][:], lp["rstd"][:], AF.Exp, scale=-0.5),
            waits=[("EV", e_ln)], tick="EV")
        bmu = P.op("tensor", lambda e: e.matmul(
            lp["ps_mu"][:], self.ONES32[0:1, 0:128], lp["mu"][:],
            start=True, stop=True),
            waits=[("EV", ev_mu), ("VV", lp["apply_prev"])], tick="MM")
        brs = P.op("tensor", lambda e: e.matmul(
            lp["ps_rs"][:], self.ONES32[0:1, 0:128], lp["rstd"][:],
            start=True, stop=True),
            waits=[("EV", e_rs), ("VV", lp["apply_prev"])], tick="MM")
        return dict(bmu=bmu, brs=brs, src_mm=max(mts, sq_mm),
                    src_ev=0 if sq_dve else sq_ev,
                    src_vv=sq_ev if sq_dve else 0)

    def ln_router_phase(self, seq, layer):
        P = self.P
        d = self.dram
        E = 4 if layer == 0 else 8
        self.CW_TOK = P.sb_in(self.layer_st, "cwtok", [128, NTT, 8], F32)
        self.xn_ev = [0] * NTB
        st = P.scope()
        rw = P.sb_in(st, "rw", [128, NCH, E], F32)
        key = "L0_ffn_router_w" if layer == 0 else "L1_ffn_Wr"
        rload = P.dma("sync", rw[:], d[key].ap().rearrange("(c p) e -> p c e", p=128))
        rb = P.sb_in(st, "rb", [1, 16], F32)
        rbc = P.sb_in(st, "rbc", [8, 1], F32)
        if layer == 0:
            t1 = P.dma("sync", rb[0:1, 0:4], d["L0_ffn_router_b"].ap())
            t2 = P.dma("sync", rb[0:1, 8:12], d["L0_ffn_load_balancer"].ap())
            v = P.op("vector", lambda e: e.tensor_tensor(
                rb[0:1, 0:4], rb[0:1, 0:4], rb[0:1, 8:12], ALU.add),
                waits=[t1, t2], tick="VV")
        else:
            t1 = P.dma("sync", rb[0:1, 0:8], d["L1_ffn_br"].ap())
            v = P.op("vector", lambda e: e.tensor_copy(rb[0:1, 0:8], rb[0:1, 0:8]),
                     waits=[t1], tick="VV")
        psrb = self.R["psP"][0][0:8, 0:1]
        mtrb = P.op("tensor", lambda e, E=E: e.matmul(
            psrb[0:E, 0:1], rb[0:1, 0:E], self.ONES32[0:1, 0:1],
            start=True, stop=True), waits=[("VV", v)], tick="MM")
        evrb = P.op("scalar", lambda e, E=E: e.activation(
            rbc[0:E, :], psrb[0:E, 0:1], AF.Copy), waits=[("MM", mtrb)], tick="EV")
        self.R["psP_free"][0] = ("EV", evrb)

        lp = self.alloc_ln_views(st, self.R)
        xt32 = P.sb_in(st, "xt32", [128, NCH, 512], F32)
        lgf = P.sb_in(st, "lgf", [8, 512], F32)
        lgt = P.sb_in(st, "lgt", [128, 4, 8], F32)
        tmp = P.sb_in(st, "tmp", [128, 4, 8], F32)
        keep = P.sb_in(st, "keep", [128, 4, 8], F32)
        m1 = P.sb_in(st, "m1", [128, 4], F32)
        m2 = P.sb_in(st, "m2", [128, 4], F32)
        rden = P.sb_in(st, "rden", [128, 4], F32)
        ps_lg = self.R["psC"][0][0:8, :]
        ps_tm = self.R["psC"][1][:, 0:32].rearrange("p (a b) -> p a b", b=8)

        prev_rmm = 0
        prev_lg_ev = 0
        prev_tm_ev = 0
        for tb in range(NTB):
            tsl = slice(tb * TB, (tb + 1) * TB)
            r = self.emit_ln_tb(lp, lambda c, tsl=tsl: self.X[:, c, tsl], True, [])
            apply_vv = 0
            for c in range(NCH):
                vs = P.op("vector", lambda e, c=c, tsl=tsl: e.tensor_tensor(
                    xt32[:, c, :], self.X[:, c, tsl], lp["ps_mu"][:], ALU.subtract),
                    waits=[("MM", max(r["bmu"], prev_rmm))], tick="VV")
                vm = P.op("vector", lambda e, c=c: e.tensor_tensor(
                    xt32[:, c, :], xt32[:, c, :], lp["ps_rs"][:], ALU.mult),
                    waits=[("MM", r["brs"]), ("VV", vs)], tick="VV")
                apply_vv = vm
                xnev = P.op("scalar", lambda e, c=c, tsl=tsl: e.activation(
                    self.XN[:, c, tsl], xt32[:, c, :], AF.Copy),
                    waits=[("VV", vm)], tick="EV")
            self.xn_ev[tb] = xnev
            lp["apply_prev"] = apply_vv
            rmm = 0
            for c in range(NCH):
                rmm = P.op("tensor", lambda e, c=c: e.matmul(
                    ps_lg[0:E, :], rw[:, c, :], xt32[:, c, :],
                    start=(c == 0), stop=(c == NCH - 1)),
                    waits=[("VV", apply_vv), rload,
                           ("EV", prev_lg_ev)], tick="MM")
            prev_rmm = rmm
            ev_lg = P.op("scalar", lambda e: e.activation(
                lgf[0:E, :], ps_lg[0:E, :], AF.Identity, bias=rbc[0:E, 0:1]),
                waits=[("MM", rmm), ("EV", evrb)], tick="EV")
            prev_lg_ev = ev_lg
            mtt = 0
            for k in range(4):
                mtt = P.op("tensor", lambda e, k=k: e.transpose(
                    ps_tm[:, k, 0:E], lgf[0:E, k * 128:(k + 1) * 128],
                    self.ID32[0:E, 0:E]),
                    waits=[("EV", max(ev_lg, prev_tm_ev))], tick="MM")
            ev_tm = P.op("scalar", lambda e: e.activation(
                lgt[:, :, 0:E], ps_tm[:, :, 0:E], AF.Copy),
                waits=[("MM", mtt)], tick="EV")
            prev_tm_ev = ev_tm
            for k in range(4):
                lg = lgt[:, k, 0:E]
                cwk = self.CW_TOK[:, tb * 4 + k, 0:E]
                tm = tmp[:, k, 0:E]
                kp = keep[:, k, 0:E]
                m1c = m1[:, k:k + 1]
                m2c = m2[:, k:k + 1]
                rdc = rden[:, k:k + 1]
                a = P.op("vector", lambda e, lg=lg, m1c=m1c: e.tensor_reduce(
                    m1c, lg, AXL.X, ALU.max), waits=[("EV", ev_tm)], tick="VV")
                b_ = P.op("vector", lambda e, lg=lg, tm=tm, m1c=m1c: e.tensor_scalar(
                    tm, lg, m1c, None, ALU.subtract), waits=[("VV", a)], tick="VV")
                if layer == 0:
                    c_ = P.op("vector", lambda e, tm=tm, kp=kp: e.tensor_scalar(
                        kp, tm, 0.0, None, ALU.is_equal), waits=[("VV", b_)], tick="VV")
                    d_ = P.op("vector", lambda e, kp=kp, tm=tm, cwk=cwk: e.scalar_tensor_tensor(
                        out=cwk, in0=kp, scalar=-1e30, in1=tm, op0=ALU.mult, op1=ALU.add),
                        waits=[("VV", c_)], tick="VV")
                    e_ = P.op("vector", lambda e, cwk=cwk, m2c=m2c: e.tensor_reduce(
                        m2c, cwk, AXL.X, ALU.max), waits=[("VV", d_)], tick="VV")
                    f_ = P.op("vector", lambda e, tm=tm, kp=kp, m2c=m2c: e.tensor_scalar(
                        kp, tm, m2c, None, ALU.is_ge), waits=[("VV", e_)], tick="VV")
                    g_ = P.op("scalar", lambda e, tm=tm: e.activation(
                        tm, tm, AF.Exp), waits=[("VV", f_)], tick="EV")
                    h_ = P.op("scalar", lambda e, m2c=m2c, rdc=rdc: e.activation(
                        rdc, m2c, AF.Exp), waits=[("VV", f_)], tick="EV")
                    i_ = P.op("vector", lambda e, rdc=rdc: e.tensor_scalar_add(
                        rdc, rdc, 1.0), waits=[("EV", h_)], tick="VV")
                    j_ = P.op("vector", lambda e, rdc=rdc: e.reciprocal(
                        rdc, rdc), waits=[("VV", i_)], tick="VV")
                    k_ = P.op("vector", lambda e, cwk=cwk, tm=tm, kp=kp: e.tensor_tensor(
                        cwk, tm, kp, ALU.mult), waits=[("EV", g_)], tick="VV")
                    P.op("vector", lambda e, cwk=cwk, rdc=rdc: e.tensor_scalar(
                        cwk, cwk, rdc, None, ALU.mult),
                        waits=[("VV", max(k_, j_))], tick="VV")
                else:
                    c_ = P.op("scalar", lambda e, tm=tm: e.activation(
                        tm, tm, AF.Exp), waits=[("VV", b_)], tick="EV")
                    d_ = P.op("vector", lambda e, tm=tm, rdc=rdc: e.tensor_reduce(
                        rdc, tm, AXL.X, ALU.add), waits=[("EV", c_)], tick="VV")
                    e_ = P.op("vector", lambda e, rdc=rdc: e.reciprocal(
                        rdc, rdc), waits=[("VV", d_)], tick="VV")
                    f_ = P.op("vector", lambda e, tm=tm, rdc=rdc: e.tensor_scalar(
                        tm, tm, rdc, None, ALU.mult), waits=[("VV", e_)], tick="VV")
                    g_ = P.op("vector", lambda e, tm=tm, m1c=m1c: e.tensor_reduce(
                        m1c, tm, AXL.X, ALU.max), waits=[("VV", f_)], tick="VV")
                    h_ = P.op("vector", lambda e, tm=tm, kp=kp, m1c=m1c: e.tensor_scalar(
                        kp, tm, m1c, None, ALU.is_equal), waits=[("VV", g_)], tick="VV")
                    i_ = P.op("vector", lambda e, kp=kp, tm=tm, cwk=cwk: e.scalar_tensor_tensor(
                        out=cwk, in0=kp, scalar=-2.0, in1=tm, op0=ALU.mult, op1=ALU.add),
                        waits=[("VV", h_)], tick="VV")
                    j_ = P.op("vector", lambda e, cwk=cwk, m2c=m2c: e.tensor_reduce(
                        m2c, cwk, AXL.X, ALU.max), waits=[("VV", i_)], tick="VV")
                    k_ = P.op("vector", lambda e, tm=tm, kp=kp, m2c=m2c: e.tensor_scalar(
                        kp, tm, m2c, None, ALU.is_ge), waits=[("VV", j_)], tick="VV")
                    P.op("vector", lambda e, cwk=cwk, tm=tm, kp=kp: e.tensor_tensor(
                        cwk, tm, kp, ALU.mult), waits=[("VV", k_)], tick="VV")
        self.cw_ready = P.cnt["VV"]
        R = self.R
        R["psS_free"][0] = ("EV", lp["stats_ev_prev"])
        R["psS_free"][1] = ("VV", lp["apply_prev"])
        R["psC_free"][0] = ("EV", prev_lg_ev)
        R["psC_free"][1] = ("EV", prev_tm_ev)
        st.close()

    def xn2_phase(self, seq):
        P = self.P
        st = P.scope()
        lp = self.alloc_ln_views(st, self.R)
        if self.R["psS_free"][0] is not None:
            lp["stats_ev_prev"] = self.R["psS_free"][0][1]
        if self.R["psS_free"][1] is not None and self.R["psS_free"][1][0] == "VV":
            lp["apply_prev"] = self.R["psS_free"][1][1]
        t32 = P.sb_in(st, "t32", [128, 512], F32)
        prev_ap = 0
        self.xn2_ev = [0] * NTB
        for tb in range(NTB):
            tsl = slice(tb * TB, (tb + 1) * TB)
            r = self.emit_ln_tb(lp, lambda c, tsl=tsl: self.XN[:, c, tsl], False,
                                [("EV", self.xn_ev[tb])])
            apply_vv = 0
            for c in range(NCH):
                vs = P.op("vector", lambda e, c=c, tsl=tsl: e.tensor_tensor(
                    t32[:], self.XN[:, c, tsl], lp["ps_mu"][:], ALU.subtract),
                    waits=[("MM", r["bmu"]), ("EV", prev_ap)], tick="VV")
                vm = P.op("vector", lambda e: e.tensor_tensor(
                    t32[:], t32[:], lp["ps_rs"][:], ALU.mult),
                    waits=[("MM", r["brs"]), ("VV", vs)], tick="VV")
                apply_vv = vm
                prev_ap = P.op("scalar", lambda e, c=c, tsl=tsl: e.activation(
                    self.XN2[:, c, tsl], t32[:], AF.Identity,
                    scale=self.bias_col("l1_ln_w", c),
                    bias=self.bias_col("l1_ln_b", c)),
                    waits=[("VV", vm)], tick="EV")
            self.xn2_ev[tb] = prev_ap
            lp["apply_prev"] = apply_vv
        R = self.R
        R["psS_free"][0] = ("EV", lp["stats_ev_prev"])
        R["psS_free"][1] = ("VV", lp["apply_prev"])
        st.close()

    def attn_phase(self, seq, layer):
        P = self.P
        src = self.XN
        if layer == 1:
            self.attn_st = P.scope()
            self.XN2 = P.sb_in(self.attn_st, "XN2", [128, NCH, S], BF16)
            self.xn2_phase(seq)
            src = self.XN2
        R = self.R
        if layer == 0:
            st = P.scope()
            CTX = P.sb_in(st, "ctx", [128, NCH, S], BF16)
            stsb = P.scope()
            self.alloc_hg_sb(stsb, R)
            R["vrow"] = P.sb_in(stsb, "vrow16", [1, 1024], BF16)
            R["vrow_dep"] = P.dma("gpsimd", R["vrow"][0:1, :],
                                  self.dram["L0_attn_b_v"].ap())
            self.emit_hg_set(seq, layer, src, list(range(NCH)), CTX, R)
            stsb.close()
            self.out_proj_l0(seq, CTX, st)
            P.flush(f"attn{seq}_{layer}")
            st.close()
        else:
            for half in range(2):
                st = P.scope()
                CTX = P.sb_in(st, "ctx", [128, 4, S], BF16)
                stsb = P.scope()
                self.alloc_hg_sb(stsb, R)
                R["vrow"] = P.sb_in(stsb, "vrow16", [1, 1024], BF16)
                R["vrow_dep"] = P.dma("gpsimd", R["vrow"][0:1, :],
                                      self.dram["L1_attn_in_proj_b"][2048:3072])
                self.emit_hg_set(seq, layer, src,
                                 [half * 4 + lg for lg in range(4)], CTX, R)
                stsb.close()
                self.out_proj_l1(seq, CTX, half, st)
                P.flush(f"attn{seq}_{layer}_{half}")
                st.close()

    def cnt_guard(self):
        P = self.P
        return [(s, P.cnt[s]) for s in ("MM", "EV", "VV", "PP")]

    def alloc_hg(self, st):
        P = self.P
        R = dict(
            psS=P.ps_in(st, "psS", [128, 2, 2, 512], F32),
            psC=[P.ps_in(st, "psC", [128, 512], F32) for _ in range(2)],
            psP=[P.ps_in(st, "psP", [128, 512], F32) for _ in range(2)],
            wnat_free=[0, 0],
            wt_free=0,
            kt_free=0, qt_free=0,
            v_free=0,
            pt_free=[[0, 0], [0, 0]],
            psS_free=[None, None],
            psC_free=[None, None],
            psP_free=[None, None],
            sq_free=[0, 0],
            stat_free=[0, 0],
            rl_free=0,
            ct32_free=0,
            ones_pp=0,
        )
        return R

    def alloc_hg_sb(self, st, R):
        P = self.P
        R.update(
            wnat=[P.sb_in(st, "wnat", [128, D], F32)],
            WT=[P.sb_in(st, "wt", [128, 3, NCH, 128], BF16) for _ in range(2)],
            KT=[P.sb_in(st, "kt", [128, S], BF16) for _ in range(2)],
            QT=[P.sb_in(st, "qt", [128, S], BF16) for _ in range(2)],
            V=[P.sb_in(st, "v", [128, NTT, 130], BF16) for _ in range(2)],
            PT=[P.sb_in(st, "pt", [128, 2, 2, 512], BF16) for _ in range(2)],
            NEGC=[P.sb_in(st, "negc", [128, 2, 4], F32) for _ in range(2)],
            CT32=P.sb_in(st, "ct32", [64, 512], F32),
            RLROW=P.sb_in(st, "rl", [1, 512], F32),
            SQ=P.sb_in(st, "sqn", [128, 2, 512], BF16),
            statq=[[P.sb_in(st, "stq", [1, 8], F32) for _ in range(2)]
                   for _ in range(2)],
            statk=[[P.sb_in(st, "stk", [1, 8], F32) for _ in range(2)]
                   for _ in range(2)],
        )
        g = self.cnt_guard()
        pp = P.op("gpsimd", lambda e: e.memset(R["V"][0][:, :, 64:65], 1.0),
                  waits=g, tick="PP")
        pp = P.op("gpsimd", lambda e: e.memset(R["V"][0][:, :, 129:130], 1.0),
                  tick="PP")
        pp = P.op("gpsimd", lambda e: e.memset(R["V"][1][:, :, 64:65], 1.0),
                  tick="PP")
        R["ones_pp"] = P.op("gpsimd", lambda e: e.memset(R["V"][1][:, :, 129:130], 1.0),
                            tick="PP")
        R["wnat_free"] = [0]
        R["wt_free"] = [0, 0]
        R["kt_free"] = [0, 0]
        R["qt_free"] = [0, 0]
        R["v_free"] = [0, 0]
        R["sq_free"] = [0, 0]
        R["stat_free"] = [[0, 0], [0, 0]]
        return R

    def emit_hg_proj(self, seq, layer, src, ch, par, R, out):
        P = self.P
        d = self.dram
        vrow, vrow_dep = R["vrow"], R["vrow_dep"]
        xn_deps = self.xn2_ev if layer == 1 else self.xn_ev
        WT, KT, QT, V = R["WT"][par], R["KT"][par], R["QT"][par], R["V"][par]
        NEGC, SQ = R["NEGC"][par], R["SQ"]
        statq, statk = R["statq"][par], R["statk"][par]
        psP = R["psP"]
        wkeys = {
            0: [("L0_attn_w_q", 0, 0.125), ("L0_attn_w_k", 0, None),
                ("L0_attn_w_v", 0, None)],
            1: [("L1_attn_in_proj_w", 0, 0.125), ("L1_attn_in_proj_w", 1024, None),
                ("L1_attn_in_proj_w", 2048, None)],
        }[layer]
        wt_ready = 0
        for mi, (key, roff, scale) in enumerate(wkeys):
            wb = 0
            wnat = R["wnat"][wb]
            dma_t = P.dma("sync", wnat[:],
                          d[key][roff + ch * 128: roff + (ch + 1) * 128, :],
                          waits=[("MM", R["wnat_free"][wb])])
            for g in range(2):
                ps = psP[g]
                mt = 0
                for k in range(4):
                    kc = g * 4 + k
                    mt = P.op("tensor", lambda e, ps=ps, k=k, kc=kc, wnat=wnat: e.transpose(
                        ps[:, k * 128:(k + 1) * 128],
                        wnat[:, kc * 128:(kc + 1) * 128], self.ID32[:]),
                        waits=[dma_t, psP_w(R, g)], tick="MM")
                if g == 1:
                    R["wnat_free"][wb] = mt
                if scale is None:
                    wt_ready = P.op("vector", lambda e, ps=ps, mi=mi, g=g: e.tensor_copy(
                        WT[:, mi, g * 4:g * 4 + 4, :],
                        ps[:].rearrange("p (c t) -> p c t", t=128)),
                        waits=[("MM", max(mt, R["wt_free"][par]))], tick="VV")
                else:
                    wt_ready = P.op("vector", lambda e, ps=ps, mi=mi, g=g, s=scale: e.tensor_scalar(
                        WT[:, mi, g * 4:g * 4 + 4, :],
                        ps[:].rearrange("p (c t) -> p c t", t=128),
                        s, None, ALU.mult),
                        waits=[("MM", max(mt, R["wt_free"][par]))], tick="VV")
                R["psP_free"][g] = ("VV", wt_ready)
                yield

        bq, bk = ("b_q8", "b_k") if layer == 0 else ("inp_b8", "inp_bk")
        kq_ev = 0
        for mi, (dst, bias, rd_free) in enumerate(
                ((QT, bq, R["qt_free"][par]), (KT, bk, R["kt_free"][par]))):
            for tb in range(NTB):
                tsl = slice(tb * TB, (tb + 1) * TB)
                b = (mi * NTB + tb) % 2
                ps = psP[b]
                mt = 0
                for kc in range(NCH):
                    mt = P.op("tensor", lambda e, ps=ps, mi=mi, kc=kc, tsl=tsl: e.matmul(
                        ps[:], WT[:, mi, kc, :], src[:, kc, tsl],
                        start=(kc == 0), stop=(kc == NCH - 1)),
                        waits=[("VV", wt_ready), ("EV", xn_deps[tb]),
                               psP_w(R, b)], tick="MM")
                kq_ev = P.op("vector", lambda e, ps=ps, dst=dst, bias=bias, tsl=tsl: e.tensor_scalar(
                    dst[:, tsl], ps[:], self.bias_col(bias, ch), None, ALU.add),
                    waits=[("MM", max(mt, rd_free))], tick="VV")
                R["psP_free"][b] = ("VV", kq_ev)
                yield
        v_ev = 0
        last_proj_mm = 0
        for tt in range(NTT):
            b = tt % 2
            ps = psP[b]
            for kc in range(NCH):
                P.op("tensor", lambda e, ps=ps, kc=kc, tt=tt: e.matmul(
                    ps[:, 0:128], src[:, kc, tt * 128:(tt + 1) * 128],
                    WT[:, 2, kc, :], start=(kc == 0), stop=False),
                    waits=[("VV", wt_ready), ("EV", xn_deps[tt // 4]),
                           psP_w(R, b)], tick="MM")
            mt = P.op("tensor", lambda e, ps=ps: e.matmul(
                ps[:, 0:128], self.ONESR16[0:1, 0:128],
                vrow[0:1, ch * 128:(ch + 1) * 128], start=False, stop=True),
                waits=[vrow_dep], tick="MM")
            last_proj_mm = mt
            P.op("vector", lambda e, ps=ps, tt=tt: e.tensor_copy(
                V[:, tt, 0:64], ps[:, 0:64]),
                waits=[("MM", max(mt, R["v_free"][par]))], tick="VV")
            v_ev = P.op("vector", lambda e, ps=ps, tt=tt: e.tensor_copy(
                V[:, tt, 65:129], ps[:, 64:128]), waits=[("MM", mt)], tick="VV")
            R["psP_free"][b] = ("VV", v_ev)
            if tt % 2 == 1:
                yield

        sqf = R["sq_free"]
        negc_ev = 0
        sidx = 0
        for hj in range(2):
            last_red = 0
            for (dstt, stat) in ((QT, statq[hj]), (KT, statk[hj])):
                for tb in range(NTB):
                    tsl = slice(tb * TB, (tb + 1) * TB)
                    b = sidx % 2
                    sidx += 1
                    evq = P.op("gpsimd", lambda e, dstt=dstt, tsl=tsl, b=b: e.tensor_mul(
                        SQ[:, b, :], dstt[:, tsl], dstt[:, tsl]),
                        waits=[("VV", max(kq_ev, v_ev)), ("MM", sqf[b])], tick="PP")
                    mmb = P.op("tensor", lambda e, hj=hj, b=b: e.matmul(
                        psP[b][0:1, :], self.MASKH[:, hj:hj + 1], SQ[:, b, :],
                        start=True, stop=True),
                        waits=[("PP", evq), psP_w(R, b)], tick="MM")
                    sqf[b] = mmb
                    last_red = P.op("vector", lambda e, stat=stat, tb=tb, b=b: e.tensor_reduce(
                        stat[0:1, tb:tb + 1], psP[b][0:1, :], AXL.X, ALU.max),
                        waits=[("MM", max(mmb, R["stat_free"][par][hj]))], tick="VV")
                    R["psP_free"][b] = ("VV", last_red)
                    if tb % 2 == 1:
                        yield
            vk = P.op("vector", lambda e, hj=hj, statk=statk: e.tensor_reduce(
                statk[hj][0:1, 4:5], statk[hj][0:1, 0:4], AXL.X, ALU.max),
                waits=[("VV", last_red)], tick="VV")
            vq = P.op("vector", lambda e, hj=hj, statq=statq, statk=statk: e.tensor_scalar(
                statq[hj][0:1, 0:4], statq[hj][0:1, 0:4],
                statk[hj][0:1, 4:5], None, ALU.mult),
                waits=[("VV", vk)], tick="VV")
            el = P.op("scalar", lambda e, hj=hj, statq=statq: e.activation(
                statq[hj][0:1, 0:4], statq[hj][0:1, 0:4], AF.Ln),
                waits=[("VV", vq)], tick="EV")
            ee = P.op("scalar", lambda e, hj=hj, statq=statq: e.activation(
                statq[hj][0:1, 0:4], statq[hj][0:1, 0:4], AF.Exp, scale=0.5),
                waits=[("EV", el)], tick="EV")
            vm = P.op("vector", lambda e, hj=hj, statq=statq: e.tensor_scalar_mul(
                statq[hj][0:1, 0:4], statq[hj][0:1, 0:4], -1.05),
                waits=[("EV", ee)], tick="VV")
            mtb = P.op("tensor", lambda e, hj=hj, statq=statq: e.matmul(
                psP[0][:, 0:4], self.ONES32[0:1, 0:128], statq[hj][0:1, 0:4],
                start=True, stop=True),
                waits=[("VV", vm), psP_w(R, 0)], tick="MM")
            R["stat_free"][par][hj] = mtb
            negc_ev = P.op("vector", lambda e, hj=hj, NEGC=NEGC: e.tensor_copy(
                NEGC[:, hj, :], psP[0][:, 0:4]),
                waits=[("MM", mtb)], tick="VV")
            R["psP_free"][0] = ("VV", negc_ev)
        out.update(kq_ev=kq_ev, v_ev=v_ev, negc_ev=negc_ev,
                   last_proj_mm=last_proj_mm)
        yield

    def emit_hg_scores(self, seq, layer, par, CTX, ctxcol, R, out, feeder):
        P = self.P
        KT, QT, V, PT = R["KT"][par], R["QT"][par], R["V"][par], R["PT"]
        NEGC, CT32, RLROW = R["NEGC"][par], R["CT32"], R["RLROW"]
        psS, psC = R["psS"], R["psC"]
        kq_ev = max(out["kq_ev"], out["v_ev"])
        negc_ev = out["negc_ev"]
        last_score_mm = 0
        last_ctx_mm = 0
        pend = None
        for qb in range(NTB):
            qsl = slice(qb * TB, (qb + 1) * TB)
            ctx_mms = [0, 0]
            for ktp in range(NTT // 2):
                exp_evs = [0, 0]
                for hj in range(2):
                    base = 64 * hj
                    smm = 0
                    for pr in range(2):
                        kt = 2 * ktp + pr
                        smm = P.op(
                            "tensor", lambda e, hj=hj, pr=pr, base=base, kt=kt, qsl=qsl: e.matmul(
                                psS[:, hj, pr, :],
                                KT[base:base + 64, kt * 128:(kt + 1) * 128],
                                QT[base:base + 64, qsl], start=True, stop=True),
                            waits=[R["psS_free"][hj],
                                   ("VV", max(negc_ev, kq_ev))],
                            tick="MM")
                    last_score_mm = smm
                    kp = ktp % 2
                    ev = P.op("scalar", lambda e, hj=hj, qb=qb, NEGC=NEGC, kp=kp: e.activation(
                        PT[kp][:, hj, :, :], psS[:, hj, :, :], AF.Exp,
                        bias=NEGC[:, hj, qb:qb + 1]),
                        waits=[("MM", max(smm, R["pt_free"][hj][kp]))],
                        tick="EV")
                    R["psS_free"][hj] = ("EV", ev)
                    exp_evs[hj] = ev
                if ktp == 0 and pend is not None:
                    pend()
                    pend = None
                next(feeder, None)
                for hj in range(2):
                    for pr in range(2):
                        kt = 2 * ktp + pr
                        cm = P.op("tensor", lambda e, hj=hj, pr=pr, kt=kt, kp=kp: e.matmul(
                            psC[hj][0:65, :], V[:, kt, hj * 65:hj * 65 + 65],
                            PT[kp][:, hj, pr, :], start=(kt == 0), stop=(kt == NTT - 1)),
                            waits=[("EV", exp_evs[hj]),
                                   (R["psC_free"][hj] if kt == 0 else None),
                                   ("VV", out["v_ev"] if kt == 0 else 0),
                                   ("PP", R["ones_pp"])], tick="MM")
                        R["pt_free"][hj][kp] = cm
                        ctx_mms[hj] = cm
                        last_ctx_mm = cm

            def fin(ctx_mms=ctx_mms, qsl=qsl):
                for hj in range(2):
                    e_ln = P.op("scalar", lambda e, hj=hj: e.activation(
                        RLROW[:], psC[hj][64:65, :], AF.Ln),
                        waits=[("MM", max(ctx_mms[hj], R["rl_free"]))], tick="EV")
                    e_rl = P.op("scalar", lambda e: e.activation(
                        RLROW[:], RLROW[:], AF.Exp, scale=-1.0),
                        waits=[("EV", e_ln)], tick="EV")
                    ev_ct = P.op("vector", lambda e, hj=hj: e.tensor_copy(
                        CT32[:], psC[hj][0:64, :]),
                        waits=[("MM", ctx_mms[hj]), ("VV", R["ct32_free"])], tick="VV")
                    mt_rb = P.op("tensor", lambda e, hj=hj: e.matmul(
                        psC[hj][0:64, :], self.ONES32[0:1, 0:64], RLROW[:],
                        start=True, stop=True),
                        waits=[("EV", e_rl), ("VV", ev_ct)], tick="MM")
                    R["rl_free"] = mt_rb
                    v_fin = P.op("vector", lambda e, hj=hj, qsl=qsl: e.tensor_tensor(
                        CTX[hj * 64:hj * 64 + 64, ctxcol, qsl], CT32[:],
                        psC[hj][0:64, :], ALU.mult),
                        waits=[("MM", mt_rb), ("VV", ev_ct)], tick="VV")
                    R["psC_free"][hj] = ("VV", v_fin)
                    R["ct32_free"] = v_fin
            pend = fin
            next(feeder, None)
        if pend is not None:
            pend()
        for _ in feeder:
            pass
        R["kt_free"][par] = last_score_mm
        R["qt_free"][par] = last_score_mm
        R["v_free"][par] = last_ctx_mm
        R["wt_free"][par] = out["last_proj_mm"]
        self.ctx_ready = P.cnt["VV"]

    def emit_hg_set(self, seq, layer, src, chs, CTX, R):
        outs = [dict() for _ in chs]
        g = self.emit_hg_proj(seq, layer, src, chs[0], 0, R, outs[0])
        for _ in g:
            pass
        for i in range(len(chs)):
            if i + 1 < len(chs):
                feeder = self.emit_hg_proj(seq, layer, src, chs[i + 1],
                                           (i + 1) % 2, R, outs[i + 1])
            else:
                feeder = iter(())
            self.emit_hg_scores(seq, layer, i % 2, CTX,
                                chs[i] if layer == 0 else chs[i] % 4,
                                R, outs[i], feeder)

    def prep_wot(self, st, key, kclist, guard=()):
        P = self.P
        d = self.dram
        nk = len(kclist)
        WOT = P.sb_in(st, "wot", [128, nk, D], BF16)
        wnat = P.sb_in(st, "wnat2", [128, D], F32)
        ps_tr = [self.R["psP"][0], self.R["psP"][1]]
        tr_free = [0, 0]
        prev_use = 0
        for co in range(8):
            dma_t = P.dma("sync", wnat[:], d[key][co * 128:(co + 1) * 128, :],
                          waits=list(guard) + [("MM", prev_use)])
            for g in range(0, nk, 4):
                gn = min(4, nk - g)
                b = (g // 4) % 2
                ps = ps_tr[b]
                mt = 0
                for j in range(gn):
                    kc = kclist[g + j]
                    mt = P.op("tensor", lambda e, ps=ps, j=j, kc=kc: e.transpose(
                        ps[:, j * 128:(j + 1) * 128],
                        wnat[:, kc * 128:(kc + 1) * 128], self.ID32[:]),
                        waits=list(guard) + [dma_t, ("EV", tr_free[b])], tick="MM")
                prev_use = mt
                tr_free[b] = P.op("scalar", lambda e, ps=ps, g=g, gn=gn, co=co: e.activation(
                    WOT[:, g:g + gn, co * 128:(co + 1) * 128],
                    ps[:, 0:gn * 128].rearrange("p (c t) -> p c t", t=128), AF.Copy),
                    waits=[("MM", mt)], tick="EV")
        return WOT, max(tr_free)

    def out_proj_l0(self, seq, CTX, st):
        P = self.P
        guard = self.cnt_guard()
        WOT, wot_ev = self.prep_wot(st, "L0_attn_w_o", list(range(8)), guard)
        borow = P.sb_in(st, "borow16", [1, 1024], BF16)
        bo_dep = P.dma("gpsimd", borow[0:1, :], self.dram["L0_attn_b_o"].ap(),
                       waits=list(guard))
        Y32 = P.sb_in(st, "y32", [128, NCH, 512], F32)
        T32 = P.sb_in(st, "t32o", [128, 512], F32)
        R = self.R
        lp = self.alloc_ln_views(st, R)
        if R["psS_free"][0] is not None:
            lp["stats_ev_prev"] = R["psS_free"][0][1]
        ps_o = [R["psC"][0], R["psC"][1]]
        pso_free = [0, 0]
        prev_x_add = 0
        prev_src_mm = 0
        prev_src_ev = 0
        for tb in range(NTB):
            first = guard if tb == 0 else []
            tsl = slice(tb * TB, (tb + 1) * TB)
            yvv = 0
            for oc in range(NCH):
                b = oc % 2
                ps = ps_o[b]
                for kc in range(NCH):
                    P.op("tensor", lambda e, ps=ps, kc=kc, oc=oc, tsl=tsl: e.matmul(
                        ps[:], WOT[:, kc, oc * 128:(oc + 1) * 128], CTX[:, kc, tsl],
                        start=(kc == 0), stop=False),
                        waits=first + [("EV", wot_ev),
                                       ("VV", max(pso_free[b], self.ctx_ready))],
                        tick="MM")
                mt = P.op("tensor", lambda e, ps=ps, oc=oc: e.matmul(
                    ps[:], borow[0:1, oc * 128:(oc + 1) * 128],
                    self.ONESR16[0:1, :],
                    start=False, stop=True), waits=[bo_dep], tick="MM")
                yvv = P.op("vector", lambda e, ps=ps, oc=oc, tsl=tsl: e.tensor_tensor(
                    Y32[:, oc, :], self.XN[:, oc, tsl], ps[:], ALU.add),
                    waits=first + [("MM", max(mt, prev_src_mm)),
                                   ("EV", prev_src_ev)], tick="VV")
                pso_free[b] = yvv
            r = self.emit_ln_tb(lp, lambda c: Y32[:, c, :], True,
                                first + [("VV", yvv)])
            prev_src_mm = r["src_mm"]
            prev_src_ev = r["src_ev"]
            apply_vv = 0
            for oc in range(NCH):
                vs = P.op("vector", lambda e, oc=oc: e.tensor_tensor(
                    T32[:], Y32[:, oc, :], lp["ps_mu"][:], ALU.subtract),
                    waits=first + [("MM", r["bmu"]), ("PP", prev_x_add)], tick="VV")
                vm = P.op("vector", lambda e: e.tensor_tensor(
                    T32[:], T32[:], lp["ps_rs"][:], ALU.mult),
                    waits=[("MM", r["brs"]), ("VV", vs)], tick="VV")
                apply_vv = vm
                vg = P.op("vector", lambda e, oc=oc: e.tensor_scalar(
                    T32[:], T32[:], self.bias_col("ln_g", oc),
                    self.bias_col("ln_b", oc), ALU.mult, ALU.add),
                    waits=[("VV", vm)], tick="VV")
                prev_x_add = P.op("gpsimd", lambda e, oc=oc, tsl=tsl: e.tensor_tensor(
                    self.X[:, oc, tsl], self.X[:, oc, tsl], T32[:], ALU.add),
                    waits=[("VV", vg)], tick="PP")
            lp["apply_prev"] = apply_vv

    def out_proj_l1(self, seq, CTX, half, st):
        P = self.P
        guard = self.cnt_guard()
        kclist = [half * 4 + j for j in range(4)]
        WOT, wot_ev = self.prep_wot(st, "L1_attn_out_proj_w", kclist, guard)
        borow, bo_dep = (None, None)
        if half == 0:
            borow = P.sb_in(st, "borow16", [1, 1024], BF16)
            bo_dep = P.dma("gpsimd", borow[0:1, :],
                           self.dram["L1_attn_out_proj_b"].ap(),
                           waits=list(guard))
        ps_o = [self.R["psC"][0], self.R["psC"][1]]
        pso_free = [0, 0]
        for tb in range(NTB):
            first = guard if tb == 0 else []
            tsl = slice(tb * TB, (tb + 1) * TB)
            for oc in range(NCH):
                b = (tb * NCH + oc) % 2
                ps = ps_o[b]
                mt = 0
                for j in range(4):
                    mt = P.op("tensor", lambda e, ps=ps, j=j, oc=oc, tsl=tsl: e.matmul(
                        ps[:], WOT[:, j, oc * 128:(oc + 1) * 128], CTX[:, j, tsl],
                        start=(j == 0), stop=(j == 3 and half == 1)),
                        waits=first + [("EV", wot_ev),
                                       ("VV", max(pso_free[b], self.ctx_ready))],
                        tick="MM")
                if half == 0:
                    mt = P.op("tensor", lambda e, ps=ps, oc=oc: e.matmul(
                        ps[:], borow[0:1, oc * 128:(oc + 1) * 128],
                        self.ONESR16[0:1, :], start=False, stop=True),
                        waits=[bo_dep], tick="MM")
                pso_free[b] = P.op("vector", lambda e, ps=ps, oc=oc, tsl=tsl: e.scalar_tensor_tensor(
                    out=self.X[:, oc, tsl], in0=ps[:], scalar=1.0,
                    in1=self.X[:, oc, tsl], op0=ALU.mult, op1=ALU.add),
                    waits=[("MM", mt)], tick="VV")

    def lact_phase(self, seq):
        P = self.P
        d = self.dram
        st = P.scope()
        W1NAT = P.sb_in(st, "w1n", [128, 2, 256], BF16)
        W0T = P.sb_in(st, "w0t", [128, 2, 256], BF16)
        W2T = P.sb_in(st, "w2t", [128, 2, 256], BF16)
        W1T = P.sb_in(st, "w1t", [128, 2, 256], BF16)
        wsm = P.sb_in(st, "wsm", [128, 2, 256], F32)
        ps_w = P.ps_in(st, "psw", [128, 256], F32)
        B = dict(
            KVT=P.sb_in(st, "kvt", [128, 8, 256], BF16),
            HID=P.sb_in(st, "hid", [128, 8, 256], BF16),
            DG=P.sb_in(st, "dgb", [128, 8, 256], BF16),
            DP=P.sb_in(st, "dpb", [128, 8, 256], BF16),
            XXT=P.sb_in(st, "xxt", [128, 2, 8, 128], BF16),
            SGf=P.sb_in(st, "sgf", [128, 256], F32),
            Hf=P.sb_in(st, "hf", [128, 256], F32),
            Gf=P.sb_in(st, "gf", [128, 256], F32),
            DHf=P.sb_in(st, "dhf", [128, 256], F32),
            T1f=P.sb_in(st, "t1f", [128, 256], F32),
            DHPf=P.sb_in(st, "dhpf", [128, 256], F32),
            W0N=P.sb_in(st, "w0nb", [128, 2, 256], BF16),
            W2N=P.sb_in(st, "w2nb", [128, 2, 256], BF16),
            W1N=P.sb_in(st, "w1nb", [128, 2, 256], BF16),
            HQf=P.sb_in(st, "hqf", [128, 256], F32),
            XXb=P.sb_in(st, "xxb", [128, 256], BF16),
            ps_a=P.ps_in(st, "psa", [128, 256], F32),
            ps_b=P.ps_in(st, "psb", [128, 256], F32),
            ps_c=P.ps_in(st, "psc", [128, 256], F32),
            ps_t=P.ps_in(st, "pst", [128, 256], BF16),
            ps_u=P.ps_in(st, "psu", [128, 256], F32),
            ps_o=P.ps_in(st, "pso2", [128, 512], F32),
        )
        statics_ev = 0
        prev_wsm_mm = 0
        for h in range(4):
            mm_gate = P.cnt["MM"]
            w1nat_dep = P.dma("gpsimd", W1NAT[:],
                              d["L1_attn_w1"][h].rearrange("(c p) e -> p c e", p=128),
                              waits=[("MM", mm_gate)])
            for (dst, key) in ((W0T, "L1_attn_w0"), (W2T, "L1_attn_w2"),
                               (W1T, "L1_attn_w1")):
                dma_t = P.dma("sync", wsm[:],
                              d[key][h].rearrange("(c p) e -> p c e", p=128),
                              waits=[("MM", max(mm_gate, prev_wsm_mm))])
                for dc in range(2):
                    mt = 0
                    for er in range(2):
                        mt = P.op("tensor", lambda e, dc=dc, er=er: e.transpose(
                            ps_w[:, er * 128:(er + 1) * 128],
                            wsm[:, er, dc * 128:(dc + 1) * 128], self.ID32[:]),
                            waits=[dma_t, ("EV", statics_ev)], tick="MM")
                    prev_wsm_mm = mt
                    statics_ev = P.op("scalar", lambda e, dst=dst, dc=dc: e.activation(
                        dst[:, dc, :], ps_w[:], AF.Copy), waits=[("MM", mt)], tick="EV")
            for n in range(2):
                gate = [("EV", P.cnt["EV"]), ("VV", P.cnt["VV"])]
                self.lact_unit(seq, h, n, B, W1NAT, W0T, W2T, W1T,
                               [("EV", statics_ev), w1nat_dep] + gate)
        st.close()

    def lact_unit(self, seq, h, n, B, W1NAT, W0T, W2T, W1T, wdeps):
        P = self.P
        KVT, HID, DG, DP, XXT = B["KVT"], B["HID"], B["DG"], B["DP"], B["XXT"]
        SGf, Hf, Gf, DHf, T1f = B["SGf"], B["Hf"], B["Gf"], B["DHf"], B["T1f"]
        DHPf, W0N, W2N, W1N = B["DHPf"], B["W0N"], B["W2N"], B["W1N"]
        HQf, XXb = B["HQf"], B["XXb"]
        ps_a, ps_b, ps_c = B["ps_a"], B["ps_b"], B["ps_c"]
        ps_t, ps_u, ps_o = B["ps_t"], B["ps_u"], B["ps_o"]
        base_t = n * 1024
        ch0 = 2 * h

        kv_ev = 0
        prev = 0
        for mi in range(8):
            mm = 0
            for kc in range(2):
                mm = P.op("tensor", lambda e, mi=mi, kc=kc: e.transpose(
                    ps_t[:, kc * 128:(kc + 1) * 128],
                    self.XN2[:, ch0 + kc,
                             base_t + mi * 128: base_t + (mi + 1) * 128],
                    self.ID16[:]),
                    waits=wdeps + [("EV", prev)], tick="MM")
            prev = P.op("scalar", lambda e, mi=mi: e.activation(
                KVT[:, mi, :], ps_t[:], AF.Copy), waits=[("MM", mm)], tick="EV")
            kv_ev = prev

        pp_done = 0
        psa_vv = 0
        ev_free = [0, 0, 0]
        for mi in range(8):
            csl = slice(base_t + mi * 128, base_t + (mi + 1) * 128)
            mmg = 0
            for kc in range(2):
                mmg = P.op("tensor", lambda e, kc=kc, csl=csl: e.matmul(
                    ps_a[:], self.XN2[:, ch0 + kc, csl], W0T[:, kc, :],
                    start=(kc == 0), stop=(kc == 1)),
                    waits=wdeps + [("VV", max(psa_vv, pp_done))], tick="MM")
            ev_sg = P.op("scalar", lambda e: e.activation(
                SGf[:], ps_a[:], AF.Sigmoid), waits=[("MM", mmg)], tick="EV")
            ev_h = P.op("vector", lambda e: e.tensor_tensor(
                Hf[:], SGf[:], ps_a[:], ALU.mult),
                waits=[("EV", ev_sg)], tick="VV")
            psa_vv = ev_h
            mm2 = 0
            for kc in range(2):
                mm2 = P.op("tensor", lambda e, kc=kc, csl=csl: e.matmul(
                    ps_b[:], self.XN2[:, ch0 + kc, csl], W2T[:, kc, :],
                    start=(kc == 0), stop=(kc == 1)),
                    waits=wdeps + [("EV", ev_free[1]), ("VV", pp_done)], tick="MM")
            ev_g = P.op("scalar", lambda e: e.activation(
                Gf[:], ps_b[:], AF.Copy), waits=[("MM", mm2)], tick="EV")
            ev_free[1] = ev_g
            mm3 = 0
            for kc in range(2):
                mm3 = P.op("tensor", lambda e, kc=kc, csl=csl: e.matmul(
                    ps_c[:], self.XN2[:, ch0 + kc, csl], W1NAT[:, kc, :],
                    start=(kc == 0), stop=(kc == 1)),
                    waits=wdeps + [("EV", ev_free[2]), ("VV", pp_done)], tick="MM")
            ev_dh = P.op("scalar", lambda e: e.activation(
                DHf[:], ps_c[:], AF.Copy), waits=[("MM", mm3)], tick="EV")
            ev_free[2] = ev_dh
            p1 = P.op("vector", lambda e, mi=mi: e.tensor_tensor(
                HID[:, mi, :], Hf[:], Gf[:], ALU.mult),
                waits=[("VV", ev_h), ("EV", ev_g)], tick="VV")
            p2 = P.op("vector", lambda e: e.tensor_tensor(
                DHPf[:], DHf[:], Gf[:], ALU.mult),
                waits=[("EV", max(ev_dh, ev_g))], tick="VV")
            p3 = P.op("vector", lambda e, mi=mi: e.tensor_tensor(
                DG[:, mi, :], DHf[:], Hf[:], ALU.mult),
                waits=[("EV", ev_dh), ("VV", ev_h)], tick="VV")
            ev_t1 = P.op("scalar", lambda e: e.activation(
                T1f[:], SGf[:], AF.Identity, scale=-1.0, bias=1.0),
                waits=[("EV", ev_sg), ("VV", pp_done)], tick="EV")
            p4 = P.op("vector", lambda e: e.tensor_tensor(
                T1f[:], Hf[:], T1f[:], ALU.mult),
                waits=[("EV", ev_t1), ("VV", p3)], tick="VV")
            p5 = P.op("vector", lambda e: e.tensor_tensor(
                T1f[:], SGf[:], T1f[:], ALU.add), waits=[("VV", p4)], tick="VV")
            pp_done = P.op("vector", lambda e, mi=mi: e.tensor_tensor(
                DP[:, mi, :], DHPf[:], T1f[:], ALU.mult),
                waits=[("VV", max(p5, p2))], tick="VV")

        upd_ev = 0
        for (OUT, LHS, WSTAT) in ((W1N, HID, W1T), (W0N, None, W0T), (W2N, None, W2T)):
            RHS = DP if OUT is W0N else (DG if OUT is W2N else None)
            for m in range(2):
                mm = 0
                for mi in range(8):
                    if OUT is W1N:
                        la = HID[:, mi, m * 128:(m + 1) * 128]
                        ra = KVT[:, mi, :]
                    else:
                        la = KVT[:, mi, m * 128:(m + 1) * 128]
                        ra = RHS[:, mi, :]
                    mm = P.op("tensor", lambda e, la=la, ra=ra, mi=mi: e.matmul(
                        ps_u[:], la, ra, start=(mi == 0), stop=(mi == 7)),
                        waits=[("VV", max(pp_done, upd_ev)), ("EV", kv_ev)],
                        tick="MM")
                upd_ev = P.op("vector", lambda e, OUT=OUT, WSTAT=WSTAT, m=m: e.tensor_tensor(
                    OUT[:, m, :], WSTAT[:, m, :], ps_u[:], ALU.add),
                    waits=[("MM", mm)], tick="VV")

        xxt_ev = 0
        psa_free = ("VV", psa_vv)
        psb_free = ("EV", ev_free[1])
        prev_vxx = 0
        prev_mmx = 0
        for mi in range(8):
            csl = slice(base_t + mi * 128, base_t + (mi + 1) * 128)
            mmq = 0
            for kc in range(2):
                mmq = P.op("tensor", lambda e, kc=kc, csl=csl: e.matmul(
                    ps_a[:], self.XN2[:, ch0 + kc, csl], W0N[:, kc, :],
                    start=(kc == 0), stop=(kc == 1)),
                    waits=[("VV", upd_ev), psa_free], tick="MM")
            ev_sgq = P.op("scalar", lambda e: e.activation(
                HQf[:], ps_a[:], AF.Sigmoid),
                waits=[("MM", mmq)], tick="EV")
            ev_hq = P.op("vector", lambda e: e.tensor_tensor(
                HQf[:], HQf[:], ps_a[:], ALU.mult),
                waits=[("EV", ev_sgq), ("VV", prev_vxx)], tick="VV")
            psa_free = ("VV", ev_hq)
            mmq2 = 0
            for kc in range(2):
                mmq2 = P.op("tensor", lambda e, kc=kc, csl=csl: e.matmul(
                    ps_b[:], self.XN2[:, ch0 + kc, csl], W2N[:, kc, :],
                    start=(kc == 0), stop=(kc == 1)),
                    waits=[("VV", upd_ev), psb_free], tick="MM")
            v_xx = P.op("vector", lambda e: e.tensor_tensor(
                XXb[:], HQf[:], ps_b[:], ALU.mult),
                waits=[("VV", ev_hq), ("MM", max(mmq2, prev_mmx))], tick="VV")
            psb_free = ("VV", v_xx)
            prev_vxx = v_xx
            mmx = 0
            for ec in range(2):
                mmx = P.op("tensor", lambda e, ec=ec: e.transpose(
                    ps_t[:, ec * 128:(ec + 1) * 128],
                    XXb[:, ec * 128:(ec + 1) * 128], self.ID16[:]),
                    waits=[("VV", v_xx), ("EV", max(xxt_ev, kv_ev))], tick="MM")
            prev_mmx = mmx
            xxt_ev = P.op("scalar", lambda e, mi=mi: e.activation(
                XXT[:, :, mi, :], ps_t[:].rearrange("p (c t) -> p c t", t=128),
                AF.Copy), waits=[("MM", mmx)], tick="EV")

        prev_add = 0
        for dm in range(2):
            for ncb in range(2):
                mm = 0
                for ec in range(2):
                    mm = P.op("tensor", lambda e, dm=dm, ec=ec, ncb=ncb: e.matmul(
                        ps_o[:], W1N[:, ec, dm * 128:(dm + 1) * 128],
                        XXT[:, ec, ncb * 4:(ncb + 1) * 4, :].rearrange(
                            "p a b -> p (a b)"),
                        start=(ec == 0), stop=(ec == 1)),
                        waits=[("EV", xxt_ev), ("VV", max(upd_ev, prev_add))],
                        tick="MM")
                tsl = slice(base_t + ncb * 512, base_t + (ncb + 1) * 512)
                prev_add = P.op("vector", lambda e, dm=dm, tsl=tsl: e.scalar_tensor_tensor(
                    out=self.X[:, ch0 + dm, tsl], in0=ps_o[:], scalar=1.0,
                    in1=self.X[:, ch0 + dm, tsl], op0=ALU.mult, op1=ALU.add),
                    waits=[("MM", mm)], tick="VV")

    def emit_cbs(self, st, ei):
        P = self.P
        CBS = P.sb_in(st, "cbsx", [128, S], F32)
        stp = P.scope()
        psb = [P.ps_in(stp, "pscb", [128, 512], F32) for _ in range(2)]
        prev_ev = [0, 0]
        ev = 0
        for tb in range(NTB):
            ps = psb[tb % 2]
            mtb = 0
            for k in range(4):
                tt = tb * 4 + k
                mtb = P.op("tensor", lambda e, ps=ps, tt=tt, k=k: e.matmul(
                    ps[:, k * 128:(k + 1) * 128],
                    self.CW_TOK[:, tt, ei:ei + 1].broadcast_to((128, 128)),
                    self.ID32[:], start=True, stop=True),
                    waits=[("VV", self.cw_ready), ("EV", prev_ev[tb % 2])], tick="MM")
            ev = P.op("scalar", lambda e, ps=ps, tb=tb: e.activation(
                CBS[:, tb * TB:(tb + 1) * TB], ps[:], AF.Copy),
                waits=[("MM", mtb)], tick="EV")
            prev_ev[tb % 2] = ev
        self.P.flush("cbs")
        stp.close()
        return CBS, ev

    def stage_proj(self, src, dst, wkey, bname, act, W, pspair, psfree,
                   w_free=(), src_dep=()):
        P = self.P
        d = self.dram
        wdep = P.dma("gpsimd", W[:],
                     d[wkey].ap().rearrange("(c p) o -> p c o", p=128),
                     waits=list(w_free))
        last = 0
        last_mm = 0
        for tb in range(NTB):
            tsl = slice(tb * TB, (tb + 1) * TB)
            for oc in range(NCH):
                b = (tb * NCH + oc) % 2
                ps = pspair[b]
                mt = 0
                for kc in range(NCH):
                    mt = P.op("tensor", lambda e, ps=ps, kc=kc, oc=oc, tsl=tsl: e.matmul(
                        ps[:], W[:, kc, oc * 128:(oc + 1) * 128], src[:, kc, tsl],
                        start=(kc == 0), stop=(kc == NCH - 1)),
                        waits=[wdep, ("EV", psfree[b])] + list(src_dep),
                        tick="MM")
                ev = P.op("scalar", lambda e, ps=ps, oc=oc, tsl=tsl: e.activation(
                    dst[:, oc, tsl], ps[:], act, bias=self.bias_col(bname, oc)),
                    waits=[("MM", mt)], tick="EV")
                psfree[b] = ev
                last = ev
                last_mm = mt
        return last, last_mm

    def alloc_2mat(self, st):
        P = self.P
        return dict(
            H=[P.sb_in(st, "h2m", [128, 8, 512], BF16) for _ in range(2)],
            TF=P.sb_in(st, "tf2m", [128, 512], F32),
            W1b=[P.sb_in(st, "w1g", [128, NCH, 1024], BF16) for _ in range(2)],
            W2b=[P.sb_in(st, "w2g", [128, 8, 1024], BF16) for _ in range(2)],
            CBS=[P.sb_in(st, "cbsx", [128, S], F32) for _ in range(2)],
            ps1=[P.ps_in(st, "ps1", [128, 512], F32) for _ in range(2)],
            ps2=[P.ps_in(st, "ps2", [128, 512], F32) for _ in range(2)],
            pcb=[P.ps_in(st, "pscb2", [128, 512], F32) for _ in range(2)],
            ROW=P.sb_in(st, "b2row2", [1, 1024], BF16),
            row_free=0,
            w_free=[(None, None), (None, None)],
            **(lambda t: dict(CBSB=[t, t]))(P.sb_in(st, "cbsb", [1, S], BF16)),
            ps1_free=[0, 0], ps2_free=[0, 0],
            h_free=[0, 0], tf_free=0,
            cbs_free=[0, 0],
            pcb_free=[0, 0],
            gidx=0,
        )

    def emit_cbs2(self, B, ei, epar):
        P = self.P
        CBS = B["CBS"][epar]
        ev = 0
        for tb in range(NTB):
            ps = B["pcb"][tb % 2]
            mtb = 0
            for k in range(4):
                tt = tb * 4 + k
                mtb = P.op("tensor", lambda e, ps=ps, tt=tt, k=k, ei=ei: e.matmul(
                    ps[:, k * 128:(k + 1) * 128],
                    self.CW_TOK[:, tt, ei:ei + 1].broadcast_to((128, 128)),
                    self.ID32[:], start=True, stop=True),
                    waits=[("VV", max(self.cw_ready, B.get("gvv", 0))),
                           ("EV", B["pcb_free"][tb % 2]),
                           ("MM", B["cbs_free"][epar])], tick="MM")
            ev = P.op("scalar", lambda e, ps=ps, CBS=CBS, tb=tb: e.activation(
                CBS[:, tb * TB:(tb + 1) * TB], ps[:], AF.Copy),
                waits=[("MM", mtb)], tick="EV")
            B["pcb_free"][tb % 2] = ev
        if "CBSB" in B:
            ev = P.op("scalar", lambda e, CBS=CBS, epar=epar: e.activation(
                B["CBSB"][epar][0:1, :], CBS[0:1, :], AF.Copy),
                waits=[("EV", ev), ("MM", max(B["cbs_free"])),
                       ("VV", B.get("gvv", 0))], tick="EV")
        return CBS, ev

    def expert_2mat(self, seq, tag, B, w1src, w2src, I, act, b1pfx, b2row_key,
                    ei, strow):
        P = self.P
        epar = ei % 2
        CBS, cbs_ev = self.emit_cbs2(B, ei, epar)
        key, bsl = b2row_key
        b2src = self.dram[key].ap() if bsl is None else self.dram[key][bsl[1], :]
        b2row = B["ROW"]
        b2dep = P.dma("gpsimd", b2row[0:1, :], b2src,
                      waits=([("MM", B["row_free"])] if B["row_free"] else [])
                      + B.get("gates", []))
        CBSB = B["CBSB"][epar]
        NG = I // 1024
        wsrc = []
        for g in range(NG):
            wsrc.append((
                w1src[:, g * 1024:(g + 1) * 1024].rearrange("(c p) o -> p c o", p=128),
                w2src[g * 1024:(g + 1) * 1024, :].rearrange("(c p) o -> p c o", p=128),
            ))
        w_dep = [None, None]

        def load_group(g):
            gp = B["gidx"] % 2
            B["gidx"] += 1
            fr = B["w_free"][gp]
            gts = B.get("gates", [])
            w_dep[gp] = (
                P.dma("gpsimd", B["W1b"][gp][:], wsrc[g][0],
                      waits=([fr[0]] if fr[0] else []) + gts),
                P.dma("gpsimd", B["W2b"][gp][:], wsrc[g][1],
                      waits=([fr[1]] if fr[1] else []) + gts),
            )
            return gp
        gps = [load_group(0)]
        if NG > 1:
            gps.append(load_group(1))
        last_cbs_mm = B["cbs_free"][epar]
        for g in range(NG):
            gp = gps[g]
            s1_last = 0
            s2_last = 0
            for tb in range(NTB):
                tsl = slice(tb * TB, (tb + 1) * TB)
                hp = tb % 2
                pp_hs = [0] * 8
                for ic in range(8):
                    b = ic % 2
                    mt = 0
                    for kc in range(NCH):
                        mt = P.op("tensor", lambda e, b=b, kc=kc, ic=ic, gp=gp, tsl=tsl: e.matmul(
                            B["ps1"][b][:], B["W1b"][gp][:, kc, ic * 128:(ic + 1) * 128],
                            self.XN[:, kc, tsl], start=(kc == 0), stop=(kc == NCH - 1)),
                            waits=[w_dep[gp][0], ("EV", B["ps1_free"][b]),
                                   ("EV", self.xn_ev[tb])],
                            tick="MM")
                    s1_last = mt
                    ev = P.op("scalar", lambda e, b=b, g=g, ic=ic: e.activation(
                        B["TF"][:], B["ps1"][b][:], act,
                        bias=self.bias_col(b1pfx, g * 8 + ic)),
                        waits=[("MM", mt), ("VV", B["tf_free"])], tick="EV")
                    B["ps1_free"][b] = ev
                    pp_hs[ic] = P.op("vector", lambda e, hp=hp, ic=ic, tsl=tsl, CBS=CBS: e.tensor_tensor(
                        B["H"][hp][:, ic, :], B["TF"][:], CBS[:, tsl], ALU.mult),
                        waits=[("EV", max(ev, cbs_ev)), ("MM", B["h_free"][hp])],
                        tick="VV")
                    B["tf_free"] = pp_hs[ic]
                for oc in range(NCH):
                    b = oc % 2
                    mt = 0
                    for ic in range(8):
                        mt = P.op("tensor", lambda e, b=b, ic=ic, oc=oc, gp=gp, hp=hp, g=g: e.matmul(
                            B["ps2"][b][:], B["W2b"][gp][:, ic, oc * 128:(oc + 1) * 128],
                            B["H"][hp][:, ic, :], start=(ic == 0),
                            stop=(ic == 7 and g > 0)),
                            waits=[w_dep[gp][1],
                                   ("VV", max(B["ps2_free"][b], pp_hs[ic]))],
                            tick="MM")
                    if g == 0:
                        mt = P.op("tensor", lambda e, b=b, oc=oc, tsl=tsl, CBSB=CBSB: e.matmul(
                            B["ps2"][b][:], b2row[0:1, oc * 128:(oc + 1) * 128],
                            CBSB[0:1, tsl], start=False, stop=True),
                            waits=[b2dep, ("EV", cbs_ev)], tick="MM")
                        last_cbs_mm = mt
                    s2_last = mt
                    if oc == NCH - 1:
                        B["h_free"][hp] = mt
                    B["ps2_free"][b] = P.op("vector", lambda e, b=b, oc=oc, tsl=tsl: e.scalar_tensor_tensor(
                        out=self.X[:, oc, tsl], in0=B["ps2"][b][:], scalar=1.0,
                        in1=self.X[:, oc, tsl], op0=ALU.mult, op1=ALU.add),
                        waits=[("MM", mt)], tick="VV")
            B["w_free"][gp] = (("MM", s1_last), ("MM", s2_last))
            if g + 2 < NG:
                gps.append(load_group(g + 2))
        B["cbs_free"][epar] = max(last_cbs_mm, s2_last)
        B["row_free"] = last_cbs_mm

    def emit_e0(self, seq):
        P = self.P
        d = self.dram
        gmm, gev, gvv = P.cnt["MM"], P.cnt["EV"], P.cnt["VV"]
        gts = [("MM", gmm), ("EV", gev), ("VV", gvv)]
        st = P.scope()
        Bc = dict(CBS=[P.sb_in(st, "cbsx", [128, S], F32)],
                  pcb=[P.ps_in(st, "pscb", [128, 512], F32) for _ in range(2)],
                  pcb_free=[gev, gev], cbs_free=[gmm, gmm], gvv=gvv)
        CBS, cbs_ev = self.emit_cbs2(Bc, 0, 0)
        H = [P.sb_in(st, "h0", [128, 4, 512], BF16) for _ in range(2)]
        TF2 = P.sb_in(st, "tf0", [128, 2, 512], F32)
        W1b = [P.sb_in(st, "w1g0", [128, NCH, 512], BF16) for _ in range(2)]
        W3b = [P.sb_in(st, "w3g0", [128, NCH, 512], BF16) for _ in range(2)]
        W2b = [P.sb_in(st, "w2g0", [128, 4, 1024], BF16) for _ in range(2)]
        ps1 = [P.ps_in(st, "ps10", [128, 512], F32) for _ in range(2)]
        ps3 = [P.ps_in(st, "ps30", [128, 512], F32) for _ in range(2)]
        ps2 = [P.ps_in(st, "ps20", [128, 512], F32) for _ in range(2)]
        NG = 8
        IG = 512
        w_dep = [None, None]

        def load_group(g, waits):
            waits = list(waits) + gts
            gp = g % 2
            gsl = slice(g * IG, (g + 1) * IG)
            w_dep[gp] = (
                P.dma("gpsimd", W1b[gp][:],
                      d["L0_ffn_sw_w1"][:, gsl].rearrange("(c p) o -> p c o", p=128),
                      waits=waits),
                P.dma("gpsimd", W3b[gp][:],
                      d["L0_ffn_sw_w3"][:, gsl].rearrange("(c p) o -> p c o", p=128),
                      waits=waits),
                P.dma("gpsimd", W2b[gp][:],
                      d["L0_ffn_sw_w2"][gsl, :].rearrange("(c p) o -> p c o", p=128),
                      waits=waits),
            )
        load_group(0, [])
        load_group(1, [])
        ps1_free = [0, 0]
        ps3_free = [0, 0]
        ps2_free = [0, 0]
        h_free = [0, 0]
        tff = [0, 0]
        tfp = [0, 0]
        for g in range(NG):
            gp = g % 2
            s_last = 0
            for tb in range(NTB):
                tsl = slice(tb * TB, (tb + 1) * TB)
                hp = tb % 2
                pp_hs = [0] * 4
                for ic in range(4):
                    b = ic % 2
                    mt1 = 0
                    mt3 = 0
                    for kc in range(NCH):
                        mt1 = P.op("tensor", lambda e, b=b, kc=kc, ic=ic, gp=gp, tsl=tsl: e.matmul(
                            ps1[b][:], W1b[gp][:, kc, ic * 128:(ic + 1) * 128],
                            self.XN[:, kc, tsl], start=(kc == 0), stop=(kc == NCH - 1)),
                            waits=[w_dep[gp][0], ("EV", ps1_free[b])],
                            tick="MM")
                    for kc in range(NCH):
                        mt3 = P.op("tensor", lambda e, b=b, kc=kc, ic=ic, gp=gp, tsl=tsl: e.matmul(
                            ps3[b][:], W3b[gp][:, kc, ic * 128:(ic + 1) * 128],
                            self.XN[:, kc, tsl], start=(kc == 0), stop=(kc == NCH - 1)),
                            waits=[w_dep[gp][1], ("VV", ps3_free[b])],
                            tick="MM")
                    TF = TF2[:, b, :]
                    ev = P.op("scalar", lambda e, b=b, TF=TF: e.activation(
                        TF[:], ps1[b][:], AF.Silu),
                        waits=[("MM", mt1), ("VV", max(tff[b], tfp[b]))],
                        tick="EV")
                    ps1_free[b] = ev
                    vv = P.op("vector", lambda e, b=b, TF=TF: e.tensor_tensor(
                        TF[:], TF[:], ps3[b][:], ALU.mult),
                        waits=[("EV", ev), ("MM", mt3)], tick="VV")
                    ps3_free[b] = vv
                    tff[b] = vv
                    pp_hs[ic] = P.op("vector", lambda e, hp=hp, ic=ic, tsl=tsl, TF=TF: e.tensor_tensor(
                        H[hp][:, ic, :], TF[:], CBS[:, tsl], ALU.mult),
                        waits=[("VV", vv), ("EV", cbs_ev), ("MM", h_free[hp])],
                        tick="VV")
                    tfp[b] = pp_hs[ic]
                for oc in range(NCH):
                    b = oc % 2
                    mt = 0
                    for ic in range(4):
                        mt = P.op("tensor", lambda e, b=b, ic=ic, oc=oc, gp=gp, hp=hp: e.matmul(
                            ps2[b][:], W2b[gp][:, ic, oc * 128:(oc + 1) * 128],
                            H[hp][:, ic, :], start=(ic == 0), stop=(ic == 3)),
                            waits=[w_dep[gp][2], ("VV", max(ps2_free[b], pp_hs[ic]))],
                            tick="MM")
                    s_last = mt
                    if oc == NCH - 1:
                        h_free[hp] = mt
                    ps2_free[b] = P.op("vector", lambda e, b=b, oc=oc, tsl=tsl: e.scalar_tensor_tensor(
                        out=self.X[:, oc, tsl], in0=ps2[b][:], scalar=1.0,
                        in1=self.X[:, oc, tsl], op0=ALU.mult, op1=ALU.add),
                        waits=[("MM", mt)], tick="VV")
            if g + 2 < NG:
                load_group(g + 2, [("MM", s_last)])
        st.close()
        P.flush(f"l0ffn_{seq}")

    def emit_e1(self, seq):
        P = self.P
        d = self.dram
        gmm, gev, gvv = P.cnt["MM"], P.cnt["EV"], P.cnt["VV"]
        gts = [("MM", gmm), ("EV", gev), ("VV", gvv)]
        st = P.scope()
        Bc = dict(CBS=[P.sb_in(st, "cbsx", [128, S], F32)],
                  pcb=[P.ps_in(st, "pscb", [128, 512], F32) for _ in range(2)],
                  pcb_free=[gev, gev], cbs_free=[gmm, gmm], gvv=gvv)
        Bc["CBSB"] = [P.sb_in(st, "cbsb", [1, S], BF16)]
        Bc["CBS"] = Bc["CBS"] * 1
        CBS, cbs_ev = self.emit_cbs2(Bc, 1, 0)
        CBSB = Bc["CBSB"][0]
        c2row = P.sb_in(st, "c2rb", [1, 1024], BF16)
        c2dep = P.dma("gpsimd", c2row[0:1, :], self.dram["L0_ffn_me_c2b"].ap(),
                      waits=gts)
        BUF1 = P.sb_in(st, "e1b1", [128, NCH, S], BF16)
        SPW = P.sb_in(st, "spw", [128, NCH, D], BF16)
        sp_ps = [P.ps_in(st, "psp", [128, 512], F32) for _ in range(2)]
        spfree = [gev, gev]
        stb = P.scope()
        BUF2 = P.sb_in(stb, "e1b2", [128, NCH, S], BF16)
        d1, m1 = self.stage_proj(self.XN, BUF1, "L0_ffn_me_eq_w", "me_eq_b",
                                 AF.Identity, SPW, sp_ps, spfree,
                                 w_free=gts, src_dep=[("VV", gvv)])
        d2, m2 = self.stage_proj(BUF1, BUF2, "L0_ffn_me_wv", "me_bv",
                                 AF.Identity, SPW, sp_ps, spfree,
                                 w_free=[("MM", m1)], src_dep=[("EV", d1)])
        d3, m3 = self.stage_proj(BUF2, BUF1, "L0_ffn_me_wo", "me_bo",
                                 AF.Identity, SPW, sp_ps, spfree,
                                 w_free=[("MM", m2)], src_dep=[("EV", d2)])
        stb.close()
        U = P.sb_in(st, "e1u", [128, 8, 512], BF16)
        TF = P.sb_in(st, "e1tf", [128, 512], F32)
        C1 = P.sb_in(st, "c1h", [128, NCH, 1024], BF16)
        C2 = P.sb_in(st, "c2h", [128, 8, 1024], BF16)
        ps1 = [P.ps_in(st, "e1p1", [128, 512], F32) for _ in range(2)]
        ps2 = [P.ps_in(st, "e1p2", [128, 512], F32) for _ in range(2)]
        last2 = 0
        last1 = 0
        for g in range(2):
            gsl = slice(g * 1024, (g + 1) * 1024)
            wd1 = P.dma("gpsimd", C1[:],
                        d["L0_ffn_me_c1w"][:, gsl].rearrange("(c p) o -> p c o", p=128),
                        waits=[("EV", d3)] if g == 0 else [("MM", last1)])
            wd2 = P.dma("gpsimd", C2[:],
                        d["L0_ffn_me_c2w"][gsl, :].rearrange("(c p) o -> p c o", p=128),
                        waits=[("EV", d3)] if g == 0 else [("MM", last2)])
            ps1_free = [0, 0]
            ps2_free = [0, 0]
            u_free = 0
            tf_free = 0
            for tb in range(NTB):
                tsl = slice(tb * TB, (tb + 1) * TB)
                pp_us = [0] * 8
                for ic in range(8):
                    b = ic % 2
                    mt = 0
                    for kc in range(NCH):
                        mt = P.op("tensor", lambda e, b=b, kc=kc, ic=ic, tsl=tsl: e.matmul(
                            ps1[b][:], C1[:, kc, ic * 128:(ic + 1) * 128],
                            BUF1[:, kc, tsl], start=(kc == 0), stop=(kc == NCH - 1)),
                            waits=[wd1, ("EV", max(ps1_free[b], d3)),
                                   ("MM", u_free)], tick="MM")
                    last1 = mt
                    ev = P.op("scalar", lambda e, b=b, g=g, ic=ic: e.activation(
                        TF[:], ps1[b][:], AF.Gelu,
                        bias=self.bias_col("me_c1b", g * 8 + ic)),
                        waits=[("MM", mt), ("VV", tf_free)], tick="EV")
                    ps1_free[b] = ev
                    pp_us[ic] = P.op("vector", lambda e, ic=ic, tsl=tsl: e.tensor_tensor(
                        U[:, ic, :], TF[:], CBS[:, tsl], ALU.mult),
                        waits=[("EV", max(ev, cbs_ev))], tick="VV")
                    tf_free = pp_us[ic]
                for oc in range(NCH):
                    b = oc % 2
                    mt = 0
                    for ic in range(8):
                        mt = P.op("tensor", lambda e, b=b, ic=ic, oc=oc, g=g: e.matmul(
                            ps2[b][:], C2[:, ic, oc * 128:(oc + 1) * 128],
                            U[:, ic, :], start=(ic == 0), stop=(ic == 7 and g == 1)),
                            waits=[wd2, ("VV", max(ps2_free[b], pp_us[ic]))], tick="MM")
                    if g == 0:
                        mt = P.op("tensor", lambda e, b=b, oc=oc, tsl=tsl: e.matmul(
                            ps2[b][:], c2row[0:1, oc * 128:(oc + 1) * 128],
                            CBSB[0:1, tsl], start=False, stop=True),
                            waits=[c2dep, ("EV", cbs_ev)], tick="MM")
                    last2 = mt
                    if oc == NCH - 1:
                        u_free = mt
                    ps2_free[b] = P.op("vector", lambda e, b=b, oc=oc, tsl=tsl: e.scalar_tensor_tensor(
                        out=self.X[:, oc, tsl], in0=ps2[b][:], scalar=1.0,
                        in1=self.X[:, oc, tsl], op0=ALU.mult, op1=ALU.add),
                        waits=[("MM", mt)], tick="VV")
        st.close()

    def emit_e2(self, seq):
        P = self.P
        d = self.dram
        st = P.scope()
        stp = P.scope()
        Bc = dict(CBS=[P.sb_in(st, "cbsx", [128, S], F32)],
                  CBSB=[P.sb_in(st, "cbsb", [1, S], BF16)],
                  pcb=[P.ps_in(stp, "pscb", [128, 512], F32) for _ in range(2)],
                  pcb_free=[0, 0], cbs_free=[0, 0])
        CBS, cbs_ev = self.emit_cbs2(Bc, 2, 0)
        stp.close()
        CBSB = Bc["CBSB"][0]
        genrow = P.sb_in(st, "genrb", [1, 1024], BF16)
        gendep = P.dma("gpsimd", genrow[0:1, :], d["L0_ffn_ce_gen_b"].ap())
        borow = P.sb_in(st, "borb", [1, 1024], BF16)
        bodep = P.dma("gpsimd", borow[0:1, :], d["L0_ffn_ce_bo"].ap())
        HS = S // 2
        BUF1 = P.sb_in(st, "e2b1", [128, NCH, HS], BF16)
        BUF2 = P.sb_in(st, "e2b2", [128, NCH, HS], BF16)
        WX = P.sb_in(st, "e2wx", [128, 16, 1024], BF16)
        Wp = WX[:, 0:8, :]
        F2v = WX[:, 8:16, :]
        Y32 = P.sb_in(st, "e2y", [128, NCH, 256], F32)
        T32P = P.sb_in(st, "e2t", [128, 2, 256], F32)
        U = P.sb_in(st, "e2u", [128, 8, 256], BF16)
        lp = self.alloc_ln(st, 256)
        lp["sq_dve"] = True
        psA = [P.ps_in(st, "e2pa", [128, 512], F32) for _ in range(2)]
        psB = [P.ps_in(st, "e2pb", [128, 512], F32) for _ in range(2)]
        lp["stats_ev_prev"] = cbs_ev
        pAf = [("EV", cbs_ev), ("EV", cbs_ev)]
        pBf = [("EV", cbs_ev), ("EV", cbs_ev)]
        ctx = dict(wx_a=0, wx_b=0, u_free=0, y32_mm=0, y32_ev=0)

        def proj_half(src, srcoff, dst, wkey, bname, src_dep=()):
            wdep = P.dma("gpsimd", Wp,
                         d[wkey].ap().rearrange("(c p) o -> p c o", p=128),
                         waits=[("MM", ctx["wx_a"])] if ctx["wx_a"] else [])
            lastv = 0
            for tb in range(HS // 512):
                ssl = slice(srcoff + tb * 512, srcoff + (tb + 1) * 512)
                dsl = slice(tb * 512, (tb + 1) * 512)
                for oc in range(NCH):
                    b = (tb * NCH + oc) % 2
                    ps = psA[b]
                    mt = 0
                    for kc in range(NCH):
                        mt = P.op("tensor", lambda e, ps=ps, kc=kc, oc=oc, ssl=ssl: e.matmul(
                            ps[:], Wp[:, kc, oc * 128:(oc + 1) * 128],
                            src[:, kc, ssl],
                            start=(kc == 0), stop=(kc == NCH - 1)),
                            waits=[wdep, pAf[b]] + list(src_dep), tick="MM")
                    ev = P.op("scalar", lambda e, ps=ps, oc=oc, dsl=dsl: e.activation(
                        dst[:, oc, dsl], ps[:], AF.Identity,
                        bias=self.bias_col(bname, oc)),
                        waits=[("MM", mt)], tick="EV")
                    pAf[b] = ("EV", ev)
                    lastv = ev
                    ctx["wx_a"] = mt
            return lastv

        for hseq in range(2):
            off = hseq * HS
            s1 = proj_half(self.XN, off, BUF1, "L0_ffn_ce_syn_w", "ce_syn_b")
            s2 = proj_half(BUF1, 0, BUF2, "L0_ffn_ce_wv", "ce_bv",
                           src_dep=[("EV", s1)])
            wdep = P.dma("gpsimd", Wp,
                         d["L0_ffn_ce_wo"].ap().rearrange("(c p) o -> p c o", p=128),
                         waits=[("MM", ctx["wx_a"])])
            h2_deps = [("EV", 0)] * 4
            prev_src_mm = ctx["y32_mm"]
            prev_src_ev = ctx["y32_ev"]
            prev_src_vv = ctx.get("y32_vv", 0)
            t32f = [("VV", 0), ("VV", 0)]
            for tb in range(4):
                tsl = slice(tb * 256, (tb + 1) * 256)
                yvv = 0
                for oc in range(NCH):
                    b = oc % 2
                    ps = psB[b][:, 0:256]
                    mt = 0
                    for kc in range(NCH):
                        mt = P.op("tensor", lambda e, ps=ps, kc=kc, oc=oc, tsl=tsl: e.matmul(
                            ps[:], Wp[:, kc, oc * 128:(oc + 1) * 128],
                            BUF2[:, kc, tsl],
                            start=(kc == 0), stop=False),
                            waits=[wdep, pBf[b], ("EV", s2)],
                            tick="MM")
                    mt = P.op("tensor", lambda e, ps=ps, oc=oc: e.matmul(
                        ps[:], borow[0:1, oc * 128:(oc + 1) * 128],
                        self.ONESR16[0:1, 0:256], start=False, stop=True),
                        waits=[bodep], tick="MM")
                    ctx["wx_a"] = mt
                    yvv = P.op("vector", lambda e, ps=ps, oc=oc, tsl=tsl: e.tensor_tensor(
                        Y32[:, oc, :], BUF1[:, oc, tsl], ps[:], ALU.add),
                        waits=[("MM", max(mt, prev_src_mm)),
                               ("EV", prev_src_ev), ("VV", prev_src_vv)],
                        tick="VV")
                    pBf[b] = ("VV", yvv)
                r = self.emit_ln_tb(lp, lambda c: Y32[:, c, :], True, [("VV", yvv)])
                prev_src_mm = r["src_mm"]
                prev_src_ev = r["src_ev"]
                prev_src_vv = r["src_vv"]
                apply_vv = 0
                for oc in range(NCH):
                    b = oc % 2
                    T32 = T32P[:, b, :]
                    vs = P.op("vector", lambda e, oc=oc, T32=T32: e.tensor_tensor(
                        T32[:], Y32[:, oc, :], lp["ps_mu"][:], ALU.subtract),
                        waits=[("MM", r["bmu"]), t32f[b]], tick="VV")
                    vm = P.op("vector", lambda e, T32=T32: e.tensor_tensor(
                        T32[:], T32[:], lp["ps_rs"][:], ALU.mult),
                        waits=[("MM", r["brs"]), ("VV", vs)], tick="VV")
                    apply_vv = vm
                    h2_ev = P.op("scalar", lambda e, oc=oc, tsl=tsl, T32=T32: e.activation(
                        BUF2[:, oc, tsl], T32[:], AF.Identity,
                        scale=self.bias_col("ce_n1g", oc),
                        bias=self.bias_col("ce_n1b", oc)),
                        waits=[("VV", vm)], tick="EV")
                    h2_deps[tb] = ("EV", h2_ev)
                    t32f[b] = ("EV", h2_ev)
                lp["apply_prev"] = apply_vv
            wo_yvv = yvv
            buf1_ev = 0
            out_pp = 0
            out_pp_tb = [0] * 4
            for g in range(2):
                gsl = slice(g * 1024, (g + 1) * 1024)
                wd1 = P.dma("gpsimd", Wp,
                            d["L0_ffn_ce_f1w"][:, gsl].rearrange("(c p) o -> p c o", p=128),
                            waits=[("MM", ctx["wx_a"])])
                wd2 = P.dma("gpsimd", F2v,
                            d["L0_ffn_ce_f2w"][gsl, :].rearrange("(c p) o -> p c o", p=128),
                            waits=[("MM", ctx["wx_b"])] if ctx["wx_b"] else [])
                last_v2 = 0
                for tb in range(4):
                    tsl = slice(tb * 256, (tb + 1) * 256)
                    ev_us = [0] * 8
                    mt1 = 0
                    for ic in range(8):
                        b = ic % 2
                        mt = 0
                        for kc in range(NCH):
                            mt = P.op("tensor", lambda e, b=b, kc=kc, ic=ic, tsl=tsl: e.matmul(
                                psA[b][:, 0:256], Wp[:, kc, ic * 128:(ic + 1) * 128],
                                BUF2[:, kc, tsl], start=(kc == 0), stop=(kc == NCH - 1)),
                                waits=[wd1, pAf[b], ("MM", ctx["u_free"]),
                                       h2_deps[tb]], tick="MM")
                        mt1 = mt
                        ev_us[ic] = P.op("scalar", lambda e, b=b, g=g, ic=ic: e.activation(
                            U[:, ic, :], psA[b][:, 0:256], AF.Relu,
                            bias=self.bias_col("ce_f1b", g * 8 + ic)),
                            waits=[("MM", mt)], tick="EV")
                        pAf[b] = ("EV", ev_us[ic])
                    vvs = []
                    mt2 = 0
                    for oc in range(NCH):
                        b = oc % 2
                        mt = 0
                        for ic in range(8):
                            mt = P.op("tensor", lambda e, b=b, ic=ic, oc=oc: e.matmul(
                                psB[b][:, 0:256], F2v[:, ic, oc * 128:(oc + 1) * 128],
                                U[:, ic, :], start=(ic == 0), stop=(ic == 7)),
                                waits=[wd2, ("EV", ev_us[ic]), pBf[b],
                                       ("VV", out_pp)], tick="MM")
                        mt2 = mt
                        if oc == NCH - 1:
                            ctx["u_free"] = mt
                        if g == 0:
                            buf1_ev = P.op("scalar", lambda e, b=b, oc=oc, tsl=tsl: e.activation(
                                BUF1[:, oc, tsl], psB[b][:, 0:256], AF.Identity,
                                bias=self.bias_col("ce_f2b", oc)),
                                waits=[("MM", mt), ("VV", wo_yvv)], tick="EV")
                            pBf[b] = ("EV", buf1_ev)
                        else:
                            T32 = T32P[:, b, :]
                            v1 = P.op("vector", lambda e, b=b, oc=oc, tsl=tsl, T32=T32: e.scalar_tensor_tensor(
                                out=T32[:], in0=psB[b][:, 0:256], scalar=1.0,
                                in1=BUF1[:, oc, tsl], op0=ALU.mult, op1=ALU.add),
                                waits=[("MM", max(mt, prev_src_mm)),
                                       ("EV", max(prev_src_ev, buf1_ev)),
                                       ("VV", max(out_pp, last_v2, prev_src_vv)),
                                       t32f[b]],
                                tick="VV")
                            v2 = P.op("vector", lambda e, oc=oc, tsl=tsl, T32=T32: e.tensor_tensor(
                                Y32[:, oc, :], BUF2[:, oc, tsl], T32[:], ALU.add),
                                waits=[("VV", v1)], tick="VV")
                            t32f[b] = ("VV", v2)
                            last_v2 = v2
                            pBf[b] = ("VV", v2)
                            vvs.append(v2)
                    if g == 1:
                        r = self.emit_ln_tb(lp, lambda c: Y32[:, c, :], True,
                                            [("VV", vvs[-1] if vvs else 0)])
                        prev_src_mm = r["src_mm"]
                        prev_src_ev = r["src_ev"]
                        prev_src_vv = r["src_vv"]
                        apply_vv = 0
                        for oc in range(NCH):
                            b2 = oc % 2
                            T32 = T32P[:, b2, :]
                            vs = P.op("vector", lambda e, oc=oc, T32=T32: e.tensor_tensor(
                                T32[:], Y32[:, oc, :], lp["ps_mu"][:], ALU.subtract),
                                waits=[("MM", r["bmu"]), ("VV", out_pp), t32f[b2]],
                                tick="VV")
                            vm = P.op("vector", lambda e, T32=T32: e.tensor_tensor(
                                T32[:], T32[:], lp["ps_rs"][:], ALU.mult),
                                waits=[("MM", r["brs"]), ("VV", vs)], tick="VV")
                            apply_vv = vm
                            ev3 = P.op("scalar", lambda e, oc=oc, T32=T32: e.activation(
                                T32[:], T32[:], AF.Identity,
                                scale=self.bias_col("ce_n2g", oc),
                                bias=self.bias_col("ce_n2b", oc)),
                                waits=[("VV", vm)], tick="EV")
                            out_pp = P.op("vector", lambda e, oc=oc, tsl=tsl, off=off, T32=T32: e.tensor_tensor(
                                BUF1[:, oc, tsl], T32[:],
                                CBS[:, off + tsl.start:off + tsl.stop], ALU.mult),
                                waits=[("EV", max(ev3, cbs_ev))], tick="VV")
                            t32f[b2] = ("VV", out_pp)
                            out_pp_tb[tb] = out_pp
                        lp["apply_prev"] = apply_vv
                ctx["wx_a"] = mt1
                ctx["wx_b"] = mt2
            ctx["y32_mm"] = prev_src_mm
            ctx["y32_ev"] = prev_src_ev
            ctx["y32_vv"] = prev_src_vv
            wdep = P.dma("gpsimd", Wp,
                         d["L0_ffn_ce_gen_w"].ap().rearrange("(c p) o -> p c o", p=128),
                         waits=[("MM", ctx["wx_a"])])
            for tb in range(HS // 512):
                lsl = slice(tb * 512, (tb + 1) * 512)
                xsl = slice(off + tb * 512, off + (tb + 1) * 512)
                for oc in range(NCH):
                    b = (tb * NCH + oc) % 2
                    ps = psA[b]
                    mt = 0
                    for kc in range(NCH):
                        mt = P.op("tensor", lambda e, ps=ps, kc=kc, oc=oc, lsl=lsl: e.matmul(
                            ps[:], Wp[:, kc, oc * 128:(oc + 1) * 128], BUF1[:, kc, lsl],
                            start=(kc == 0), stop=False),
                            waits=[wdep, pAf[b],
                                   ("VV", max(out_pp_tb[2 * tb], out_pp_tb[2 * tb + 1]))],
                            tick="MM")
                    mt = P.op("tensor", lambda e, ps=ps, oc=oc, xsl=xsl: e.matmul(
                        ps[:], genrow[0:1, oc * 128:(oc + 1) * 128], CBSB[0:1, xsl],
                        start=False, stop=True),
                        waits=[gendep, ("EV", cbs_ev)], tick="MM")
                    ctx["wx_a"] = mt
                    stt = P.op("vector", lambda e, ps=ps, oc=oc, xsl=xsl: e.scalar_tensor_tensor(
                        out=self.X[:, oc, xsl], in0=ps[:], scalar=1.0,
                        in1=self.X[:, oc, xsl], op0=ALU.mult, op1=ALU.add),
                        waits=[("MM", mt)], tick="VV")
                    pAf[b] = ("VV", stt)
        st.close()

    def ffn_phase(self, seq, layer):
        P = self.P
        d = self.dram
        if layer == 0:
            sel = self.stage.split(":")[1] if ":" in self.stage else "e0e1e2e3"
            if "e2" in sel:
                self.emit_e2(seq)
            if "e1" in sel:
                self.emit_e1(seq)
            if "e3" in sel:
                st = P.scope()
                B = self.alloc_2mat(st)
                B["gates"] = [("MM", P.cnt["MM"]), ("EV", P.cnt["EV"]),
                              ("VV", P.cnt["VV"])]
                B["pcb_free"] = [P.cnt["EV"]] * 2
                B["cbs_free"] = [P.cnt["MM"]] * 2
                B["gvv"] = P.cnt["VV"]
                self.expert_2mat(seq, "e3", B, d["L0_ffn_ml_w1"].ap(),
                                 d["L0_ffn_ml_w2"].ap(), 4096, AF.Gelu, "ml_b1",
                                 ("L0_ffn_ml_b2", None), 3, st)
                st.close()
            if "e0" in sel:
                self.emit_e0(seq)
        else:
            st = P.scope()
            B = self.alloc_2mat(st)
            B["gates"] = [("MM", P.cnt["MM"]), ("EV", P.cnt["EV"]),
                          ("VV", P.cnt["VV"]), ("PP", P.cnt["PP"])]
            B["pcb_free"] = [P.cnt["EV"]] * 2
            B["cbs_free"] = [P.cnt["MM"]] * 2
            B["gvv"] = P.cnt["VV"]
            for e in range(8):
                self.expert_2mat(seq, f"x{e}", B, d["L1_ffn_W1"][e],
                                 d["L1_ffn_W2"][e], 2048, AF.Relu, f"l1_b1_{e}",
                                 ("L1_ffn_b2", ("row", e)), e, st)
            P.flush(f"l1ffn_{seq}")
            st.close()


def kernel(**inputs):
    x = np.ascontiguousarray(inputs["L0_attn_x"])
    nc = build_nc("full")
    in_maps = []
    for core in range(8):
        m = {"x": x[2 * core:2 * core + 2]}
        for name in INPUT_SPECS:
            if name == "L0_attn_x":
                continue
            m[name] = np.ascontiguousarray(inputs[name])
        in_maps.append(m)
    res = run_bass_kernel_spmd(nc, in_maps, core_ids=list(range(8)))
    return np.concatenate([r["out"] for r in res.results], axis=0)
